# version 17 (final)
# baseline (speedup 1.0000x reference)
import numpy as np
import ml_dtypes
import concourse.bass as bass
import concourse.mybir as mybir
from concourse.bass_utils import run_bass_kernel_spmd

F32 = mybir.dt.float32
BF16 = mybir.dt.bfloat16
AF = mybir.ActivationFunctionType
ALU = mybir.AluOpType
AX = mybir.AxisListType

D = 4096
NCT = 32
NHC = 4
FW = 1024
EPS = 1e-5
SB_BASE = 16544


def _host_tcd():
    n = 8
    masks = sorted(range(n), key=lambda m: (bin(m).count('1'), m))
    inv = {m: i for i, m in enumerate(masks)}
    C = np.zeros((n, n, n))
    for i, a in enumerate(masks):
        for j, bb in enumerate(masks):
            sw, at = (0, a >> 1)
            while at:
                sw += bin(at & bb).count('1')
                at >>= 1
            C[i, j, inv[a ^ bb]] = -1.0 if sw & 1 else 1.0
    grades = np.array([bin(m).count('1') for m in masks])
    rev = np.where(grades * (grades - 1) // 2 % 2 == 1, -1.0, 1.0)
    bivk = C[:, :, [4, 5, 6]]
    TCD = np.zeros((4, 256, 256), dtype=np.float32)
    TCD[0] = np.kron(np.eye(32), np.diag(rev) / 16.0)
    for c in range(3):
        TCD[c + 1] = np.kron(np.eye(32), rev[:, None] * bivk[:, :, c])
    return TCD


ENGS = ("te", "ve", "ac", "sy", "gp")
SEM_OF = {"te": "TE", "ve": "VE", "ac": "AC", "sy": "SY", "gp": "GP"}


class Op:
    __slots__ = ("eng", "fn", "deps", "hazard", "dma", "stream", "pos", "val",
                 "needs_inc", "waits")

    def __init__(self, eng, fn, deps, hazard, dma, stream, pos):
        self.eng, self.fn, self.deps = eng, fn, deps
        self.hazard, self.dma, self.stream, self.pos = hazard, dma, stream, pos
        self.val = 0
        self.needs_inc = dma
        self.waits = []


class Sched:
    def __init__(self):
        self.ops = {e: [] for e in ENGS}
        self.streams = {}

    def op(self, eng, fn, deps=(), hazard=False):
        deps = [d for d in deps if d is not None]
        o = Op(eng, fn, deps, hazard, False, None, len(self.ops[eng]))
        self.ops[eng].append(o)
        return o

    def dma(self, eng, stream, fn, deps=()):
        deps = [d for d in deps if d is not None]
        o = Op(eng, fn, deps, False, True, stream, len(self.ops[eng]))
        self.ops[eng].append(o)
        self.streams.setdefault(stream, []).append(o)
        return o

    def resolve(self):
        def skip_self(d, o):
            return d.eng == o.eng and d.eng == "te" and not d.hazard

        for e in ENGS:
            for o in self.ops[e]:
                for d in o.deps:
                    if (not d.dma) and (not skip_self(d, o)):
                        d.needs_inc = True
        for e in ENGS:
            c = 0
            for o in self.ops[e]:
                if o.dma:
                    continue
                if o.needs_inc:
                    c += 1
                o.val = c
        for st, ops in self.streams.items():
            c = 0
            for o in ops:
                c += 16
                o.val = c
        for e in ENGS:
            seen = {}
            for o in self.ops[e]:
                want = {}
                for d in o.deps:
                    if d.dma:
                        s = "D:" + d.stream
                    elif skip_self(d, o):
                        assert d.pos < o.pos, (e, o.pos, d.pos)
                        continue
                    else:
                        s = SEM_OF[d.eng]
                    want[s] = max(want.get(s, 0), d.val)
                for s, v in sorted(want.items()):
                    if seen.get(s, 0) < v:
                        o.waits.append((s, v))
                        seen[s] = v

    def emit(self, nc, ent):
        self.resolve()
        sems = {}
        for s in set(SEM_OF.values()):
            sems[s] = ent(nc.semaphore(s))
        for st in self.streams:
            sems["D:" + st] = ent(nc.semaphore("D_" + st))
        with nc.Block() as block:
            hooks = {"te": block.tensor, "ve": block.vector, "ac": block.scalar,
                     "sy": block.sync, "gp": block.gpsimd}
            for e in ENGS:
                ops = self.ops[e]
                if not ops:
                    continue

                def body(eng, ops=ops, e=e):
                    for o in ops:
                        for s, v in o.waits:
                            eng.wait_ge(sems[s], v)
                        inst = o.fn(eng)
                        if o.dma:
                            inst.then_inc(sems["D:" + o.stream], 16)
                        elif o.needs_inc:
                            inst.then_inc(sems[SEM_OF[e]], 1)
                    tot = max((o.val for o in ops if not o.dma), default=0)
                    if tot > 0:
                        eng.wait_ge(sems[SEM_OF[e]], tot)
                    done = set()
                    for o in ops:
                        if o.dma and o.stream not in done:
                            done.add(o.stream)
                            eng.wait_ge(sems["D:" + o.stream],
                                        self.streams[o.stream][-1].val)
                hooks[e](body)


def build_program(L):
    assert L % 512 == 0
    NLB = L // 512
    NLT = L // 128
    NKT = L // 128
    NG = NKT // 2
    NQB = L // 512

    nc = bass.Bass("TRN2", target_bir_lowering=False, debug=False)
    ctx = []

    def ent(cm):
        v = cm.__enter__()
        ctx.append(cm)
        return v

    def din(name, shape, dt):
        return nc.dram_tensor(name, shape, dt, kind="ExternalInput")

    xb = din("xb", [L, D], F32)
    xbT = din("xbT", [128, NCT * L], F32)
    xres = din("xres", [L, FW], F32)
    wkT = din("wkT", [128, NCT * FW], F32)
    wvT = din("wvT", [128, NCT * FW], F32)
    kern = din("kern", [128, NCT * FW], F32)
    wq = din("wq", [128, 8 * D], F32)
    tcd = din("tcd", [128, 8 * 256], F32)
    i128 = din("i128", [128, 128], F32)
    ones128 = din("ones128", [128, 128], BF16)
    onesf = din("onesf", [1, 128], F32)
    i32t = din("i32t", [128, 32], F32)
    pn = din("pn", [16, 512], F32)
    gateT = din("gateT", [16, 4], F32)
    bwrow = din("bwrow", [1, 12], F32)
    bqc = din("bqc", [128, 8], F32)
    bkc = din("bkc", [128, 8], F32)
    bvrow = din("bvrow", [1, FW], F32)
    fbrow = din("fbrow", [1, FW], F32)
    lAnr = din("lAnr", [128, 4 * D], F32)
    lBnr = din("lBnr", [128, 4 * FW], F32)
    lAsT = din("lAsT", [128, 32 * 32], F32)
    lBsT = din("lBsT", [32, FW], F32)
    y_out = nc.dram_tensor("y", [L, FW], F32, kind="ExternalOutput")

    s_wqe = nc.dram_tensor("s_wqe", [128, NHC * NCT * 256], BF16)
    s_xnt = nc.dram_tensor("s_xnt", [NLB, 128, NCT * 512], BF16)
    s_kt = nc.dram_tensor("s_kt", [NHC, 2, 128, L], BF16)
    s_vv = nc.dram_tensor("s_vv", [NHC, NLT, 128, 256], BF16)
    s_qt = nc.dram_tensor("s_qt", [NHC, 2, 128, L], BF16)
    s_att = nc.dram_tensor("s_att", [NLT, 128, FW], BF16)
    s_aaug = nc.dram_tensor("s_aaug", [NHC, 128, NCT * 64], BF16)
    s_wvb = nc.dram_tensor("s_wvb", [128, NCT * FW], BF16)
    s_kernb = nc.dram_tensor("s_kernb", [128, NCT * FW], BF16)
    s_baug = nc.dram_tensor("s_baug", [NHC, 64, FW], BF16)

    PA = [ent(nc.psum_tensor("PA%d" % i, [128, 1024], F32)) for i in range(2)]
    PB = [ent(nc.psum_tensor("PB%d" % i, [128, 512], F32)) for i in range(4)]

    class Arena:
        def __init__(self):
            self.off = SB_BASE
            self.persist_end = None
            self.ranges = {}

        def alloc(self, name, shape, dt):
            per = int(np.prod(shape[1:])) * mybir.dt.size(dt)
            per = (per + 31) & ~31
            t = nc.alloc_sbuf_tensor_at(name, list(shape), dt, offset=self.off)
            self.ranges[id(t)] = (self.off, self.off + per)
            self.off += per
            assert self.off <= SB_BASE + 212800, (name, self.off)
            return t

        def mark_persist(self):
            self.persist_end = self.off

        def phase_reset(self):
            self.off = self.persist_end

    A = Arena()
    p_i128 = A.alloc("p_i128", [128, 128], F32)
    p_ones128 = A.alloc("p_ones128", [128, 128], BF16)
    p_onesf = A.alloc("p_onesf", [1, 128], F32)
    p_bqc = A.alloc("p_bqc", [128, 8], F32)
    p_bkc = A.alloc("p_bkc", [128, 8], F32)
    p_bqe = A.alloc("p_bqe", [128, 8], F32)
    p_bvbc = A.alloc("p_bvbc", [128, FW], F32)
    p_fbbc = A.alloc("p_fbbc", [128, FW], F32)
    p_negC = A.alloc("p_negC", [128, NHC], F32)
    A.mark_persist()

    S = Sched()

    A.phase_reset()
    t_tcd = A.alloc("t_tcd", [128, 8 * 256], F32)
    t_i32t = A.alloc("t_i32t", [128, 32], F32)
    t_pn = A.alloc("t_pn", [16, 512], F32)
    t_gateT = A.alloc("t_gateT", [16, 4], F32)
    t_bwrow = A.alloc("t_bwrow", [1, 12], F32)
    t_bvrow = A.alloc("t_bvrow", [1, FW], F32)
    t_fbrow = A.alloc("t_fbrow", [1, FW], F32)
    t_bwbc = A.alloc("t_bwbc", [128, 12], F32)
    t_gc = A.alloc("t_gc", [128, 16], F32)
    t_ttf = A.alloc("t_ttf", [128, 8 * 256], F32)
    t_ttb = A.alloc("t_ttb", [128, 8 * 256], BF16)
    t_ssb = A.alloc("t_ssb", [128, 16 * 32], BF16)
    t_lA = A.alloc("t_lA", [128, 4 * D], BF16)
    t_lB = A.alloc("t_lB", [128, 4 * FW], BF16)
    t_lAsT = A.alloc("t_lAsT", [128, 32 * 32], BF16)
    t_wqs = [A.alloc("t_wqs%d" % i, [128, 2 * D], BF16) for i in range(2)]
    t_wqeh = [A.alloc("t_wqeh%d" % i, [128, NCT * 256], BF16) for i in range(2)]
    t_aaug = A.alloc("t_aaug", [128, NCT * 64], BF16)
    t_baug = A.alloc("t_baug", [64, FW], BF16)

    for src, dst in ((i128, p_i128), (ones128, p_ones128), (lAnr, t_lA),
                     (lBnr, t_lB), (lAsT, t_lAsT)):
        S.dma("gp", "g_once", lambda e, s=src, d=dst: e.dma_start(out=d[:], in_=s[:]))
    gload = S.dma("gp", "g_once",
                  lambda e: e.dma_start(out=t_baug[32:64, :], in_=lBsT[:]))
    wv_pc = S.dma("gp", "g_pc0", lambda e: e.dma_start(out=s_wvb[:], in_=wvT[:]))
    kern_pc = S.dma("gp", "g_pc1", lambda e: e.dma_start(out=s_kernb[:], in_=kern[:]))
    for src, dst in ((tcd, t_tcd), (i32t, t_i32t), (pn, t_pn), (gateT, t_gateT),
                     (bwrow, t_bwrow), (onesf, p_onesf), (bqc, p_bqc),
                     (bkc, p_bkc), (bvrow, t_bvrow)):
        S.dma("sy", "s_once", lambda e, s=src, d=dst: e.dma_start(out=d[:], in_=s[:]))
    sload = S.dma("sy", "s_once",
                  lambda e: e.dma_start(out=t_fbrow[:], in_=fbrow[:]))

    bw_mm = S.op("te", lambda e: e.matmul(PB[0][:, 0:12], p_onesf[:], t_bwrow[:],
                                          start=True, stop=True), deps=[sload])
    bv_mm = [S.op("te", lambda e, c0=c0: e.matmul(
        PA[0][:, c0:c0 + 512], p_onesf[:], t_bvrow[:, c0:c0 + 512],
        start=True, stop=True), deps=[sload]) for c0 in (0, 512)]
    fb_mm = [S.op("te", lambda e, c0=c0: e.matmul(
        PA[1][:, c0:c0 + 512], p_onesf[:], t_fbrow[:, c0:c0 + 512],
        start=True, stop=True), deps=[sload]) for c0 in (0, 512)]
    gc_mm = []
    for g4 in range(4):
        for tt in range(4):
            c = g4 * 4 + tt
            gc_mm.append(S.op("te", lambda e, c=c, g4=g4, tt=tt: e.matmul(
                PB[1][:, c:c + 1], t_pn[:, tt * 128:(tt + 1) * 128],
                t_gateT[:, g4:g4 + 1], start=True, stop=True), deps=[sload]))
    ev_bw = S.op("ve", lambda e: e.tensor_copy(t_bwbc[:], PB[0][:, 0:12]), deps=[bw_mm])
    ev_bv = S.op("ve", lambda e: e.tensor_copy(p_bvbc[:], PA[0][:, 0:1024]), deps=bv_mm)
    ev_fb = S.op("ve", lambda e: e.tensor_copy(p_fbbc[:], PA[1][:, 0:1024]), deps=fb_mm)
    ev_gc = S.op("ve", lambda e: e.tensor_copy(t_gc[:], PB[1][:, 0:16]), deps=gc_mm)

    s_tiles = []
    for c in range(16):
        s_tiles.append(S.op("ve", lambda e, c=c: e.tensor_scalar(
            t_ssb[:, c * 32:(c + 1) * 32], t_i32t[:], t_gc[:, c:c + 1], None,
            ALU.mult), deps=[ev_gc, sload]))

    tt_done = []
    for h in range(NHC):
        for st in range(2):
            o = (h * 2 + st) * 256
            tds = t_tcd[:, st * 256:(st + 1) * 256]
            tc0 = t_tcd[:, (2 + st) * 256:(3 + st) * 256]
            tc1 = t_tcd[:, (4 + st) * 256:(5 + st) * 256]
            tc2 = t_tcd[:, (6 + st) * 256:(7 + st) * 256]
            d1 = S.op("ve", lambda e, o=o, h=h, tc0=tc0, tds=tds: e.scalar_tensor_tensor(
                t_ttf[:, o:o + 256], tc0, t_bwbc[:, h * 3:h * 3 + 1], tds,
                ALU.mult, ALU.add), deps=[sload, ev_bw], hazard=True)
            d2 = S.op("ve", lambda e, o=o, h=h, tc1=tc1: e.scalar_tensor_tensor(
                t_ttf[:, o:o + 256], tc1, t_bwbc[:, h * 3 + 1:h * 3 + 2],
                t_ttf[:, o:o + 256], ALU.mult, ALU.add), deps=[d1], hazard=True)
            d3 = S.op("ve", lambda e, o=o, h=h, tc2=tc2: e.scalar_tensor_tensor(
                t_ttf[:, o:o + 256], tc2, t_bwbc[:, h * 3 + 2:h * 3 + 3],
                t_ttf[:, o:o + 256], ALU.mult, ALU.add), deps=[d2], hazard=True)
            tt_done.append(d3)
    tt_cast = S.op("ve", lambda e: e.tensor_copy(t_ttb[:], t_ttf[:]), deps=tt_done,
                   hazard=True)

    bqe_mm = []
    for h in range(NHC):
        for rs in range(2):
            col = h * 2 + rs
            mm = None
            for st in range(2):
                mm = S.op("te", lambda e, h=h, rs=rs, st=st, col=col: e.matmul(
                    PB[2][:, col:col + 1],
                    t_ttf[:, (h * 2 + st) * 256 + rs * 128:
                          (h * 2 + st) * 256 + rs * 128 + 128],
                    p_bqc[:, h * 2 + st:h * 2 + st + 1],
                    start=(st == 0), stop=(st == 1)),
                    deps=(tt_done + [sload] if st == 0 else [mm]))
            bqe_mm.append(mm)
    ev_bqe = S.op("ve", lambda e: e.tensor_copy(p_bqe[:], PB[2][:, 0:8]), deps=bqe_mm)

    ab_last = None
    st_a_prev = None
    for g4 in range(4):
        ae_ev = {}
        for isl in range(32):
            mm = None
            for tt in range(4):
                first = (tt == 0)
                dpm = [gload, s_tiles[g4 * 4 + tt]]
                if first and isl >= 8:
                    dpm.append(ae_ev[isl - 8])
                mm = S.op("te", lambda e, g4=g4, isl=isl, tt=tt: e.matmul(
                    PB[3][:, (isl % 8) * 32:(isl % 8) * 32 + 32],
                    t_lA[:, tt * D + isl * 128:tt * D + (isl + 1) * 128],
                    t_ssb[:, (g4 * 4 + tt) * 32:(g4 * 4 + tt + 1) * 32],
                    start=(tt == 0), stop=(tt == 3)),
                    deps=(dpm if first else [mm]))
            if isl % 8 == 7:
                ev = S.op("ve", lambda e, isl=isl: e.tensor_copy(
                    t_aaug[:].rearrange("p (i r) -> p i r", i=NCT)
                    [:, isl - 7:isl + 1, 0:32],
                    PB[3][:, 0:256].rearrange("p (i r) -> p i r", i=8)),
                    deps=[mm] + ([st_a_prev] if isl == 7 else []))
                for k in range(isl - 7, isl + 1):
                    ae_ev[k] = ev
        sh_cp = S.op("ve", lambda e: e.tensor_copy(
            t_aaug[:].rearrange("p (i r) -> p i r", i=NCT)[:, :, 32:64],
            t_lAsT[:].rearrange("p (i r) -> p i r", i=NCT)),
            deps=[gload] + ([ab_last] if ab_last else []))
        st_a = S.dma("gp", "g_ab", lambda e, g4=g4: e.dma_start(
            out=s_aaug[g4], in_=t_aaug[:]), deps=[ae_ev[31], sh_cp])
        st_a_prev = st_a
        be_ev = []
        for oc in range(2):
            mm = None
            for tt in range(4):
                first = (tt == 0)
                dpm = [gload, s_tiles[g4 * 4 + tt], ev_bqe]
                if first and be_ev:
                    dpm.append(be_ev[-1])
                mm = S.op("te", lambda e, g4=g4, oc=oc, tt=tt: e.matmul(
                    PB[2][0:32, 0:512],
                    t_ssb[:, (g4 * 4 + tt) * 32:(g4 * 4 + tt + 1) * 32],
                    t_lB[:, tt * FW + oc * 512:tt * FW + oc * 512 + 512],
                    start=(tt == 0), stop=(tt == 3)),
                    deps=(dpm if first else [mm]))
            be_ev.append(S.op("ve", lambda e, oc=oc: e.tensor_copy(
                t_baug[0:32, oc * 512:oc * 512 + 512], PB[2][0:32, 0:512]),
                deps=[mm]))
        ab_last = S.dma("gp", "g_ab", lambda e, g4=g4: e.dma_start(
            out=s_baug[g4], in_=t_baug[:]), deps=be_ev + [gload, st_a])

    wq_ld = []
    for h in range(NHC):
        wq_ld.append(S.dma("gp", "g_wq%d" % (h % 2), lambda e, h=h: e.dma_start(
            out=t_wqs[h % 2][:], in_=wq[:, h * 2 * D:(h + 1) * 2 * D])))
    fold_done = []
    prev_evs_last = None
    for h in range(NHC):
        evs = []
        for ct in range(NCT):
            slot = ct % 4
            pa = PA[0] if slot < 2 else PA[1]
            lo = (slot % 2) * 512
            mm = None
            for st in range(2):
                first = (st == 0)
                dpm = [wq_ld[h], tt_cast]
                if first and ct < 4:
                    dpm += [ev_bv, ev_fb, prev_evs_last]
                if first and ct >= 4:
                    dpm.append(evs[ct - 4])
                if first and ct == 0 and h >= 2:
                    dpm.append(fold_done[h - 2])
                mm = S.op("te", lambda e, h=h, ct=ct, st=st, pa=pa, lo=lo: e.matmul(
                    pa[:, lo:lo + 256],
                    t_wqs[h % 2][:, st * D + ct * 128:st * D + (ct + 1) * 128],
                    t_ttb[:, (h * 2 + st) * 256:(h * 2 + st + 1) * 256],
                    start=(st == 0), stop=(st == 1)),
                    deps=(dpm if first else [mm]))
            evs.append(S.op("ve", lambda e, h=h, ct=ct, pa=pa, lo=lo: e.tensor_copy(
                t_wqeh[h % 2][:, ct * 256:(ct + 1) * 256], pa[:, lo:lo + 256]),
                deps=[mm]))
        stw = S.dma("sy", "s_wqe%d" % (h % 2), lambda e, h=h: e.dma_start(
            out=s_wqe[:, h * NCT * 256:(h + 1) * NCT * 256], in_=t_wqeh[h % 2][:]),
            deps=[evs[-1]])
        fold_done.append(stw)
        prev_evs_last = evs[-1]
        if h + 2 < NHC:
            wq_ld[h + 2].deps.append(evs[-1])

    p0_done = [ab_last, fold_done[-1], fold_done[-2], ev_bv, ev_fb, ev_bqe,
               gload, sload]

    A.phase_reset()
    t_wkT = A.alloc("t_wkT", [128, NCT * FW], BF16)
    t_xt = [A.alloc("t_xt%d" % i, [128, D], BF16) for i in range(2)]
    t_xtb = [A.alloc("t_xtb%d" % i, [128, 16 * 512], BF16) for i in range(2)]
    t_xnt = [A.alloc("t_xnt%d" % i, [128, NCT * 512], BF16) for i in range(2)]
    t_stk = [A.alloc("t_stk0", [128, 8 * 512], BF16)] * 2
    t_tmp = A.alloc("t_tmp", [128, 2 * 512], F32)
    t_bns = [A.alloc("t_bns%d" % i, [128, 48], F32) for i in range(4)]
    t_mv = [A.alloc("t_mv%d" % i, [128, 2], F32) for i in range(4)]
    t_lnv = [A.alloc("t_lnv%d" % i, [128, 1], F32) for i in range(4)]
    t_veps = [A.alloc("t_veps%d" % i, [128, 1], F32) for i in range(4)]
    t_mrc = [A.alloc("t_mrc%d" % i, [128, 8], F32) for i in range(2)]
    t_mrow = [A.alloc("t_mrow%d" % i, [1, 512], F32) for i in range(2)]
    t_rrow = [A.alloc("t_rrow%d" % i, [1, 512], F32) for i in range(2)]

    wk_ld = S.dma("gp", "g_wk", lambda e: e.dma_start(out=t_wkT[:], in_=wkT[:]),
                  deps=p0_done)
    xbTv = xbT[:].rearrange("p (c l) -> p c l", c=NCT)

    xload = [None] * NLT
    bn_last = [None] * NLT
    stat_done = [None] * NLT
    stats_ring = {}
    kmm_last = {}
    kevac = {}
    kdma = {}
    xnt_st = {}
    tt_last = {}
    tth_last = {}
    bc_ops = {}
    ev_mr = {}

    def p1_stats(lt):
        lbpar = (lt // 4) % 2
        par4 = lt % 4
        slot = lt % 2
        t4 = lt % 4
        dpx = list(p0_done) if lt < 2 else [bn_last[lt - 2]]
        xload[lt] = S.dma("gp", "g_x%d" % slot, lambda e, lt=lt, slot=slot: e.dma_start(
            out=t_xt[slot][:], in_=xb[lt * 128:(lt + 1) * 128, :]), deps=dpx)
        pv = stats_ring.get(lt - 4, {})
        bns = []
        for j2 in range(8):
            bns.append(S.op("ve", lambda e, slot=slot, j2=j2, par4=par4: e.bn_stats(
                t_bns[par4][:, j2 * 6:(j2 + 1) * 6],
                t_xt[slot][:, j2 * 512:(j2 + 1) * 512]),
                deps=[xload[lt]] + ([pv.get("aggr")] if j2 == 0 else []),
                hazard=True))
        bn_last[lt] = bns[-1]
        aggr = S.op("ve", lambda e, par4=par4: e.bn_aggr(
            t_mv[par4][:, 0:2], t_bns[par4][:]),
            deps=bns + [pv.get("veps"), pv.get("negm")], hazard=True)
        veps = S.op("ve", lambda e, par4=par4: e.tensor_scalar_add(
            t_veps[par4][:], t_mv[par4][:, 1:2], EPS),
            deps=[aggr, pv.get("lnv")], hazard=True)
        lnv = S.op("ac", lambda e, par4=par4: e.activation(
            t_lnv[par4][:], t_veps[par4][:], AF.Ln),
            deps=[veps, pv.get("rstd")], hazard=True)
        wdep = [lnv]
        if lt >= 8:
            wdep.append(bc_ops[lt // 4 - 2])
        rstd = S.op("ac", lambda e, par4=par4, lbpar=lbpar, t4=t4: e.activation(
            t_mrc[lbpar][:, 4 + t4:5 + t4], t_lnv[par4][:], AF.Exp, scale=-0.5),
            deps=wdep, hazard=True)
        negm = S.op("ve", lambda e, par4=par4, lbpar=lbpar, t4=t4: e.tensor_scalar_mul(
            t_mrc[lbpar][:, t4:t4 + 1], t_mv[par4][:, 0:1], -1.0),
            deps=[aggr] + ([bc_ops[lt // 4 - 2]] if lt >= 8 else []), hazard=True)
        stat_done[lt] = (rstd, negm)
        stats_ring[lt] = {"aggr": aggr, "veps": veps, "lnv": lnv,
                          "rstd": rstd, "negm": negm}

    def p1_mr_tr(lblk):
        lbpar = lblk % 2
        deps = [op for lt in range(lblk * 4, lblk * 4 + 4)
                for op in stat_done[lt]]
        if lblk >= 1:
            deps.append(ev_mr[lblk - 1])
        trs = []
        for j2 in range(8):
            pbx = PB[2 + j2 // 4]
            trs.append(S.op("te", lambda e, j2=j2, lbpar=lbpar, pbx=pbx: e.transpose(
                pbx[0:1, (j2 % 4) * 128:(j2 % 4) * 128 + 128],
                t_mrc[lbpar][:, j2:j2 + 1], p_i128[:]),
                deps=(deps if j2 == 0 else [])))
        edep = [trs[-1]]
        if lblk >= 2:
            edep.append(bc_ops[lblk - 2])
        e1 = S.op("ve", lambda e, lbpar=lbpar: e.tensor_copy(
            t_mrow[lbpar][:], PB[2][0:1, 0:512]), deps=edep)
        e2 = S.op("ve", lambda e, lbpar=lbpar: e.tensor_copy(
            t_rrow[lbpar][:], PB[3][0:1, 0:512]), deps=[trs[-1], e1])
        ev_mr[lblk] = e2

    def p1_mr_bc(lblk):
        lbpar = lblk % 2
        bdep = [ev_mr[lblk]]
        if lblk >= 2:
            bdep.append(tt_last[lblk - 2])
        b1 = S.op("te", lambda e, lbpar=lbpar: e.matmul(
            PA[lbpar][:, 0:512], p_onesf[:], t_mrow[lbpar][:],
            start=True, stop=True), deps=bdep)
        b2 = S.op("te", lambda e, lbpar=lbpar: e.matmul(
            PA[lbpar][:, 512:1024], p_onesf[:], t_rrow[lbpar][:],
            start=True, stop=True), deps=[b1])
        bc_ops[lblk] = b2

    def p1_norm(lblk):
        lbpar = lblk % 2
        prev = tt_last.get(lblk - 1)
        for half in range(2):
            hdep = [tth_last[(lblk - 1, half)]] if lblk >= 1 else []
            xl = S.dma("gp", "g_xt%d" % half, lambda e, lblk=lblk, half=half: e.dma_start(
                out=t_xtb[half][:],
                in_=xbTv[:, half * 16:(half + 1) * 16,
                         lblk * 512:(lblk + 1) * 512]),
                deps=hdep)
            for q in range(8):
                o1 = None
                for k in range(2):
                    c16 = q * 2 + k
                    d1 = [xl, bc_ops[lblk]]
                    if lblk >= 2 and half == 0 and c16 == 0:
                        d1 += [kmm_last[(lblk - 2, 7)], xnt_st[lblk - 2]]
                    if k == 0 and prev is not None:
                        d1.append(prev)
                    o1 = S.op("ve", lambda e, c16=c16, k=k, lbpar=lbpar, half=half: e.tensor_tensor(
                        t_tmp[:, k * 512:(k + 1) * 512],
                        t_xtb[half][:, c16 * 512:(c16 + 1) * 512],
                        PA[lbpar][:, 0:512], ALU.add),
                        deps=d1, hazard=(k == 1))
                for k in range(2):
                    ct = half * 16 + q * 2 + k
                    prev = S.op("ve", lambda e, ct=ct, k=k, lbpar=lbpar: e.tensor_tensor(
                        t_xnt[lbpar][:, ct * 512:(ct + 1) * 512],
                        t_tmp[:, k * 512:(k + 1) * 512],
                        PA[lbpar][:, 512:1024], ALU.mult),
                        deps=[o1], hazard=(k == 1))
            tth_last[(lblk, half)] = prev
        tt_last[lblk] = prev
        xnt_st[lblk] = S.dma("sy", "s_xnt%d" % lbpar,
                             lambda e, lblk=lblk, lbpar=lbpar: e.dma_start(
            out=s_xnt[lblk], in_=t_xnt[lbpar][:]), deps=[prev])

    def p1_kt(lblk):
        bpar = lblk % 2
        last_tt = tt_last[lblk]
        for ft in range(8):
            pb = PB[ft % 2]
            mm = None
            for ct in range(NCT):
                first = (ct == 0)
                dpm = [last_tt, wk_ld]
                if first and ft >= 2:
                    dpm.append(kevac[(lblk, ft - 2)])
                if first and ft < 2 and lblk >= 1:
                    dpm.append(kevac[(lblk - 1, 6 + ft)])
                mm = S.op("te", lambda e, ft=ft, ct=ct, pb=pb, bpar=bpar: e.matmul(
                    pb[:, 0:512],
                    t_wkT[:, ct * FW + ft * 128:ct * FW + (ft + 1) * 128],
                    t_xnt[bpar][:, ct * 512:(ct + 1) * 512],
                    start=(ct == 0), stop=(ct == NCT - 1)),
                    deps=(dpm if first else [mm]))
            kmm_last[(lblk, ft)] = mm
            edp = [mm]
            if lblk >= 1:
                edp.append(kdma[(lblk - 1, 7)])
            kevac[(lblk, ft)] = S.op("ac", lambda e, ft=ft, pb=pb, bpar=bpar: e.activation(
                t_stk[bpar][:, ft * 512:(ft + 1) * 512], pb[:, 0:512],
                AF.Identity, bias=p_bkc[:, ft:ft + 1]), deps=edp)
            kdma[(lblk, ft)] = S.dma("sy", "s_kd%d" % bpar,
                                     lambda e, lblk=lblk, ft=ft, bpar=bpar: e.dma_start(
                out=s_kt[ft // 2, ft % 2, :, lblk * 512:(lblk + 1) * 512],
                in_=t_stk[bpar][:, ft * 512:(ft + 1) * 512]),
                deps=[kevac[(lblk, ft)]])

    for lt in range(min(8, NLT)):
        p1_stats(lt)
    p1_mr_tr(0)
    p1_mr_bc(0)
    for lblk in range(NLB):
        if lblk + 1 < NLB:
            p1_mr_tr(lblk + 1)
        for lt in range(4 * (lblk + 2), min(4 * (lblk + 3), NLT)):
            p1_stats(lt)
        if lblk >= 1:
            p1_kt(lblk - 1)
        if lblk + 1 < NLB:
            p1_mr_bc(lblk + 1)
        p1_norm(lblk)
    p1_kt(NLB - 1)
    p1_tail = [kdma[(NLB - 1, 7)], kdma[(NLB - 2, 7)],
               xnt_st[NLB - 1], xnt_st[NLB - 2]]

    A.phase_reset()
    t_wvT = A.alloc("t_wvT", [128, NCT * FW], BF16)
    t_xv = [A.alloc("t_xv%d" % i, [128, NCT * 512], BF16) for i in range(2)]
    t_stv = [A.alloc("t_stv%d" % i, [128, FW], BF16) for i in range(2)]

    lev = (NLB - 1) if (NLB - 1) % 2 == 0 else (NLB - 2)
    lod = (NLB - 1) if (NLB - 1) % 2 == 1 else (NLB - 2)
    p1_guards = [
        (A.ranges[id(t_wkT)], [kmm_last[(NLB - 1, 7)]]),
        (A.ranges[id(t_xt[0])], [bn_last[NLT - 1], bn_last[NLT - 2]]),
        (A.ranges[id(t_xt[1])], [bn_last[NLT - 1], bn_last[NLT - 2]]),
        (A.ranges[id(t_xtb[0])], [tt_last[NLB - 1]]),
        (A.ranges[id(t_xtb[1])], [tt_last[NLB - 1]]),
        (A.ranges[id(t_xnt[0])], [kmm_last[(lev, 7)], xnt_st[lev]]),
        (A.ranges[id(t_xnt[1])], [kmm_last[(lod, 7)], xnt_st[lod]]),
        (A.ranges[id(t_stk[0])], [kdma[(NLB - 1, 7)], kdma[(NLB - 2, 7)]]),
        (A.ranges[id(t_tmp)], [tt_last[NLB - 1]]),
        (A.ranges[id(t_bns[0])], [tt_last[NLB - 1], stat_done[NLT - 1][0],
                                  stat_done[NLT - 1][1]]),
    ]

    def p1_guard(t):
        lo, hi = A.ranges[id(t)]
        deps = []
        for (glo, ghi), ds2 in p1_guards:
            if glo < hi and lo < ghi:
                deps += ds2
        if hi > A.ranges[id(t_bns[0])][0]:
            deps += [tt_last[NLB - 1], stat_done[NLT - 1][0], stat_done[NLT - 1][1],
                     ev_mr[NLB - 1], bc_ops[NLB - 1]]
        return deps

    wv_ld = S.dma("gp", "g_wv", lambda e: e.dma_start(out=t_wvT[:], in_=s_wvb[:]),
                  deps=p1_guard(t_wvT) + [wv_pc])
    xv_ld = {}
    vmm_last = {}
    vevac = {}
    vdma = {}
    for lblk in range(NLB):
        par = lblk % 2
        dp = (p1_guard(t_xv[par]) if lblk < 2 else [vmm_last[(lblk - 2, 3)]]) + \
            [xnt_st[lblk]]
        xv_ld[lblk] = S.dma("sy", "s_xv%d" % par, lambda e, lblk=lblk, par=par: e.dma_start(
            out=t_xv[par][:], in_=s_xnt[lblk]), deps=dp)
        for mt in range(4):
            pa = PA[mt % 2]
            mm = None
            for oc in range(2):
                for ct in range(NCT):
                    first = (oc == 0 and ct == 0)
                    dpm = [xv_ld[lblk], wv_ld]
                    if first:
                        key = (lblk, mt - 2) if mt >= 2 else (lblk - 1, mt + 2)
                        dpm.append(vevac.get(key))
                    mm = S.op("te", lambda e, mt=mt, oc=oc, ct=ct, pa=pa, par=par: e.matmul(
                        pa[:, oc * 512:(oc + 1) * 512],
                        t_xv[par][:, ct * 512 + mt * 128:ct * 512 + (mt + 1) * 128],
                        t_wvT[:, ct * FW + oc * 512:ct * FW + (oc + 1) * 512],
                        start=(ct == 0), stop=(ct == NCT - 1)),
                        deps=(dpm if first else [mm]))
            vmm_last[(lblk, mt)] = mm
            edp = [mm, ev_bv]
            key = (lblk, mt - 2) if mt >= 2 else (lblk - 1, mt + 2)
            if key in vdma:
                edp.append(vdma[key][-1])
            elif lblk == 0:
                edp += p1_guard(t_stv[mt % 2])
            vevac[(lblk, mt)] = S.op("ve", lambda e, mt=mt, pa=pa: e.tensor_tensor(
                t_stv[mt % 2][:], pa[:, 0:1024], p_bvbc[:], ALU.add), deps=edp)
            vdma[(lblk, mt)] = [S.dma("gp", "g_vd%d" % (mt % 2),
                                      lambda e, lblk=lblk, mt=mt, h=h: e.dma_start(
                out=s_vv[h, lblk * 4 + mt],
                in_=t_stv[mt % 2][:, h * 256:(h + 1) * 256]),
                deps=[vevac[(lblk, mt)]]) for h in range(NHC)]
    p2v_tail = [vdma[(NLB - 1, 2)][-1], vdma[(NLB - 1, 3)][-1]]

    A.phase_reset()
    t_wqe = A.alloc("t_wqe", [128, NHC * NCT * 256], BF16)
    t_xq = [A.alloc("t_xq%d" % i, [128, NCT * 512], BF16) for i in range(2)]
    t_stq = [A.alloc("t_stq%d" % i, [128, 512], BF16) for i in range(2)]

    def p2v_guard(t):
        lo, hi = A.ranges[id(t)]
        deps = []
        for tt2, ds2 in ((t_wvT, [vmm_last[(NLB - 1, 3)]]),
                         (t_xv[0], [vmm_last[(NLB - 1, 3)]]),
                         (t_xv[1], [vmm_last[(NLB - 1, 3)]]),
                         (t_stv[0], [vdma[(NLB - 1, 2)][-1]]),
                         (t_stv[1], [vdma[(NLB - 1, 3)][-1]])):
            glo, ghi = A.ranges[id(tt2)]
            if glo < hi and lo < ghi:
                deps += ds2
        if hi > A.ranges[id(t_stv[1])][1]:
            deps += p2v_tail
        return deps

    wqe_ld = S.dma("sy", "s_wqeld", lambda e: e.dma_start(out=t_wqe[:], in_=s_wqe[:]),
                   deps=p2v_guard(t_wqe) + [fold_done[-1], fold_done[-2]])
    xq_ld = {}
    qmm_last = {}
    qevac = {}
    qdma = {}
    for lblk in range(NLB):
        par = lblk % 2
        dp = (p2v_guard(t_xq[par]) if lblk < 2 else [qmm_last[(lblk - 2, 7)]]) + \
            [xnt_st[lblk]]
        xq_ld[lblk] = S.dma("sy", "s_xq%d" % par, lambda e, lblk=lblk, par=par: e.dma_start(
            out=t_xq[par][:], in_=s_xnt[lblk]), deps=dp)
        for rt in range(8):
            h, rs = rt // 2, rt % 2
            pb = PB[rt % 2]
            mm = None
            for ct in range(NCT):
                first = (ct == 0)
                dpm = [xq_ld[lblk], wqe_ld]
                if first:
                    key = (lblk, rt - 2) if rt >= 2 else (lblk - 1, rt + 6)
                    dpm.append(qevac.get(key))
                mm = S.op("te", lambda e, h=h, rs=rs, ct=ct, pb=pb, par=par: e.matmul(
                    pb[:, 0:512],
                    t_wqe[:, (h * NCT + ct) * 256 + rs * 128:
                          (h * NCT + ct) * 256 + (rs + 1) * 128],
                    t_xq[par][:, ct * 512:(ct + 1) * 512],
                    start=(ct == 0), stop=(ct == NCT - 1)),
                    deps=(dpm if first else [mm]))
            qmm_last[(lblk, rt)] = mm
            edp = [mm, ev_bqe]
            key = (lblk, rt - 2) if rt >= 2 else (lblk - 1, rt + 6)
            edp.append(qdma.get(key))
            if key not in qdma and lblk == 0:
                edp += p2v_guard(t_stq[rt % 2])
            qevac[(lblk, rt)] = S.op("ac", lambda e, rt=rt, pb=pb: e.activation(
                t_stq[rt % 2][:], pb[:, 0:512], AF.Identity,
                bias=p_bqe[:, rt:rt + 1]), deps=edp)
            qdma[(lblk, rt)] = S.dma("gp", "g_qd%d" % (rt % 2),
                                     lambda e, lblk=lblk, rt=rt: e.dma_start(
                out=s_qt[rt // 2, rt % 2, :, lblk * 512:(lblk + 1) * 512],
                in_=t_stq[rt % 2][:]), deps=[qevac[(lblk, rt)]])
    p2_tail = [qdma[(NLB - 1, 6)], qdma[(NLB - 1, 7)]]

    A.phase_reset()
    t_vs2 = [A.alloc("t_vs%d" % i, [128, NLT * 257], BF16) for i in range(2)]
    t_sqpe = A.alloc("t_sqpe", [128, 4 * L], BF16)
    t_sq = t_sqpe
    t_mx = A.alloc("t_mx", [128, 40], F32)
    t_sums = A.alloc("t_sums", [128, 4], F32)
    t_rec = A.alloc("t_rec", [128, 4], F32)
    t_sta = [A.alloc("t_sta%d" % i, [128, NHC * 256], BF16) for i in range(2)]
    t_kt = A.alloc("t_kt", [128, 2 * L], BF16)
    t_qt = A.alloc("t_qt", [128, 2 * L], BF16)
    q_ext = A.ranges[id(t_stq[1])][1]
    v_ext = A.ranges[id(t_stv[1])][1]
    assert A.ranges[id(t_kt)][0] >= max(q_ext, v_ext), (
        A.ranges[id(t_kt)], q_ext, v_ext)

    q_last = qmm_last[(NLB - 1, 7)]
    vs_memset = [S.op("ve", lambda e, i=i: e.memset(
        t_vs2[i][:].rearrange("p (m f) -> p m f", m=NLT)[:, :, 256:257], 1.0),
        deps=[q_last]) for i in range(2)]

    exp_ops = []
    attn_ev = {}
    attn_dma = {}
    head_explast = {}
    head_avlast = {}
    avdone = {}
    avof = {}
    expof = {}
    explast = {}
    pend = [None]

    CG = NG // 2
    nch2 = 2 * NQB

    def pexp_ap(gK, gc, sl=None):
        base = (gK % 2) * 2 * L + gc * 1024
        if sl is None:
            return t_sqpe[:, base:base + 1024]
        return t_sqpe[:, base + sl[0]:base + sl[1]]

    def emit_denoms(hp, lb, avl):
        cps = []
        for ls in range(4):
            cps.append(S.op("ve", lambda e, ls=ls: e.tensor_copy(
                t_sums[:, ls:ls + 1], PB[ls][:, 256:257]), deps=[avl]))
        rec = S.op("ve", lambda e: e.reciprocal(t_rec[:], t_sums[:]),
                   deps=cps, hazard=True)
        spar = lb % 2
        evs = []
        for ls in range(4):
            edp2 = [rec, avl]
            wkey = (hp, lb - 2) if lb >= 2 else (hp - 1, NQB - 2 + lb)
            if wkey in attn_dma:
                edp2.append(attn_dma[wkey][-1])
            evs.append(S.op("ve", lambda e, ls=ls, spar=spar: e.tensor_scalar_mul(
                t_sta[spar][:, ls * 256:(ls + 1) * 256],
                PB[ls][:, 0:256], t_rec[:, ls:ls + 1]),
                deps=edp2))
        attn_ev[(hp, lb)] = evs
        attn_dma[(hp, lb)] = [S.dma("gp", "g_att%d" % spar,
                                    lambda e, hp=hp, lb=lb, ls=ls, spar=spar: e.dma_start(
            out=s_att[lb * 4 + ls, :, hp * 256:(hp + 1) * 256],
            in_=t_sta[spar][:, ls * 256:(ls + 1) * 256]),
            deps=[evs[ls]]) for ls in range(4)]

    def emit_av_group(gc, lds_cur):
        hp, Xp, gKp, t_vsp = pend[0]
        lbp, cp_ = Xp // 2, Xp % 2
        gp = cp_ * CG + gc
        avdp0 = [expof[(gKp, gc)], lds_cur]
        if gc == 0 and cp_ == 0:
            prev_ev = (attn_ev[(hp, lbp - 1)] if lbp >= 1 else
                       (attn_ev[(hp - 1, NQB - 1)] if hp >= 1 else []))
            avdp0 += prev_ev
        avmm = None
        for kq in range(2):
            kt = 2 * gp + kq
            for ls in range(4):
                first = (kq == 0 and ls == 0)
                avmm = S.op("te", lambda e, gKp=gKp, gc=gc, kq=kq, ls=ls, kt=kt, gp=gp, cp_=cp_, t_vsp=t_vsp: e.matmul(
                    PB[ls][:, 0:257],
                    pexp_ap(gKp, gc, (kq * 512 + ls * 128,
                                      kq * 512 + (ls + 1) * 128)),
                    t_vsp[:, kt * 257:(kt + 1) * 257],
                    start=(cp_ == 0 and gp == 0 and kq == 0),
                    stop=(cp_ == 1 and gp == NG - 1 and kq == 1)),
                    deps=(avdp0 if first else [avmm]))
        avdone[gKp] = avmm
        avof[(gKp, gc)] = avmm
        if gc == CG - 1:
            if cp_ == 1:
                emit_denoms(hp, lbp, avmm)
            if Xp == nch2 - 1:
                head_avlast[hp] = avmm
            pend[0] = None
        return avmm

    kt_done = {}
    qt_done = {}
    smm_of = {}
    smm_head_last = {}
    LH = L // 2
    for h in range(NHC):
        t_vs = t_vs2[h % 2]
        dpv = [vs_memset[h % 2]] + \
            ([head_avlast[h - 2]] if h >= 2 else [q_last] + p2v_tail)
        lds = None
        for mt8 in range(0, NLT, 8):
            lds = S.dma("sy", "s_h3", lambda e, h=h, mt8=mt8, t_vs=t_vs: e.dma_start(
                out=t_vs[:].rearrange("p (m f) -> p m f", m=NLT)
                [:, mt8:mt8 + 8, 0:256],
                in_=s_vv[h, mt8:mt8 + 8].rearrange("m p f -> p m f")),
                deps=(dpv if mt8 == 0 else []))
        for ch in range(2):
            dpk = [smm_head_last[h - 1]] if h >= 1 else (
                p1_guard(t_kt) + [kdma[(NLB - 1, 7)], kdma[(NLB - 2, 7)]])
            kt_done[(h, ch)] = [S.dma("ac", "a_k3",
                                      lambda e, h=h, rs=rs, ch=ch: e.dma_start(
                out=t_kt[:, rs * L + ch * LH:rs * L + (ch + 1) * LH],
                in_=s_kt[h, rs, :, ch * LH:(ch + 1) * LH]),
                deps=(dpk if ch == 0 and rs == 0 else []))
                for rs in range(2)]
        for lb in range(NQB):
            if h == 0:
                dq0 = [qdma[(lb, 0)], qdma[(lb, 1)]]
                if lb == 0:
                    dq0 = dq0 + p1_guard(t_qt)
                dq1 = dq0
            else:
                dq0 = [smm_of[(h - 1, 2 * lb + 1)]]
                dq1 = dq0
            qt_done[(h, lb)] = [
                S.dma("sy", "s_q3", lambda e, h=h, lb=lb: e.dma_start(
                    out=t_qt[:, lb * 512:(lb + 1) * 512],
                    in_=s_qt[h, 0, :, lb * 512:(lb + 1) * 512]), deps=dq0),
                S.dma("gp", "g_q3", lambda e, h=h, lb=lb: e.dma_start(
                    out=t_qt[:, L + lb * 512:L + (lb + 1) * 512],
                    in_=s_qt[h, 1, :, lb * 512:(lb + 1) * 512]), deps=dq1)]
        ktq_all = (kt_done[(h, 0)] + kt_done[(h, 1)] +
                   qt_done[(h, NQB - 1)])

        gK0 = h * nch2
        fms = []
        norm_red_last = {}
        sq_mm_last = None
        for side, (src2, base) in enumerate(((t_qt, 0), (t_kt, 16))):
            muls = []
            for ct in range(2):
                dpm = list(ktq_all)
                if h == 0 and side == 0 and ct == 0:
                    dpm += [qmm_last[(NLB - 1, 7)], qmm_last[(NLB - 2, 7)]]
                    dpm += p2_tail
                if h >= 1:
                    dpm.append(avdone[gK0 - 2])
                if side == 1:
                    dpm.append(sq_mm_last)
                muls.append(S.op("ve", lambda e, src2=src2, ct=ct: e.tensor_mul(
                    t_sq[:, ct * L:(ct + 1) * L], src2[:, ct * L:(ct + 1) * L],
                    src2[:, ct * L:(ct + 1) * L]), deps=dpm))
            nch = L // 512
            pan = PA[side]
            reds = []
            for ch in range(nch):
                lo = (ch % 2) * 512
                mm = None
                for ct in range(2):
                    first = (ct == 0)
                    dpm = muls + [gload]
                    if first and ch >= 2:
                        dpm.append(reds[ch - 2])
                    if first and ch < 2 and h >= 1:
                        dpm.append(explast[gK0 - 1])
                    mm = S.op("te", lambda e, ch=ch, ct=ct, pan=pan, lo=lo: e.matmul(
                        pan[:, lo:lo + 512], p_ones128[:],
                        t_sq[:, ct * L + ch * 512:ct * L + (ch + 1) * 512],
                        start=(ct == 0), stop=(ct == 1)),
                        deps=(dpm if first else [mm]))
                sq_mm_last = mm
                reds.append(S.op("ve", lambda e, ch=ch, base=base, pan=pan, lo=lo: e.reduce_max(
                    t_mx[:, base + ch:base + ch + 1], pan[:, lo:lo + 512], axis=AX.X),
                    deps=[mm], hazard=True))
            norm_red_last[side] = reds[-1]
            fms.append(S.op("ve", lambda e, base=base, side=side, nch=nch: e.reduce_max(
                t_mx[:, 32 + side:33 + side], t_mx[:, base:base + nch], axis=AX.X),
                deps=reds, hazard=True))
        prod = S.op("ve", lambda e: e.tensor_mul(
            t_mx[:, 34:35], t_mx[:, 32:33], t_mx[:, 33:34]), deps=fms, hazard=True)
        lnc = S.op("ac", lambda e: e.activation(
            t_mx[:, 35:36], t_mx[:, 34:35], AF.Ln),
            deps=[prod] + (p2_tail if h == 0 else []), hazard=True)
        cexp = S.op("ac", lambda e: e.activation(
            t_mx[:, 36:37], t_mx[:, 35:36], AF.Exp, scale=0.5),
            deps=[lnc], hazard=True)
        negc = S.op("ve", lambda e, h=h: e.tensor_scalar(
            p_negC[:, h:h + 1], t_mx[:, 36:37], -1.0, None, ALU.mult),
            deps=[cexp], hazard=True)

        for K in range(nch2):
            lb, c = K // 2, K % 2
            gK = gK0 + K
            for gc in range(CG):
                g = c * CG + gc
                gg = len(exp_ops)
                pa = PA[gg % 2]
                if pend[0] is not None:
                    emit_av_group(gc, lds)
                smm = None
                sdeps = kt_done[(h, c)] + qt_done[(h, lb)]
                if gg >= 2:
                    sdeps.append(exp_ops[gg - 2])
                if K == 0 and gc < 2:
                    sdeps.append(norm_red_last[gg % 2])
                for kq in range(2):
                    kt = 2 * g + kq
                    for ct in range(2):
                        first = (kq == 0 and ct == 0)
                        smm = S.op("te", lambda e, lb=lb, kt=kt, ct=ct, kq=kq, pa=pa: e.matmul(
                            pa[:, kq * 512:(kq + 1) * 512],
                            t_kt[:, ct * L + kt * 128:ct * L + (kt + 1) * 128],
                            t_qt[:, ct * L + lb * 512:ct * L + (lb + 1) * 512],
                            start=(ct == 0), stop=(ct == 1)),
                            deps=(sdeps if first else [smm]))
                edp = [smm, negc]
                if (gK - 2, gc) in avof:
                    edp.append(avof[(gK - 2, gc)])
                if gc == 0 and K == 0:
                    edp.append(sq_mm_last)
                eo = S.op("ac", lambda e, h=h, gK=gK, gc=gc, pa=pa: e.activation(
                    pexp_ap(gK, gc), pa[:, 0:1024], AF.Exp,
                    bias=p_negC[:, h:h + 1]), deps=edp)
                exp_ops.append(eo)
                expof[(gK, gc)] = eo
            smm_of[(h, K)] = smm
            explast[gK] = exp_ops[-1]
            pend[0] = (h, K, gK, t_vs)
        smm_head_last[h] = smm_of[(h, nch2 - 1)]
        head_explast[h] = explast[gK0 + nch2 - 1]
    for gc in range(CG):
        emit_av_group(gc, lds)
    p3_tail = [attn_dma[(NHC - 1, NQB - 1)][-1], attn_dma[(NHC - 1, NQB - 2)][-1],
               attn_ev[(NHC - 1, NQB - 1)][-1]]

    A.phase_reset()
    t_kern = A.alloc("t_kern", [128, NCT * FW], BF16)
    t_xf = [A.alloc("t_xf%d" % i, [128, NCT * 512], BF16) for i in range(2)]
    t_aau = A.alloc("t_aau", [128, NCT * 64], BF16)
    t_bau = A.alloc("t_bau", [64, FW], BF16)
    t_axv = [A.alloc("t_axv%d" % i, [64, 512], BF16) for i in range(2)]
    t_xr = A.alloc("t_xr", [128, 4 * FW], F32)
    t_at = A.alloc("t_at", [128, 4 * FW], F32)
    t_t1 = [A.alloc("t_t1%d" % i, [128, FW], F32) for i in range(2)]
    t_ys = [A.alloc("t_ys%d" % i, [128, FW], F32) for i in range(2)]

    p3_guards = [
        (A.ranges[id(t_kt)], head_explast[NHC - 1]),
        (A.ranges[id(t_qt)], head_explast[NHC - 1]),
        (A.ranges[id(t_vs2[0])], head_avlast.get(NHC - 2)),
        (A.ranges[id(t_vs2[1])], head_avlast[NHC - 1]),
        (A.ranges[id(t_sqpe)], head_avlast[NHC - 1]),
        (A.ranges[id(t_mx)], attn_ev[(NHC - 1, NQB - 1)][-1]),
        (A.ranges[id(t_sums)], attn_ev[(NHC - 1, NQB - 1)][-1]),
        (A.ranges[id(t_rec)], attn_ev[(NHC - 1, NQB - 1)][-1]),
        (A.ranges[id(t_sta[0])], attn_dma[(NHC - 1, NQB - 2)][-1]),
        (A.ranges[id(t_sta[1])], attn_dma[(NHC - 1, NQB - 1)][-1]),
    ]

    def p3_guard(t):
        lo, hi = A.ranges[id(t)]
        deps = []
        for (glo, ghi), dep in p3_guards:
            if glo < hi and lo < ghi and dep is not None:
                deps.append(dep)
        return deps

    kern_ld = S.dma("gp", "g_kern", lambda e: e.dma_start(out=t_kern[:], in_=s_kernb[:]),
                    deps=p3_guard(t_kern) + [kern_pc])
    PER_G = NLB // 4
    xf_ld = {}
    aau_ld = {}
    bau_ld = {}
    at_ld = {}
    xr_ld = {}
    ffn_last = {}
    ax_mm_last = {}
    ax_ev = {}
    t1_op = {}
    y2_op = {}
    y3_op = {}
    ydma = {}
    for lblk in range(NLB):
        par = lblk % 2
        g4 = lblk // PER_G
        if lblk == 0:
            dp = p3_guard(t_xf[0])
        elif lblk == 1:
            dp = p3_guard(t_xf[1])
        else:
            dp = [ffn_last[(lblk - 2, 3)], ax_mm_last[lblk - 2]]
        xf_ld[lblk] = S.dma("sy", "s_xf%d" % par, lambda e, lblk=lblk, par=par: e.dma_start(
            out=t_xf[par][:], in_=s_xnt[lblk]), deps=dp + [xnt_st[lblk]])
        if lblk % PER_G == 0:
            adp = [ax_mm_last[lblk - 1]] if lblk >= 1 else p3_guard(t_aau)
            aau_ld[g4] = S.dma("sy", "s_aau", lambda e, g4=g4: e.dma_start(
                out=t_aau[:], in_=s_aaug[g4]), deps=adp + [ab_last])
            bdp = [ffn_last[(lblk - 1, 3)]] if lblk >= 1 else p3_guard(t_bau)
            bau_ld[g4] = S.dma("sy", "s_bau", lambda e, g4=g4: e.dma_start(
                out=t_bau[:], in_=s_baug[g4]), deps=bdp + [ab_last])
        atdp = ([y3_op[(lblk - 1, lt)] for lt in range(4)]
                if lblk >= 1 else p3_guard(t_at))
        at_ld[lblk] = S.dma("gp", "g_at", lambda e, lblk=lblk: e.dma_start(
            out=t_at[:].rearrange("p (l f) -> p l f", l=4),
            in_=s_att[lblk * 4:lblk * 4 + 4].rearrange("l p f -> p l f")),
            deps=atdp + [attn_dma[(NHC - 1, lblk)][-1]])
        xrdp = ([y2_op[(lblk - 1, lt)] for lt in range(4)]
                if lblk >= 1 else p3_guard(t_xr))
        xr_ld[lblk] = S.dma("sy", "s_xr", lambda e, lblk=lblk: e.dma_start(
            out=t_xr[:].rearrange("p (l f) -> p l f", l=4),
            in_=xres[lblk * 512:(lblk + 1) * 512, :].rearrange(
                "(l p) f -> p l f", p=128)), deps=xrdp)
        pax = PB[2 + par]
        mm = None
        for ct in range(NCT):
            first = (ct == 0)
            dpm = [xf_ld[lblk], aau_ld[g4]]
            if first and lblk < 2:
                dpm.append(attn_ev[(NHC - 1, NQB - 1)][-1])
            if first and lblk >= 2:
                dpm.append(ax_ev[lblk - 2])
            mm = S.op("te", lambda e, ct=ct, par=par, pax=pax: e.matmul(
                pax[0:64, 0:512], t_aau[:, ct * 64:(ct + 1) * 64],
                t_xf[par][:, ct * 512:(ct + 1) * 512],
                start=(ct == 0), stop=(ct == NCT - 1)),
                deps=(dpm if first else [mm]))
        ax_mm_last[lblk] = mm
        ax_ev[lblk] = S.op("ve", lambda e, par=par, pax=pax: e.tensor_copy(
            t_axv[par][:], pax[0:64, 0:512]), deps=[mm])
        for lt in range(4):
            pa = PA[lt % 2]
            mm = None
            for oc in range(2):
                for ct in range(NCT):
                    first = (oc == 0 and ct == 0)
                    dpm = [xf_ld[lblk], kern_ld]
                    if first:
                        key = (lblk, lt - 2) if lt >= 2 else (lblk - 1, lt + 2)
                        dpm.append(t1_op.get(key))
                    mm = S.op("te", lambda e, lt=lt, oc=oc, ct=ct, pa=pa, par=par: e.matmul(
                        pa[:, oc * 512:(oc + 1) * 512],
                        t_xf[par][:, ct * 512 + lt * 128:ct * 512 + (lt + 1) * 128],
                        t_kern[:, ct * FW + oc * 512:ct * FW + (oc + 1) * 512],
                        start=(ct == 0), stop=False),
                        deps=(dpm if first else [mm]))
            for oc in range(2):
                mm = S.op("te", lambda e, lt=lt, oc=oc, pa=pa, par=par: e.matmul(
                    pa[:, oc * 512:(oc + 1) * 512],
                    t_axv[par][:, lt * 128:(lt + 1) * 128],
                    t_bau[:, oc * 512:(oc + 1) * 512],
                    start=False, stop=True),
                    deps=[mm, ax_ev[lblk], bau_ld[g4]])
            ffn_last[(lblk, lt)] = mm
            key = (lblk, lt - 2) if lt >= 2 else (lblk - 1, lt + 2)
            t1_op[(lblk, lt)] = S.op("ve", lambda e, lt=lt, pa=pa: e.tensor_tensor(
                t_t1[lt % 2][:], pa[:, 0:1024], p_fbbc[:], ALU.add),
                deps=[mm, ev_fb, y2_op.get(key)])
            y2_op[(lblk, lt)] = S.op("gp", lambda e, lt=lt: e.tensor_tensor(
                t_ys[lt % 2][:], t_t1[lt % 2][:], t_xr[:, lt * FW:(lt + 1) * FW],
                ALU.add), deps=[t1_op[(lblk, lt)], xr_ld[lblk], ydma.get(key)])
            y3_op[(lblk, lt)] = S.op("ve", lambda e, lt=lt: e.tensor_tensor(
                t_ys[lt % 2][:], t_ys[lt % 2][:], t_at[:, lt * FW:(lt + 1) * FW],
                ALU.add), deps=[y2_op[(lblk, lt)], at_ld[lblk]])
            ydma[(lblk, lt)] = S.dma("gp", "g_y%d" % (lt % 2),
                                     lambda e, lblk=lblk, lt=lt: e.dma_start(
                out=y_out[lblk * 512 + lt * 128:lblk * 512 + (lt + 1) * 128, :],
                in_=t_ys[lt % 2][:]), deps=[y3_op[(lblk, lt)]])

    S.emit(nc, ent)
    return nc


def _prep_core(inputs, b, hg, L):
    F0 = hg * FW
    x = inputs["L0_attn_x"]
    Wq = inputs["L0_attn_Wq"]
    Wk = inputs["L0_attn_Wk"]
    Wv = inputs["L0_attn_Wv"]
    kernw = inputs["L0_ffn_kernel"]

    def tile_pm(a):
        W = a.shape[1]
        return np.ascontiguousarray(
            a.reshape(32, 128, W).transpose(1, 0, 2).reshape(128, 32 * W))

    m = {}
    m["xb"] = np.ascontiguousarray(x[b, :L])
    m["xbT"] = np.ascontiguousarray(
        x[b, :L].T.reshape(32, 128, L).transpose(1, 0, 2).reshape(128, 32 * L))
    m["xres"] = np.ascontiguousarray(x[b, :L, F0:F0 + FW])
    m["wkT"] = tile_pm(np.ascontiguousarray(Wk[F0:F0 + FW].T))
    m["wvT"] = tile_pm(np.ascontiguousarray(Wv[F0:F0 + FW].T))
    m["kern"] = tile_pm(np.ascontiguousarray(kernw[:, F0:F0 + FW]))
    m["wq"] = np.ascontiguousarray(
        Wq[F0:F0 + FW].reshape(8, 128, D).transpose(1, 0, 2).reshape(128, 8 * D))
    m["tcd"] = np.ascontiguousarray(
        _host_tcd().reshape(4, 2, 128, 256).transpose(2, 0, 1, 3).reshape(128, 8 * 256))
    m["i128"] = np.eye(128, dtype=np.float32)
    m["ones128"] = np.ones((128, 128), dtype=ml_dtypes.bfloat16)
    m["onesf"] = np.ones((1, 128), dtype=np.float32)
    m["i32t"] = np.tile(np.eye(32, dtype=np.float32), (4, 1))
    pnm = np.zeros((16, 512), dtype=np.float32)
    for t in range(4):
        for p in range(128):
            pnm[(t * 128 + p) // 32, t * 128 + p] = 1.0
    m["pn"] = pnm
    m["gateT"] = np.ascontiguousarray(inputs["L0_ffn_gate"][4 * b:4 * b + 4].T)
    m["bwrow"] = np.ascontiguousarray(
        inputs["L0_attn_bias_weight"][4 * hg:4 * hg + 4].reshape(1, 12))
    m["bqc"] = np.ascontiguousarray(
        inputs["L0_attn_bq"][F0:F0 + FW].reshape(8, 128).T)
    m["bkc"] = np.ascontiguousarray(
        inputs["L0_attn_bk"][F0:F0 + FW].reshape(8, 128).T)
    m["bvrow"] = np.ascontiguousarray(
        inputs["L0_attn_bv"][F0:F0 + FW].reshape(1, FW))
    m["fbrow"] = np.ascontiguousarray(
        inputs["L0_ffn_bias"][F0:F0 + FW].reshape(1, FW))
    lA = inputs["L0_ffn_lora_A"]
    m["lAnr"] = np.ascontiguousarray(
        lA.reshape(512, D).reshape(4, 128, D).transpose(1, 0, 2).reshape(128, 4 * D))
    lB = inputs["L0_ffn_lora_B"][:, F0:F0 + FW, :]
    lbnr = lB.transpose(0, 2, 1).reshape(512, FW)
    m["lBnr"] = np.ascontiguousarray(
        lbnr.reshape(4, 128, FW).transpose(1, 0, 2).reshape(128, 4 * FW))
    lAs = inputs["L0_ffn_lora_A_shared"]
    m["lAsT"] = np.ascontiguousarray(
        lAs.T.reshape(32, 128, 32).transpose(1, 0, 2).reshape(128, 32 * 32))
    m["lBsT"] = np.ascontiguousarray(inputs["L0_ffn_lora_B_shared"][F0:F0 + FW].T)
    for k in m:
        if m[k].dtype == np.float64:
            m[k] = m[k].astype(np.float32)
    return m


def kernel(**inputs):
    L = int(inputs["L0_attn_x"].shape[1])
    nc = build_program(L)
    in_maps = [_prep_core(inputs, c // 4, c % 4, L) for c in range(8)]
    res = run_bass_kernel_spmd(nc, in_maps, core_ids=list(range(8)))
    y = np.empty((2, L, D), dtype=np.float32)
    for c in range(8):
        y[c // 4, :, (c % 4) * FW:(c % 4 + 1) * FW] = res.results[c]["y"]
    return y


# version 19
# speedup vs baseline: 1.0152x; 1.0152x over previous
import numpy as np
import ml_dtypes
import concourse.bass as bass
import concourse.mybir as mybir
from concourse.bass_utils import run_bass_kernel_spmd

F32 = mybir.dt.float32
BF16 = mybir.dt.bfloat16
AF = mybir.ActivationFunctionType
ALU = mybir.AluOpType
AX = mybir.AxisListType

D = 4096
NCT = 32
NHC = 4
FW = 1024
EPS = 1e-5
SB_BASE = 16544


def _host_tcd():
    n = 8
    masks = sorted(range(n), key=lambda m: (bin(m).count('1'), m))
    inv = {m: i for i, m in enumerate(masks)}
    C = np.zeros((n, n, n))
    for i, a in enumerate(masks):
        for j, bb in enumerate(masks):
            sw, at = (0, a >> 1)
            while at:
                sw += bin(at & bb).count('1')
                at >>= 1
            C[i, j, inv[a ^ bb]] = -1.0 if sw & 1 else 1.0
    grades = np.array([bin(m).count('1') for m in masks])
    rev = np.where(grades * (grades - 1) // 2 % 2 == 1, -1.0, 1.0)
    bivk = C[:, :, [4, 5, 6]]
    TCD = np.zeros((4, 256, 256), dtype=np.float32)
    TCD[0] = np.kron(np.eye(32), np.diag(rev) / 16.0)
    for c in range(3):
        TCD[c + 1] = np.kron(np.eye(32), rev[:, None] * bivk[:, :, c])
    return TCD


ENGS = ("te", "ve", "ac", "sy", "gp")
SEM_OF = {"te": "TE", "ve": "VE", "ac": "AC", "sy": "SY", "gp": "GP"}


class Op:
    __slots__ = ("eng", "fn", "deps", "hazard", "dma", "stream", "pos", "val",
                 "needs_inc", "waits")

    def __init__(self, eng, fn, deps, hazard, dma, stream, pos):
        self.eng, self.fn, self.deps = eng, fn, deps
        self.hazard, self.dma, self.stream, self.pos = hazard, dma, stream, pos
        self.val = 0
        self.needs_inc = dma
        self.waits = []


class Sched:
    def __init__(self):
        self.ops = {e: [] for e in ENGS}
        self.streams = {}

    def op(self, eng, fn, deps=(), hazard=False):
        deps = [d for d in deps if d is not None]
        o = Op(eng, fn, deps, hazard, False, None, len(self.ops[eng]))
        self.ops[eng].append(o)
        return o

    def dma(self, eng, stream, fn, deps=()):
        deps = [d for d in deps if d is not None]
        o = Op(eng, fn, deps, False, True, stream, len(self.ops[eng]))
        self.ops[eng].append(o)
        self.streams.setdefault(stream, []).append(o)
        return o

    def resolve(self):
        def skip_self(d, o):
            return d.eng == o.eng and d.eng == "te" and not d.hazard

        for e in ENGS:
            for o in self.ops[e]:
                for d in o.deps:
                    if (not d.dma) and (not skip_self(d, o)):
                        d.needs_inc = True
        for e in ENGS:
            c = 0
            for o in self.ops[e]:
                if o.dma:
                    continue
                if o.needs_inc:
                    c += 1
                o.val = c
        for st, ops in self.streams.items():
            c = 0
            for o in ops:
                c += 16
                o.val = c
        for e in ENGS:
            seen = {}
            for o in self.ops[e]:
                want = {}
                for d in o.deps:
                    if d.dma:
                        s = "D:" + d.stream
                    elif skip_self(d, o):
                        assert d.pos < o.pos, (e, o.pos, d.pos)
                        continue
                    else:
                        s = SEM_OF[d.eng]
                    want[s] = max(want.get(s, 0), d.val)
                for s, v in sorted(want.items()):
                    if seen.get(s, 0) < v:
                        o.waits.append((s, v))
                        seen[s] = v

    def emit(self, nc, ent):
        self.resolve()
        sems = {}
        for s in set(SEM_OF.values()):
            sems[s] = ent(nc.semaphore(s))
        for st in self.streams:
            sems["D:" + st] = ent(nc.semaphore("D_" + st))
        with nc.Block() as block:
            hooks = {"te": block.tensor, "ve": block.vector, "ac": block.scalar,
                     "sy": block.sync, "gp": block.gpsimd}
            for e in ENGS:
                ops = self.ops[e]
                if not ops:
                    continue

                def body(eng, ops=ops, e=e):
                    for o in ops:
                        for s, v in o.waits:
                            eng.wait_ge(sems[s], v)
                        inst = o.fn(eng)
                        if o.dma:
                            inst.then_inc(sems["D:" + o.stream], 16)
                        elif o.needs_inc:
                            inst.then_inc(sems[SEM_OF[e]], 1)
                    tot = max((o.val for o in ops if not o.dma), default=0)
                    if tot > 0:
                        eng.wait_ge(sems[SEM_OF[e]], tot)
                    done = set()
                    for o in ops:
                        if o.dma and o.stream not in done:
                            done.add(o.stream)
                            eng.wait_ge(sems["D:" + o.stream],
                                        self.streams[o.stream][-1].val)
                hooks[e](body)


def build_program(L):
    assert L % 512 == 0
    NLB = L // 512
    NLT = L // 128
    NKT = L // 128
    NG = NKT // 2
    NQB = L // 512

    nc = bass.Bass("TRN2", target_bir_lowering=False, debug=False)
    ctx = []

    def ent(cm):
        v = cm.__enter__()
        ctx.append(cm)
        return v

    def din(name, shape, dt):
        return nc.dram_tensor(name, shape, dt, kind="ExternalInput")

    xb = din("xb", [L, D], F32)
    xbT = din("xbT", [128, NCT * L], F32)
    xres = din("xres", [L, FW], F32)
    wkT = din("wkT", [128, NCT * FW], F32)
    wvT = din("wvT", [128, NCT * FW], F32)
    kern = din("kern", [128, NCT * FW], F32)
    wq = din("wq", [128, 8 * D], F32)
    tcd = din("tcd", [128, 8 * 256], F32)
    i128 = din("i128", [128, 128], F32)
    ones128 = din("ones128", [128, 128], BF16)
    onesf = din("onesf", [1, 128], F32)
    i32t = din("i32t", [128, 32], F32)
    pn = din("pn", [16, 512], F32)
    gateT = din("gateT", [16, 4], F32)
    bwrow = din("bwrow", [1, 12], F32)
    bqc = din("bqc", [128, 8], F32)
    bkc = din("bkc", [128, 8], F32)
    bvrow = din("bvrow", [1, FW], F32)
    fbrow = din("fbrow", [1, FW], F32)
    lAnr = din("lAnr", [128, 4 * D], F32)
    lBnr = din("lBnr", [128, 4 * FW], F32)
    lAsT = din("lAsT", [128, 32 * 32], F32)
    lBsT = din("lBsT", [32, FW], F32)
    y_out = nc.dram_tensor("y", [L, FW], F32, kind="ExternalOutput")

    s_wqe = nc.dram_tensor("s_wqe", [128, NHC * NCT * 256], BF16)
    s_xnt = nc.dram_tensor("s_xnt", [NLB, 128, NCT * 512], BF16)
    s_kt = nc.dram_tensor("s_kt", [NHC, 2, 128, L], BF16)
    s_vv = nc.dram_tensor("s_vv", [NHC, NLT, 128, 256], BF16)
    s_qt = nc.dram_tensor("s_qt", [NHC, 2, 128, L], BF16)
    s_att = nc.dram_tensor("s_att", [NLT, 128, FW], BF16)
    s_aaug = nc.dram_tensor("s_aaug", [NHC, 128, NCT * 64], BF16)
    s_wvb = nc.dram_tensor("s_wvb", [128, NCT * FW], BF16)
    s_kernb = nc.dram_tensor("s_kernb", [128, NCT * FW], BF16)
    s_baug = nc.dram_tensor("s_baug", [NHC, 64, FW], BF16)

    PA = [ent(nc.psum_tensor("PA%d" % i, [128, 1024], F32)) for i in range(2)]
    PB = [ent(nc.psum_tensor("PB%d" % i, [128, 512], F32)) for i in range(4)]

    class Arena:
        def __init__(self):
            self.off = SB_BASE
            self.persist_end = None
            self.ranges = {}

        def alloc(self, name, shape, dt):
            per = int(np.prod(shape[1:])) * mybir.dt.size(dt)
            per = (per + 31) & ~31
            t = nc.alloc_sbuf_tensor_at(name, list(shape), dt, offset=self.off)
            self.ranges[id(t)] = (self.off, self.off + per)
            self.off += per
            assert self.off <= SB_BASE + 212800, (name, self.off)
            return t

        def mark_persist(self):
            self.persist_end = self.off

        def phase_reset(self):
            self.off = self.persist_end

    A = Arena()
    p_i128 = A.alloc("p_i128", [128, 128], F32)
    p_ones128 = A.alloc("p_ones128", [128, 128], BF16)
    p_onesf = A.alloc("p_onesf", [1, 128], F32)
    p_bqc = A.alloc("p_bqc", [128, 8], F32)
    p_bkc = A.alloc("p_bkc", [128, 8], F32)
    p_bqe = A.alloc("p_bqe", [128, 8], F32)
    p_bvbc = A.alloc("p_bvbc", [128, FW], F32)
    p_fbbc = A.alloc("p_fbbc", [128, FW], F32)
    p_negC = A.alloc("p_negC", [128, NHC], F32)
    A.mark_persist()

    S = Sched()

    A.phase_reset()
    t_tcd = A.alloc("t_tcd", [128, 8 * 256], F32)
    t_i32t = A.alloc("t_i32t", [128, 32], F32)
    t_pn = A.alloc("t_pn", [16, 512], F32)
    t_gateT = A.alloc("t_gateT", [16, 4], F32)
    t_bwrow = A.alloc("t_bwrow", [1, 12], F32)
    t_bvrow = A.alloc("t_bvrow", [1, FW], F32)
    t_fbrow = A.alloc("t_fbrow", [1, FW], F32)
    t_bwbc = A.alloc("t_bwbc", [128, 12], F32)
    t_gc = A.alloc("t_gc", [128, 16], F32)
    t_ttf = A.alloc("t_ttf", [128, 8 * 256], F32)
    t_ttb = A.alloc("t_ttb", [128, 8 * 256], BF16)
    t_ssb = A.alloc("t_ssb", [128, 16 * 32], BF16)
    t_lA = A.alloc("t_lA", [128, 4 * D], BF16)
    t_lB = A.alloc("t_lB", [128, 4 * FW], BF16)
    t_lAsT = A.alloc("t_lAsT", [128, 32 * 32], BF16)
    t_wqs = [A.alloc("t_wqs%d" % i, [128, 2 * D], BF16) for i in range(2)]
    t_wqeh = [A.alloc("t_wqeh%d" % i, [128, NCT * 256], BF16) for i in range(2)]
    t_aaug = A.alloc("t_aaug", [128, NCT * 64], BF16)
    t_baug = A.alloc("t_baug", [64, FW], BF16)

    for src, dst in ((i128, p_i128), (ones128, p_ones128), (lAnr, t_lA),
                     (lBnr, t_lB), (lAsT, t_lAsT)):
        S.dma("gp", "g_once", lambda e, s=src, d=dst: e.dma_start(out=d[:], in_=s[:]))
    gload = S.dma("gp", "g_once",
                  lambda e: e.dma_start(out=t_baug[32:64, :], in_=lBsT[:]))
    wv_pc = S.dma("gp", "g_pc0", lambda e: e.dma_start(out=s_wvb[:], in_=wvT[:]))
    kern_pc = S.dma("gp", "g_pc1", lambda e: e.dma_start(out=s_kernb[:], in_=kern[:]))
    for src, dst in ((tcd, t_tcd), (i32t, t_i32t), (pn, t_pn), (gateT, t_gateT),
                     (bwrow, t_bwrow), (onesf, p_onesf), (bqc, p_bqc),
                     (bkc, p_bkc), (bvrow, t_bvrow)):
        S.dma("sy", "s_once", lambda e, s=src, d=dst: e.dma_start(out=d[:], in_=s[:]))
    sload = S.dma("sy", "s_once",
                  lambda e: e.dma_start(out=t_fbrow[:], in_=fbrow[:]))

    bw_mm = S.op("te", lambda e: e.matmul(PB[0][:, 0:12], p_onesf[:], t_bwrow[:],
                                          start=True, stop=True), deps=[sload])
    bv_mm = [S.op("te", lambda e, c0=c0: e.matmul(
        PA[0][:, c0:c0 + 512], p_onesf[:], t_bvrow[:, c0:c0 + 512],
        start=True, stop=True), deps=[sload]) for c0 in (0, 512)]
    fb_mm = [S.op("te", lambda e, c0=c0: e.matmul(
        PA[1][:, c0:c0 + 512], p_onesf[:], t_fbrow[:, c0:c0 + 512],
        start=True, stop=True), deps=[sload]) for c0 in (0, 512)]
    gc_mm = []
    for g4 in range(4):
        for tt in range(4):
            c = g4 * 4 + tt
            gc_mm.append(S.op("te", lambda e, c=c, g4=g4, tt=tt: e.matmul(
                PB[1][:, c:c + 1], t_pn[:, tt * 128:(tt + 1) * 128],
                t_gateT[:, g4:g4 + 1], start=True, stop=True), deps=[sload]))
    ev_bw = S.op("ve", lambda e: e.tensor_copy(t_bwbc[:], PB[0][:, 0:12]), deps=[bw_mm])
    ev_bv = S.op("ve", lambda e: e.tensor_copy(p_bvbc[:], PA[0][:, 0:1024]), deps=bv_mm)
    ev_fb = S.op("ve", lambda e: e.tensor_copy(p_fbbc[:], PA[1][:, 0:1024]), deps=fb_mm)
    ev_gc = S.op("ve", lambda e: e.tensor_copy(t_gc[:], PB[1][:, 0:16]), deps=gc_mm)

    s_tiles = []
    for c in range(16):
        s_tiles.append(S.op("ve", lambda e, c=c: e.tensor_scalar(
            t_ssb[:, c * 32:(c + 1) * 32], t_i32t[:], t_gc[:, c:c + 1], None,
            ALU.mult), deps=[ev_gc, sload]))

    tt_done = []
    for h in range(NHC):
        for st in range(2):
            o = (h * 2 + st) * 256
            tds = t_tcd[:, st * 256:(st + 1) * 256]
            tc0 = t_tcd[:, (2 + st) * 256:(3 + st) * 256]
            tc1 = t_tcd[:, (4 + st) * 256:(5 + st) * 256]
            tc2 = t_tcd[:, (6 + st) * 256:(7 + st) * 256]
            d1 = S.op("ve", lambda e, o=o, h=h, tc0=tc0, tds=tds: e.scalar_tensor_tensor(
                t_ttf[:, o:o + 256], tc0, t_bwbc[:, h * 3:h * 3 + 1], tds,
                ALU.mult, ALU.add), deps=[sload, ev_bw], hazard=True)
            d2 = S.op("ve", lambda e, o=o, h=h, tc1=tc1: e.scalar_tensor_tensor(
                t_ttf[:, o:o + 256], tc1, t_bwbc[:, h * 3 + 1:h * 3 + 2],
                t_ttf[:, o:o + 256], ALU.mult, ALU.add), deps=[d1], hazard=True)
            d3 = S.op("ve", lambda e, o=o, h=h, tc2=tc2: e.scalar_tensor_tensor(
                t_ttf[:, o:o + 256], tc2, t_bwbc[:, h * 3 + 2:h * 3 + 3],
                t_ttf[:, o:o + 256], ALU.mult, ALU.add), deps=[d2], hazard=True)
            tt_done.append(d3)
    tt_cast = S.op("ve", lambda e: e.tensor_copy(t_ttb[:], t_ttf[:]), deps=tt_done,
                   hazard=True)

    bqe_mm = []
    for h in range(NHC):
        for rs in range(2):
            col = h * 2 + rs
            mm = None
            for st in range(2):
                mm = S.op("te", lambda e, h=h, rs=rs, st=st, col=col: e.matmul(
                    PB[2][:, col:col + 1],
                    t_ttf[:, (h * 2 + st) * 256 + rs * 128:
                          (h * 2 + st) * 256 + rs * 128 + 128],
                    p_bqc[:, h * 2 + st:h * 2 + st + 1],
                    start=(st == 0), stop=(st == 1)),
                    deps=(tt_done + [sload] if st == 0 else [mm]))
            bqe_mm.append(mm)
    ev_bqe = S.op("ve", lambda e: e.tensor_copy(p_bqe[:], PB[2][:, 0:8]), deps=bqe_mm)

    ab_last = None
    st_a_prev = None
    for g4 in range(4):
        ae_ev = {}
        for isl in range(32):
            mm = None
            for tt in range(4):
                first = (tt == 0)
                dpm = [gload, s_tiles[g4 * 4 + tt]]
                if first and isl >= 8:
                    dpm.append(ae_ev[isl - 8])
                mm = S.op("te", lambda e, g4=g4, isl=isl, tt=tt: e.matmul(
                    PB[3][:, (isl % 8) * 32:(isl % 8) * 32 + 32],
                    t_lA[:, tt * D + isl * 128:tt * D + (isl + 1) * 128],
                    t_ssb[:, (g4 * 4 + tt) * 32:(g4 * 4 + tt + 1) * 32],
                    start=(tt == 0), stop=(tt == 3)),
                    deps=(dpm if first else [mm]))
            if isl % 8 == 7:
                ev = S.op("ve", lambda e, isl=isl: e.tensor_copy(
                    t_aaug[:].rearrange("p (i r) -> p i r", i=NCT)
                    [:, isl - 7:isl + 1, 0:32],
                    PB[3][:, 0:256].rearrange("p (i r) -> p i r", i=8)),
                    deps=[mm] + ([st_a_prev] if isl == 7 else []))
                for k in range(isl - 7, isl + 1):
                    ae_ev[k] = ev
        sh_cp = S.op("ve", lambda e: e.tensor_copy(
            t_aaug[:].rearrange("p (i r) -> p i r", i=NCT)[:, :, 32:64],
            t_lAsT[:].rearrange("p (i r) -> p i r", i=NCT)),
            deps=[gload] + ([ab_last] if ab_last else []))
        st_a = S.dma("gp", "g_ab", lambda e, g4=g4: e.dma_start(
            out=s_aaug[g4], in_=t_aaug[:]), deps=[ae_ev[31], sh_cp])
        st_a_prev = st_a
        be_ev = []
        for oc in range(2):
            mm = None
            for tt in range(4):
                first = (tt == 0)
                dpm = [gload, s_tiles[g4 * 4 + tt], ev_bqe]
                if first and be_ev:
                    dpm.append(be_ev[-1])
                mm = S.op("te", lambda e, g4=g4, oc=oc, tt=tt: e.matmul(
                    PB[2][0:32, 0:512],
                    t_ssb[:, (g4 * 4 + tt) * 32:(g4 * 4 + tt + 1) * 32],
                    t_lB[:, tt * FW + oc * 512:tt * FW + oc * 512 + 512],
                    start=(tt == 0), stop=(tt == 3)),
                    deps=(dpm if first else [mm]))
            be_ev.append(S.op("ve", lambda e, oc=oc: e.tensor_copy(
                t_baug[0:32, oc * 512:oc * 512 + 512], PB[2][0:32, 0:512]),
                deps=[mm]))
        ab_last = S.dma("gp", "g_ab", lambda e, g4=g4: e.dma_start(
            out=s_baug[g4], in_=t_baug[:]), deps=be_ev + [gload, st_a])

    wq_ld = []
    for h in range(NHC):
        wq_ld.append(S.dma("gp", "g_wq%d" % (h % 2), lambda e, h=h: e.dma_start(
            out=t_wqs[h % 2][:], in_=wq[:, h * 2 * D:(h + 1) * 2 * D])))
    fold_done = []
    prev_evs_last = None
    for h in range(NHC):
        evs = []
        for ct in range(NCT):
            slot = ct % 4
            pa = PA[0] if slot < 2 else PA[1]
            lo = (slot % 2) * 512
            mm = None
            for st in range(2):
                first = (st == 0)
                dpm = [wq_ld[h], tt_cast]
                if first and ct < 4:
                    dpm += [ev_bv, ev_fb, prev_evs_last]
                if first and ct >= 4:
                    dpm.append(evs[ct - 4])
                if first and ct == 0 and h >= 2:
                    dpm.append(fold_done[h - 2])
                mm = S.op("te", lambda e, h=h, ct=ct, st=st, pa=pa, lo=lo: e.matmul(
                    pa[:, lo:lo + 256],
                    t_wqs[h % 2][:, st * D + ct * 128:st * D + (ct + 1) * 128],
                    t_ttb[:, (h * 2 + st) * 256:(h * 2 + st + 1) * 256],
                    start=(st == 0), stop=(st == 1)),
                    deps=(dpm if first else [mm]))
            evs.append(S.op("ve", lambda e, h=h, ct=ct, pa=pa, lo=lo: e.tensor_copy(
                t_wqeh[h % 2][:, ct * 256:(ct + 1) * 256], pa[:, lo:lo + 256]),
                deps=[mm]))
        stw = S.dma("sy", "s_wqe%d" % (h % 2), lambda e, h=h: e.dma_start(
            out=s_wqe[:, h * NCT * 256:(h + 1) * NCT * 256], in_=t_wqeh[h % 2][:]),
            deps=[evs[-1]])
        fold_done.append(stw)
        prev_evs_last = evs[-1]
        if h + 2 < NHC:
            wq_ld[h + 2].deps.append(evs[-1])

    p0_done = [ab_last, fold_done[-1], fold_done[-2], ev_bv, ev_fb, ev_bqe,
               gload, sload]

    A.phase_reset()
    t_wkT = A.alloc("t_wkT", [128, NCT * FW], BF16)
    t_xt = [A.alloc("t_xt%d" % i, [128, D], BF16) for i in range(2)]
    t_xtb = [A.alloc("t_xtb%d" % i, [128, 16 * 512], BF16) for i in range(2)]
    t_xnt = [A.alloc("t_xnt%d" % i, [128, NCT * 512], BF16) for i in range(2)]
    t_stk = [A.alloc("t_stk0", [128, 8 * 512], BF16)] * 2
    t_tmp = A.alloc("t_tmp", [128, 2 * 512], F32)
    t_bns = [A.alloc("t_bns%d" % i, [128, 48], F32) for i in range(4)]
    t_mv = [A.alloc("t_mv%d" % i, [128, 2], F32) for i in range(4)]
    t_lnv = [A.alloc("t_lnv%d" % i, [128, 1], F32) for i in range(4)]
    t_veps = [A.alloc("t_veps%d" % i, [128, 1], F32) for i in range(4)]
    t_mrc = [A.alloc("t_mrc%d" % i, [128, 8], F32) for i in range(2)]
    t_mrow = [A.alloc("t_mrow%d" % i, [1, 512], F32) for i in range(2)]
    t_rrow = [A.alloc("t_rrow%d" % i, [1, 512], F32) for i in range(2)]

    wk_ld = S.dma("gp", "g_wk", lambda e: e.dma_start(out=t_wkT[:], in_=wkT[:]),
                  deps=p0_done)
    xbTv = xbT[:].rearrange("p (c l) -> p c l", c=NCT)

    xload = [None] * NLT
    bn_last = [None] * NLT
    stat_done = [None] * NLT
    stats_ring = {}
    kmm_last = {}
    kevac = {}
    kdma = {}
    xnt_st = {}
    tt_last = {}
    tth_last = {}
    bc_ops = {}
    ev_mr = {}

    def p1_stats(lt):
        lbpar = (lt // 4) % 2
        par4 = lt % 4
        slot = lt % 2
        t4 = lt % 4
        dpx = list(p0_done) if lt < 2 else [bn_last[lt - 2]]
        xload[lt] = S.dma("gp", "g_x%d" % slot, lambda e, lt=lt, slot=slot: e.dma_start(
            out=t_xt[slot][:], in_=xb[lt * 128:(lt + 1) * 128, :]), deps=dpx)
        pv = stats_ring.get(lt - 4, {})
        bns = []
        for j2 in range(8):
            bns.append(S.op("ve", lambda e, slot=slot, j2=j2, par4=par4: e.bn_stats(
                t_bns[par4][:, j2 * 6:(j2 + 1) * 6],
                t_xt[slot][:, j2 * 512:(j2 + 1) * 512]),
                deps=[xload[lt]] + ([pv.get("aggr")] if j2 == 0 else []),
                hazard=True))
        bn_last[lt] = bns[-1]
        aggr = S.op("ve", lambda e, par4=par4: e.bn_aggr(
            t_mv[par4][:, 0:2], t_bns[par4][:]),
            deps=bns + [pv.get("veps"), pv.get("negm")], hazard=True)
        veps = S.op("ve", lambda e, par4=par4: e.tensor_scalar_add(
            t_veps[par4][:], t_mv[par4][:, 1:2], EPS),
            deps=[aggr, pv.get("lnv")], hazard=True)
        lnv = S.op("ac", lambda e, par4=par4: e.activation(
            t_lnv[par4][:], t_veps[par4][:], AF.Ln),
            deps=[veps, pv.get("rstd")], hazard=True)
        wdep = [lnv]
        if lt >= 8:
            wdep.append(bc_ops[lt // 4 - 2])
        rstd = S.op("ac", lambda e, par4=par4, lbpar=lbpar, t4=t4: e.activation(
            t_mrc[lbpar][:, 4 + t4:5 + t4], t_lnv[par4][:], AF.Exp, scale=-0.5),
            deps=wdep, hazard=True)
        negm = S.op("ve", lambda e, par4=par4, lbpar=lbpar, t4=t4: e.tensor_scalar_mul(
            t_mrc[lbpar][:, t4:t4 + 1], t_mv[par4][:, 0:1], -1.0),
            deps=[aggr] + ([bc_ops[lt // 4 - 2]] if lt >= 8 else []), hazard=True)
        stat_done[lt] = (rstd, negm)
        stats_ring[lt] = {"aggr": aggr, "veps": veps, "lnv": lnv,
                          "rstd": rstd, "negm": negm}

    def p1_mr_tr(lblk):
        lbpar = lblk % 2
        deps = [op for lt in range(lblk * 4, lblk * 4 + 4)
                for op in stat_done[lt]]
        if lblk >= 1:
            deps.append(ev_mr[lblk - 1])
        trs = []
        for j2 in range(8):
            pbx = PB[2 + j2 // 4]
            trs.append(S.op("te", lambda e, j2=j2, lbpar=lbpar, pbx=pbx: e.transpose(
                pbx[0:1, (j2 % 4) * 128:(j2 % 4) * 128 + 128],
                t_mrc[lbpar][:, j2:j2 + 1], p_i128[:]),
                deps=(deps if j2 == 0 else [])))
        edep = [trs[-1]]
        if lblk >= 2:
            edep.append(bc_ops[lblk - 2])
        e1 = S.op("ve", lambda e, lbpar=lbpar: e.tensor_copy(
            t_mrow[lbpar][:], PB[2][0:1, 0:512]), deps=edep)
        e2 = S.op("ve", lambda e, lbpar=lbpar: e.tensor_copy(
            t_rrow[lbpar][:], PB[3][0:1, 0:512]), deps=[trs[-1], e1])
        ev_mr[lblk] = e2

    def p1_mr_bc(lblk):
        lbpar = lblk % 2
        bdep = [ev_mr[lblk]]
        if lblk >= 2:
            bdep.append(tt_last[lblk - 2])
        b1 = S.op("te", lambda e, lbpar=lbpar: e.matmul(
            PA[lbpar][:, 0:512], p_onesf[:], t_mrow[lbpar][:],
            start=True, stop=True), deps=bdep)
        b2 = S.op("te", lambda e, lbpar=lbpar: e.matmul(
            PA[lbpar][:, 512:1024], p_onesf[:], t_rrow[lbpar][:],
            start=True, stop=True), deps=[b1])
        bc_ops[lblk] = b2

    def p1_norm(lblk):
        lbpar = lblk % 2
        prev = tt_last.get(lblk - 1)
        for half in range(2):
            hdep = [tth_last[(lblk - 1, half)]] if lblk >= 1 else []
            xl = S.dma("gp", "g_xt%d" % half, lambda e, lblk=lblk, half=half: e.dma_start(
                out=t_xtb[half][:],
                in_=xbTv[:, half * 16:(half + 1) * 16,
                         lblk * 512:(lblk + 1) * 512]),
                deps=hdep)
            for q in range(8):
                o1 = None
                for k in range(2):
                    c16 = q * 2 + k
                    d1 = [xl, bc_ops[lblk]]
                    if lblk >= 2 and half == 0 and c16 == 0:
                        d1 += [kmm_last[(lblk - 2, 7)], xnt_st[lblk - 2]]
                    if k == 0 and prev is not None:
                        d1.append(prev)
                    o1 = S.op("ve", lambda e, c16=c16, k=k, lbpar=lbpar, half=half: e.tensor_tensor(
                        t_tmp[:, k * 512:(k + 1) * 512],
                        t_xtb[half][:, c16 * 512:(c16 + 1) * 512],
                        PA[lbpar][:, 0:512], ALU.add),
                        deps=d1, hazard=(k == 1))
                for k in range(2):
                    ct = half * 16 + q * 2 + k
                    prev = S.op("ve", lambda e, ct=ct, k=k, lbpar=lbpar: e.tensor_tensor(
                        t_xnt[lbpar][:, ct * 512:(ct + 1) * 512],
                        t_tmp[:, k * 512:(k + 1) * 512],
                        PA[lbpar][:, 512:1024], ALU.mult),
                        deps=[o1], hazard=(k == 1))
            tth_last[(lblk, half)] = prev
        tt_last[lblk] = prev
        xnt_st[lblk] = S.dma("sy", "s_xnt%d" % lbpar,
                             lambda e, lblk=lblk, lbpar=lbpar: e.dma_start(
            out=s_xnt[lblk], in_=t_xnt[lbpar][:]), deps=[prev])

    def p1_kt(lblk):
        bpar = lblk % 2
        last_tt = tt_last[lblk]
        for ft in range(8):
            pb = PB[ft % 2]
            mm = None
            for ct in range(NCT):
                first = (ct == 0)
                dpm = [last_tt, wk_ld]
                if first and ft >= 2:
                    dpm.append(kevac[(lblk, ft - 2)])
                if first and ft < 2 and lblk >= 1:
                    dpm.append(kevac[(lblk - 1, 6 + ft)])
                mm = S.op("te", lambda e, ft=ft, ct=ct, pb=pb, bpar=bpar: e.matmul(
                    pb[:, 0:512],
                    t_wkT[:, ct * FW + ft * 128:ct * FW + (ft + 1) * 128],
                    t_xnt[bpar][:, ct * 512:(ct + 1) * 512],
                    start=(ct == 0), stop=(ct == NCT - 1)),
                    deps=(dpm if first else [mm]))
            kmm_last[(lblk, ft)] = mm
            edp = [mm]
            if lblk >= 1:
                edp.append(kdma[(lblk - 1, 7)])
            kevac[(lblk, ft)] = S.op("ac", lambda e, ft=ft, pb=pb, bpar=bpar: e.activation(
                t_stk[bpar][:, ft * 512:(ft + 1) * 512], pb[:, 0:512],
                AF.Identity, bias=p_bkc[:, ft:ft + 1]), deps=edp)
            kdma[(lblk, ft)] = S.dma("sy", "s_kd%d" % bpar,
                                     lambda e, lblk=lblk, ft=ft, bpar=bpar: e.dma_start(
                out=s_kt[ft // 2, ft % 2, :, lblk * 512:(lblk + 1) * 512],
                in_=t_stk[bpar][:, ft * 512:(ft + 1) * 512]),
                deps=[kevac[(lblk, ft)]])

    for lt in range(min(8, NLT)):
        p1_stats(lt)
    p1_mr_tr(0)
    p1_mr_bc(0)
    for lblk in range(NLB):
        if lblk + 1 < NLB:
            p1_mr_tr(lblk + 1)
        for lt in range(4 * (lblk + 2), min(4 * (lblk + 3), NLT)):
            p1_stats(lt)
        if lblk >= 1:
            p1_kt(lblk - 1)
        if lblk + 1 < NLB:
            p1_mr_bc(lblk + 1)
        p1_norm(lblk)
    p1_kt(NLB - 1)
    p1_tail = [kdma[(NLB - 1, 7)], kdma[(NLB - 2, 7)],
               xnt_st[NLB - 1], xnt_st[NLB - 2]]

    A.phase_reset()
    t_wvT = A.alloc("t_wvT", [128, NCT * FW], BF16)
    t_xv = [A.alloc("t_xv%d" % i, [128, NCT * 512], BF16) for i in range(2)]
    t_stv = [A.alloc("t_stv%d" % i, [128, FW], BF16) for i in range(2)]

    lev = (NLB - 1) if (NLB - 1) % 2 == 0 else (NLB - 2)
    lod = (NLB - 1) if (NLB - 1) % 2 == 1 else (NLB - 2)
    p1_guards = [
        (A.ranges[id(t_wkT)], [kmm_last[(NLB - 1, 7)]]),
        (A.ranges[id(t_xt[0])], [bn_last[NLT - 1], bn_last[NLT - 2]]),
        (A.ranges[id(t_xt[1])], [bn_last[NLT - 1], bn_last[NLT - 2]]),
        (A.ranges[id(t_xtb[0])], [tt_last[NLB - 1]]),
        (A.ranges[id(t_xtb[1])], [tt_last[NLB - 1]]),
        (A.ranges[id(t_xnt[0])], [kmm_last[(lev, 7)], xnt_st[lev]]),
        (A.ranges[id(t_xnt[1])], [kmm_last[(lod, 7)], xnt_st[lod]]),
        (A.ranges[id(t_stk[0])], [kdma[(NLB - 1, 7)], kdma[(NLB - 2, 7)]]),
        (A.ranges[id(t_tmp)], [tt_last[NLB - 1]]),
        (A.ranges[id(t_bns[0])], [tt_last[NLB - 1], stat_done[NLT - 1][0],
                                  stat_done[NLT - 1][1]]),
    ]

    def p1_guard(t):
        lo, hi = A.ranges[id(t)]
        deps = []
        for (glo, ghi), ds2 in p1_guards:
            if glo < hi and lo < ghi:
                deps += ds2
        if hi > A.ranges[id(t_bns[0])][0]:
            deps += [tt_last[NLB - 1], stat_done[NLT - 1][0], stat_done[NLT - 1][1],
                     ev_mr[NLB - 1], bc_ops[NLB - 1]]
        return deps

    P3_HEAD = 2 * (NLT * 257 * 2 + 31 & ~31) + 4 * L * 2 + 160 + 32 + 32 + \
        2 * 2048
    _sv = A.off
    A.off = A.persist_end + P3_HEAD
    t_kt = A.alloc("t_kt", [128, 2 * L], BF16)
    t_qt = A.alloc("t_qt", [128, 2 * L], BF16)
    p3_kt_end = A.off
    A.off = _sv
    kt_done = {}
    qt_done = {}
    LH = L // 2
    for ch in range(2):
        dpk = p1_guard(t_kt) + [kdma[(NLB - 1, 7)], kdma[(NLB - 2, 7)]] \
            if ch == 0 else []
        kt_done[(0, ch)] = [S.dma("ac", "a_k3",
                                  lambda e, rs=rs, ch=ch: e.dma_start(
            out=t_kt[:, rs * L + ch * LH:rs * L + (ch + 1) * LH],
            in_=s_kt[0, rs, :, ch * LH:(ch + 1) * LH]),
            deps=(dpk if rs == 0 else []))
            for rs in range(2)]

    wv_ld = S.dma("gp", "g_wv", lambda e: e.dma_start(out=t_wvT[:], in_=s_wvb[:]),
                  deps=p1_guard(t_wvT) + [wv_pc])
    xv_ld = {}
    vmm_last = {}
    vevac = {}
    vdma = {}
    for lblk in range(NLB):
        par = lblk % 2
        dp = (p1_guard(t_xv[par]) if lblk < 2 else [vmm_last[(lblk - 2, 3)]]) + \
            [xnt_st[lblk]]
        xv_ld[lblk] = S.dma("sy", "s_xv%d" % par, lambda e, lblk=lblk, par=par: e.dma_start(
            out=t_xv[par][:], in_=s_xnt[lblk]), deps=dp)
        for mt in range(4):
            pa = PA[mt % 2]
            mm = None
            for oc in range(2):
                for ct in range(NCT):
                    first = (oc == 0 and ct == 0)
                    dpm = [xv_ld[lblk], wv_ld]
                    if first:
                        key = (lblk, mt - 2) if mt >= 2 else (lblk - 1, mt + 2)
                        dpm.append(vevac.get(key))
                    mm = S.op("te", lambda e, mt=mt, oc=oc, ct=ct, pa=pa, par=par: e.matmul(
                        pa[:, oc * 512:(oc + 1) * 512],
                        t_xv[par][:, ct * 512 + mt * 128:ct * 512 + (mt + 1) * 128],
                        t_wvT[:, ct * FW + oc * 512:ct * FW + (oc + 1) * 512],
                        start=(ct == 0), stop=(ct == NCT - 1)),
                        deps=(dpm if first else [mm]))
            vmm_last[(lblk, mt)] = mm
            edp = [mm, ev_bv]
            key = (lblk, mt - 2) if mt >= 2 else (lblk - 1, mt + 2)
            if key in vdma:
                edp.append(vdma[key][-1])
            elif lblk == 0:
                edp += p1_guard(t_stv[mt % 2])
            vevac[(lblk, mt)] = S.op("ve", lambda e, mt=mt, pa=pa: e.tensor_tensor(
                t_stv[mt % 2][:], pa[:, 0:1024], p_bvbc[:], ALU.add), deps=edp)
            vdma[(lblk, mt)] = [S.dma("gp", "g_vd%d" % (mt % 2),
                                      lambda e, lblk=lblk, mt=mt, h=h: e.dma_start(
                out=s_vv[h, lblk * 4 + mt],
                in_=t_stv[mt % 2][:, h * 256:(h + 1) * 256]),
                deps=[vevac[(lblk, mt)]]) for h in range(NHC)]
    p2v_tail = [vdma[(NLB - 1, 2)][-1], vdma[(NLB - 1, 3)][-1]]

    A.phase_reset()
    t_wqe = A.alloc("t_wqe", [128, NHC * NCT * 256], BF16)
    t_xq = [A.alloc("t_xq%d" % i, [128, NCT * 512], BF16) for i in range(2)]
    t_stq = [A.alloc("t_stq%d" % i, [128, 512], BF16) for i in range(2)]

    def p2v_guard(t):
        lo, hi = A.ranges[id(t)]
        deps = []
        for tt2, ds2 in ((t_wvT, [vmm_last[(NLB - 1, 3)]]),
                         (t_xv[0], [vmm_last[(NLB - 1, 3)]]),
                         (t_xv[1], [vmm_last[(NLB - 1, 3)]]),
                         (t_stv[0], [vdma[(NLB - 1, 2)][-1]]),
                         (t_stv[1], [vdma[(NLB - 1, 3)][-1]])):
            glo, ghi = A.ranges[id(tt2)]
            if glo < hi and lo < ghi:
                deps += ds2
        if hi > A.ranges[id(t_stv[1])][1]:
            deps += p2v_tail
        return deps

    wqe_ld = S.dma("sy", "s_wqeld", lambda e: e.dma_start(out=t_wqe[:], in_=s_wqe[:]),
                   deps=p2v_guard(t_wqe) + [fold_done[-1], fold_done[-2]])
    xq_ld = {}
    qmm_last = {}
    qevac = {}
    qdma = {}
    for lblk in range(NLB):
        par = lblk % 2
        dp = (p2v_guard(t_xq[par]) if lblk < 2 else [qmm_last[(lblk - 2, 7)]]) + \
            [xnt_st[lblk]]
        xq_ld[lblk] = S.dma("sy", "s_xq%d" % par, lambda e, lblk=lblk, par=par: e.dma_start(
            out=t_xq[par][:], in_=s_xnt[lblk]), deps=dp)
        for rt in range(8):
            h, rs = rt // 2, rt % 2
            pb = PB[rt % 2]
            mm = None
            for ct in range(NCT):
                first = (ct == 0)
                dpm = [xq_ld[lblk], wqe_ld]
                if first:
                    key = (lblk, rt - 2) if rt >= 2 else (lblk - 1, rt + 6)
                    dpm.append(qevac.get(key))
                mm = S.op("te", lambda e, h=h, rs=rs, ct=ct, pb=pb, par=par: e.matmul(
                    pb[:, 0:512],
                    t_wqe[:, (h * NCT + ct) * 256 + rs * 128:
                          (h * NCT + ct) * 256 + (rs + 1) * 128],
                    t_xq[par][:, ct * 512:(ct + 1) * 512],
                    start=(ct == 0), stop=(ct == NCT - 1)),
                    deps=(dpm if first else [mm]))
            qmm_last[(lblk, rt)] = mm
            edp = [mm, ev_bqe]
            key = (lblk, rt - 2) if rt >= 2 else (lblk - 1, rt + 6)
            edp.append(qdma.get(key))
            if key not in qdma and lblk == 0:
                edp += p2v_guard(t_stq[rt % 2])
            qevac[(lblk, rt)] = S.op("ac", lambda e, rt=rt, pb=pb: e.activation(
                t_stq[rt % 2][:], pb[:, 0:512], AF.Identity,
                bias=p_bqe[:, rt:rt + 1]), deps=edp)
            qdma[(lblk, rt)] = S.dma("gp", "g_qd%d" % (rt % 2),
                                     lambda e, lblk=lblk, rt=rt: e.dma_start(
                out=s_qt[rt // 2, rt % 2, :, lblk * 512:(lblk + 1) * 512],
                in_=t_stq[rt % 2][:]), deps=[qevac[(lblk, rt)]])
        dq0 = [qdma[(lblk, 0)], qdma[(lblk, 1)]]
        if lblk == 0:
            dq0 = dq0 + p1_guard(t_qt)
        qt_done[(0, lblk)] = [
            S.dma("gp", "g_q3", lambda e, lblk=lblk: e.dma_start(
                out=t_qt[:, lblk * 512:(lblk + 1) * 512],
                in_=s_qt[0, 0, :, lblk * 512:(lblk + 1) * 512]), deps=dq0),
            S.dma("gp", "g_q3", lambda e, lblk=lblk: e.dma_start(
                out=t_qt[:, L + lblk * 512:L + (lblk + 1) * 512],
                in_=s_qt[0, 1, :, lblk * 512:(lblk + 1) * 512]), deps=dq0)]
    p2_tail = [qdma[(NLB - 1, 6)], qdma[(NLB - 1, 7)]]

    A.phase_reset()
    t_vs2 = [A.alloc("t_vs%d" % i, [128, NLT * 257], BF16) for i in range(2)]
    t_sqpe = A.alloc("t_sqpe", [128, 4 * L], BF16)
    t_sq = t_sqpe
    t_mx = A.alloc("t_mx", [128, 40], F32)
    t_sums = A.alloc("t_sums", [128, 4], F32)
    t_rec = A.alloc("t_rec", [128, 4], F32)
    t_sta = [A.alloc("t_sta%d" % i, [128, NHC * 256], BF16) for i in range(2)]
    q_ext = A.ranges[id(t_stq[1])][1]
    v_ext = A.ranges[id(t_stv[1])][1]
    assert A.off == A.ranges[id(t_kt)][0] >= max(q_ext, v_ext), (
        A.off, A.ranges[id(t_kt)], q_ext, v_ext)
    A.off = p3_kt_end
    _sv = A.off
    A.off = A.persist_end
    t_kern = A.alloc("t_kern", [128, NCT * FW], BF16)
    A.off = _sv
    KSPL = (NLT * 257 * 2 + 31 & ~31) // 2
    kern_ldA = None

    q_last = qmm_last[(NLB - 1, 7)]
    vs_memset = [S.op("ve", lambda e, i=i: e.memset(
        t_vs2[i][:].rearrange("p (m f) -> p m f", m=NLT)[:, :, 256:257], 1.0),
        deps=[q_last]) for i in range(2)]

    exp_ops = []
    attn_ev = {}
    attn_dma = {}
    head_explast = {}
    head_avlast = {}
    avdone = {}
    avof = {}
    expof = {}
    explast = {}
    pend = [None]

    CG = NG // 2
    nch2 = 2 * NQB

    def pexp_ap(gK, gc, sl=None):
        base = (gK % 2) * 2 * L + gc * 1024
        if sl is None:
            return t_sqpe[:, base:base + 1024]
        return t_sqpe[:, base + sl[0]:base + sl[1]]

    def emit_denoms(hp, lb, avl):
        cps = []
        for ls in range(4):
            cps.append(S.op("ve", lambda e, ls=ls: e.tensor_copy(
                t_sums[:, ls:ls + 1], PB[ls][:, 256:257]), deps=[avl]))
        rec = S.op("ve", lambda e: e.reciprocal(t_rec[:], t_sums[:]),
                   deps=cps, hazard=True)
        spar = lb % 2
        evs = []
        for ls in range(4):
            edp2 = [rec, avl]
            wkey = (hp, lb - 2) if lb >= 2 else (hp - 1, NQB - 2 + lb)
            if wkey in attn_dma:
                edp2.append(attn_dma[wkey][-1])
            evs.append(S.op("ve", lambda e, ls=ls, spar=spar: e.tensor_scalar_mul(
                t_sta[spar][:, ls * 256:(ls + 1) * 256],
                PB[ls][:, 0:256], t_rec[:, ls:ls + 1]),
                deps=edp2))
        attn_ev[(hp, lb)] = evs
        attn_dma[(hp, lb)] = [S.dma("gp", "g_att%d" % spar,
                                    lambda e, hp=hp, lb=lb, ls=ls, spar=spar: e.dma_start(
            out=s_att[lb * 4 + ls, :, hp * 256:(hp + 1) * 256],
            in_=t_sta[spar][:, ls * 256:(ls + 1) * 256]),
            deps=[evs[ls]]) for ls in range(4)]

    def emit_av_group(gc, lds_cur):
        hp, Xp, gKp, t_vsp = pend[0]
        lbp, cp_ = Xp // 2, Xp % 2
        gp = cp_ * CG + gc
        avdp0 = [expof[(gKp, gc)], lds_cur]
        if gc == 0 and cp_ == 0:
            prev_ev = (attn_ev[(hp, lbp - 1)] if lbp >= 1 else
                       (attn_ev[(hp - 1, NQB - 1)] if hp >= 1 else []))
            avdp0 += prev_ev
        avmm = None
        for kq in range(2):
            kt = 2 * gp + kq
            for ls in range(4):
                first = (kq == 0 and ls == 0)
                avmm = S.op("te", lambda e, gKp=gKp, gc=gc, kq=kq, ls=ls, kt=kt, gp=gp, cp_=cp_, t_vsp=t_vsp: e.matmul(
                    PB[ls][:, 0:257],
                    pexp_ap(gKp, gc, (kq * 512 + ls * 128,
                                      kq * 512 + (ls + 1) * 128)),
                    t_vsp[:, kt * 257:(kt + 1) * 257],
                    start=(cp_ == 0 and gp == 0 and kq == 0),
                    stop=(cp_ == 1 and gp == NG - 1 and kq == 1)),
                    deps=(avdp0 if first else [avmm]))
        avdone[gKp] = avmm
        avof[(gKp, gc)] = avmm
        if gc == CG - 1:
            if cp_ == 1:
                emit_denoms(hp, lbp, avmm)
            if Xp == nch2 - 1:
                head_avlast[hp] = avmm
            pend[0] = None
        return avmm

    smm_of = {}
    smm_head_last = {}
    for h in range(NHC):
        t_vs = t_vs2[h % 2]
        dpv = [vs_memset[h % 2]] + \
            ([head_avlast[h - 2]] if h >= 2 else [q_last] + p2v_tail)
        lds = None
        for mt8 in range(0, NLT, 8):
            lds = S.dma("sy", "s_h3", lambda e, h=h, mt8=mt8, t_vs=t_vs: e.dma_start(
                out=t_vs[:].rearrange("p (m f) -> p m f", m=NLT)
                [:, mt8:mt8 + 8, 0:256],
                in_=s_vv[h, mt8:mt8 + 8].rearrange("m p f -> p m f")),
                deps=(dpv if mt8 == 0 else []))
        if h >= 1:
            for ch in range(2):
                kt_done[(h, ch)] = [S.dma("ac", "a_k3",
                                          lambda e, h=h, rs=rs, ch=ch: e.dma_start(
                    out=t_kt[:, rs * L + ch * LH:rs * L + (ch + 1) * LH],
                    in_=s_kt[h, rs, :, ch * LH:(ch + 1) * LH]),
                    deps=([smm_head_last[h - 1]]
                          if ch == 0 and rs == 0 else []))
                    for rs in range(2)]
            for lb in range(NQB):
                dq0 = [smm_of[(h - 1, 2 * lb + 1)]]
                qt_done[(h, lb)] = [
                    S.dma("sy", "s_q3", lambda e, h=h, lb=lb: e.dma_start(
                        out=t_qt[:, lb * 512:(lb + 1) * 512],
                        in_=s_qt[h, 0, :, lb * 512:(lb + 1) * 512]), deps=dq0),
                    S.dma("gp", "g_q3", lambda e, h=h, lb=lb: e.dma_start(
                        out=t_qt[:, L + lb * 512:L + (lb + 1) * 512],
                        in_=s_qt[h, 1, :, lb * 512:(lb + 1) * 512]), deps=dq0)]
        ktq_all = (kt_done[(h, 0)] + kt_done[(h, 1)] +
                   qt_done[(h, NQB - 1)])

        gK0 = h * nch2
        fms = []
        norm_red_last = {}
        sq_mm_last = None
        for side, (src2, base) in enumerate(((t_qt, 0), (t_kt, 16))):
            muls = []
            for ct in range(2):
                dpm = list(ktq_all)
                if h == 0 and side == 0 and ct == 0:
                    dpm += [qmm_last[(NLB - 1, 7)], qmm_last[(NLB - 2, 7)]]
                    dpm += p2_tail
                if h >= 1:
                    dpm.append(avdone[gK0 - 2])
                if side == 1:
                    dpm.append(sq_mm_last)
                muls.append(S.op("ve", lambda e, src2=src2, ct=ct: e.tensor_mul(
                    t_sq[:, ct * L:(ct + 1) * L], src2[:, ct * L:(ct + 1) * L],
                    src2[:, ct * L:(ct + 1) * L]), deps=dpm))
            nch = L // 512
            pan = PA[side]
            reds = []
            for ch in range(nch):
                lo = (ch % 2) * 512
                mm = None
                for ct in range(2):
                    first = (ct == 0)
                    dpm = muls + [gload]
                    if first and ch >= 2:
                        dpm.append(reds[ch - 2])
                    if first and ch < 2 and h >= 1:
                        dpm.append(explast[gK0 - 1])
                    mm = S.op("te", lambda e, ch=ch, ct=ct, pan=pan, lo=lo: e.matmul(
                        pan[:, lo:lo + 512], p_ones128[:],
                        t_sq[:, ct * L + ch * 512:ct * L + (ch + 1) * 512],
                        start=(ct == 0), stop=(ct == 1)),
                        deps=(dpm if first else [mm]))
                sq_mm_last = mm
                reds.append(S.op("ve", lambda e, ch=ch, base=base, pan=pan, lo=lo: e.reduce_max(
                    t_mx[:, base + ch:base + ch + 1], pan[:, lo:lo + 512], axis=AX.X),
                    deps=[mm], hazard=True))
            norm_red_last[side] = reds[-1]
            fms.append(S.op("ve", lambda e, base=base, side=side, nch=nch: e.reduce_max(
                t_mx[:, 32 + side:33 + side], t_mx[:, base:base + nch], axis=AX.X),
                deps=reds, hazard=True))
        prod = S.op("ve", lambda e: e.tensor_mul(
            t_mx[:, 34:35], t_mx[:, 32:33], t_mx[:, 33:34]), deps=fms, hazard=True)
        lnc = S.op("ac", lambda e: e.activation(
            t_mx[:, 35:36], t_mx[:, 34:35], AF.Ln),
            deps=[prod] + (p2_tail if h == 0 else []), hazard=True)
        cexp = S.op("ac", lambda e: e.activation(
            t_mx[:, 36:37], t_mx[:, 35:36], AF.Exp, scale=0.5),
            deps=[lnc], hazard=True)
        negc = S.op("ve", lambda e, h=h: e.tensor_scalar(
            p_negC[:, h:h + 1], t_mx[:, 36:37], -1.0, None, ALU.mult),
            deps=[cexp], hazard=True)

        for K in range(nch2):
            lb, c = K // 2, K % 2
            gK = gK0 + K
            for gc in range(CG):
                g = c * CG + gc
                gg = len(exp_ops)
                pa = PA[gg % 2]
                if pend[0] is not None:
                    emit_av_group(gc, lds)
                smm = None
                sdeps = kt_done[(h, c)] + qt_done[(h, lb)]
                if gg >= 2:
                    sdeps.append(exp_ops[gg - 2])
                if K == 0 and gc < 2:
                    sdeps.append(norm_red_last[gg % 2])
                for kq in range(2):
                    kt = 2 * g + kq
                    for ct in range(2):
                        first = (kq == 0 and ct == 0)
                        smm = S.op("te", lambda e, lb=lb, kt=kt, ct=ct, kq=kq, pa=pa: e.matmul(
                            pa[:, kq * 512:(kq + 1) * 512],
                            t_kt[:, ct * L + kt * 128:ct * L + (kt + 1) * 128],
                            t_qt[:, ct * L + lb * 512:ct * L + (lb + 1) * 512],
                            start=(ct == 0), stop=(ct == 1)),
                            deps=(sdeps if first else [smm]))
                edp = [smm, negc]
                if (gK - 2, gc) in avof:
                    edp.append(avof[(gK - 2, gc)])
                if gc == 0 and K == 0:
                    edp.append(sq_mm_last)
                eo = S.op("ac", lambda e, h=h, gK=gK, gc=gc, pa=pa: e.activation(
                    pexp_ap(gK, gc), pa[:, 0:1024], AF.Exp,
                    bias=p_negC[:, h:h + 1]), deps=edp)
                exp_ops.append(eo)
                expof[(gK, gc)] = eo
            smm_of[(h, K)] = smm
            explast[gK] = exp_ops[-1]
            pend[0] = (h, K, gK, t_vs)
            if h == NHC - 1 and K == 0:
                kern_ldA = S.dma("gp", "g_kern", lambda e: e.dma_start(
                    out=t_kern[:, 0:KSPL], in_=s_kernb[:, 0:KSPL]),
                    deps=[head_avlast[NHC - 2], kern_pc])
        smm_head_last[h] = smm_of[(h, nch2 - 1)]
        head_explast[h] = explast[gK0 + nch2 - 1]
    for gc in range(CG):
        emit_av_group(gc, lds)
    p3_tail = [attn_dma[(NHC - 1, NQB - 1)][-1], attn_dma[(NHC - 1, NQB - 2)][-1],
               attn_ev[(NHC - 1, NQB - 1)][-1]]

    A.phase_reset()
    A.off = A.ranges[id(t_kern)][1]
    t_xf = [A.alloc("t_xf%d" % i, [128, NCT * 512], BF16) for i in range(2)]
    t_aau = A.alloc("t_aau", [128, NCT * 64], BF16)
    t_bau = A.alloc("t_bau", [64, FW], BF16)
    t_axv = [A.alloc("t_axv%d" % i, [64, 512], BF16) for i in range(2)]
    t_xr = A.alloc("t_xr", [128, 4 * FW], F32)
    t_at = A.alloc("t_at", [128, 4 * FW], F32)
    t_t1 = [A.alloc("t_t1%d" % i, [128, FW], F32) for i in range(2)]
    t_ys = [A.alloc("t_ys%d" % i, [128, FW], F32) for i in range(2)]

    p3_guards = [
        (A.ranges[id(t_kt)], head_explast[NHC - 1]),
        (A.ranges[id(t_qt)], head_explast[NHC - 1]),
        (A.ranges[id(t_vs2[0])], head_avlast.get(NHC - 2)),
        (A.ranges[id(t_vs2[1])], head_avlast[NHC - 1]),
        (A.ranges[id(t_sqpe)], head_avlast[NHC - 1]),
        (A.ranges[id(t_mx)], attn_ev[(NHC - 1, NQB - 1)][-1]),
        (A.ranges[id(t_sums)], attn_ev[(NHC - 1, NQB - 1)][-1]),
        (A.ranges[id(t_rec)], attn_ev[(NHC - 1, NQB - 1)][-1]),
        (A.ranges[id(t_sta[0])], attn_dma[(NHC - 1, NQB - 2)][-1]),
        (A.ranges[id(t_sta[1])], attn_dma[(NHC - 1, NQB - 1)][-1]),
    ]

    def p3_guard(t):
        lo, hi = A.ranges[id(t)]
        deps = []
        for (glo, ghi), dep in p3_guards:
            if glo < hi and lo < ghi and dep is not None:
                deps.append(dep)
        return deps

    kern_ld = S.dma("gp", "g_kern", lambda e: e.dma_start(
        out=t_kern[:, KSPL:NCT * FW], in_=s_kernb[:, KSPL:NCT * FW]),
        deps=p3_guard(t_kern))
    PER_G = NLB // 4
    xf_ld = {}
    aau_ld = {}
    bau_ld = {}
    at_ld = {}
    xr_ld = {}
    ffn_last = {}
    ax_mm_last = {}
    ax_ev = {}
    t1_op = {}
    y2_op = {}
    y3_op = {}
    ydma = {}
    for lblk in range(NLB):
        par = lblk % 2
        g4 = lblk // PER_G
        if lblk == 0:
            dp = p3_guard(t_xf[0])
        elif lblk == 1:
            dp = p3_guard(t_xf[1])
        else:
            dp = [ffn_last[(lblk - 2, 3)], ax_mm_last[lblk - 2]]
        xf_ld[lblk] = S.dma("sy", "s_xf%d" % par, lambda e, lblk=lblk, par=par: e.dma_start(
            out=t_xf[par][:], in_=s_xnt[lblk]), deps=dp + [xnt_st[lblk]])
        if lblk % PER_G == 0:
            adp = [ax_mm_last[lblk - 1]] if lblk >= 1 else p3_guard(t_aau)
            aau_ld[g4] = S.dma("sy", "s_aau", lambda e, g4=g4: e.dma_start(
                out=t_aau[:], in_=s_aaug[g4]), deps=adp + [ab_last])
            bdp = [ffn_last[(lblk - 1, 3)]] if lblk >= 1 else p3_guard(t_bau)
            bau_ld[g4] = S.dma("sy", "s_bau", lambda e, g4=g4: e.dma_start(
                out=t_bau[:], in_=s_baug[g4]), deps=bdp + [ab_last])
        atdp = ([y3_op[(lblk - 1, lt)] for lt in range(4)]
                if lblk >= 1 else p3_guard(t_at))
        at_ld[lblk] = S.dma("gp", "g_at", lambda e, lblk=lblk: e.dma_start(
            out=t_at[:].rearrange("p (l f) -> p l f", l=4),
            in_=s_att[lblk * 4:lblk * 4 + 4].rearrange("l p f -> p l f")),
            deps=atdp + [attn_dma[(NHC - 1, lblk)][-1]])
        xrdp = ([y2_op[(lblk - 1, lt)] for lt in range(4)]
                if lblk >= 1 else p3_guard(t_xr))
        xr_ld[lblk] = S.dma("sy", "s_xr", lambda e, lblk=lblk: e.dma_start(
            out=t_xr[:].rearrange("p (l f) -> p l f", l=4),
            in_=xres[lblk * 512:(lblk + 1) * 512, :].rearrange(
                "(l p) f -> p l f", p=128)), deps=xrdp)
        pax = PB[2 + par]
        mm = None
        for ct in range(NCT):
            first = (ct == 0)
            dpm = [xf_ld[lblk], aau_ld[g4]]
            if first and lblk < 2:
                dpm.append(attn_ev[(NHC - 1, NQB - 1)][-1])
            if first and lblk >= 2:
                dpm.append(ax_ev[lblk - 2])
            mm = S.op("te", lambda e, ct=ct, par=par, pax=pax: e.matmul(
                pax[0:64, 0:512], t_aau[:, ct * 64:(ct + 1) * 64],
                t_xf[par][:, ct * 512:(ct + 1) * 512],
                start=(ct == 0), stop=(ct == NCT - 1)),
                deps=(dpm if first else [mm]))
        ax_mm_last[lblk] = mm
        ax_ev[lblk] = S.op("ve", lambda e, par=par, pax=pax: e.tensor_copy(
            t_axv[par][:], pax[0:64, 0:512]), deps=[mm])
        for lt in range(4):
            pa = PA[lt % 2]
            mm = None
            for oc in range(2):
                for ct in range(NCT):
                    first = (oc == 0 and ct == 0)
                    dpm = [xf_ld[lblk], kern_ld]
                    if first:
                        key = (lblk, lt - 2) if lt >= 2 else (lblk - 1, lt + 2)
                        dpm.append(t1_op.get(key))
                    mm = S.op("te", lambda e, lt=lt, oc=oc, ct=ct, pa=pa, par=par: e.matmul(
                        pa[:, oc * 512:(oc + 1) * 512],
                        t_xf[par][:, ct * 512 + lt * 128:ct * 512 + (lt + 1) * 128],
                        t_kern[:, ct * FW + oc * 512:ct * FW + (oc + 1) * 512],
                        start=(ct == 0), stop=False),
                        deps=(dpm if first else [mm]))
            for oc in range(2):
                mm = S.op("te", lambda e, lt=lt, oc=oc, pa=pa, par=par: e.matmul(
                    pa[:, oc * 512:(oc + 1) * 512],
                    t_axv[par][:, lt * 128:(lt + 1) * 128],
                    t_bau[:, oc * 512:(oc + 1) * 512],
                    start=False, stop=True),
                    deps=[mm, ax_ev[lblk], bau_ld[g4]])
            ffn_last[(lblk, lt)] = mm
            key = (lblk, lt - 2) if lt >= 2 else (lblk - 1, lt + 2)
            t1_op[(lblk, lt)] = S.op("ve", lambda e, lt=lt, pa=pa: e.tensor_tensor(
                t_t1[lt % 2][:], pa[:, 0:1024], p_fbbc[:], ALU.add),
                deps=[mm, ev_fb, y2_op.get(key)])
            y2_op[(lblk, lt)] = S.op("gp", lambda e, lt=lt: e.tensor_tensor(
                t_ys[lt % 2][:], t_t1[lt % 2][:], t_xr[:, lt * FW:(lt + 1) * FW],
                ALU.add), deps=[t1_op[(lblk, lt)], xr_ld[lblk], ydma.get(key)])
            y3_op[(lblk, lt)] = S.op("ve", lambda e, lt=lt: e.tensor_tensor(
                t_ys[lt % 2][:], t_ys[lt % 2][:], t_at[:, lt * FW:(lt + 1) * FW],
                ALU.add), deps=[y2_op[(lblk, lt)], at_ld[lblk]])
            ydma[(lblk, lt)] = S.dma("gp", "g_y%d" % (lt % 2),
                                     lambda e, lblk=lblk, lt=lt: e.dma_start(
                out=y_out[lblk * 512 + lt * 128:lblk * 512 + (lt + 1) * 128, :],
                in_=t_ys[lt % 2][:]), deps=[y3_op[(lblk, lt)]])

    S.emit(nc, ent)
    return nc


def _prep_core(inputs, b, hg, L):
    F0 = hg * FW
    x = inputs["L0_attn_x"]
    Wq = inputs["L0_attn_Wq"]
    Wk = inputs["L0_attn_Wk"]
    Wv = inputs["L0_attn_Wv"]
    kernw = inputs["L0_ffn_kernel"]

    def tile_pm(a):
        W = a.shape[1]
        return np.ascontiguousarray(
            a.reshape(32, 128, W).transpose(1, 0, 2).reshape(128, 32 * W))

    m = {}
    m["xb"] = np.ascontiguousarray(x[b, :L])
    m["xbT"] = np.ascontiguousarray(
        x[b, :L].T.reshape(32, 128, L).transpose(1, 0, 2).reshape(128, 32 * L))
    m["xres"] = np.ascontiguousarray(x[b, :L, F0:F0 + FW])
    m["wkT"] = tile_pm(np.ascontiguousarray(Wk[F0:F0 + FW].T))
    m["wvT"] = tile_pm(np.ascontiguousarray(Wv[F0:F0 + FW].T))
    m["kern"] = tile_pm(np.ascontiguousarray(kernw[:, F0:F0 + FW]))
    m["wq"] = np.ascontiguousarray(
        Wq[F0:F0 + FW].reshape(8, 128, D).transpose(1, 0, 2).reshape(128, 8 * D))
    m["tcd"] = np.ascontiguousarray(
        _host_tcd().reshape(4, 2, 128, 256).transpose(2, 0, 1, 3).reshape(128, 8 * 256))
    m["i128"] = np.eye(128, dtype=np.float32)
    m["ones128"] = np.ones((128, 128), dtype=ml_dtypes.bfloat16)
    m["onesf"] = np.ones((1, 128), dtype=np.float32)
    m["i32t"] = np.tile(np.eye(32, dtype=np.float32), (4, 1))
    pnm = np.zeros((16, 512), dtype=np.float32)
    for t in range(4):
        for p in range(128):
            pnm[(t * 128 + p) // 32, t * 128 + p] = 1.0
    m["pn"] = pnm
    m["gateT"] = np.ascontiguousarray(inputs["L0_ffn_gate"][4 * b:4 * b + 4].T)
    m["bwrow"] = np.ascontiguousarray(
        inputs["L0_attn_bias_weight"][4 * hg:4 * hg + 4].reshape(1, 12))
    m["bqc"] = np.ascontiguousarray(
        inputs["L0_attn_bq"][F0:F0 + FW].reshape(8, 128).T)
    m["bkc"] = np.ascontiguousarray(
        inputs["L0_attn_bk"][F0:F0 + FW].reshape(8, 128).T)
    m["bvrow"] = np.ascontiguousarray(
        inputs["L0_attn_bv"][F0:F0 + FW].reshape(1, FW))
    m["fbrow"] = np.ascontiguousarray(
        inputs["L0_ffn_bias"][F0:F0 + FW].reshape(1, FW))
    lA = inputs["L0_ffn_lora_A"]
    m["lAnr"] = np.ascontiguousarray(
        lA.reshape(512, D).reshape(4, 128, D).transpose(1, 0, 2).reshape(128, 4 * D))
    lB = inputs["L0_ffn_lora_B"][:, F0:F0 + FW, :]
    lbnr = lB.transpose(0, 2, 1).reshape(512, FW)
    m["lBnr"] = np.ascontiguousarray(
        lbnr.reshape(4, 128, FW).transpose(1, 0, 2).reshape(128, 4 * FW))
    lAs = inputs["L0_ffn_lora_A_shared"]
    m["lAsT"] = np.ascontiguousarray(
        lAs.T.reshape(32, 128, 32).transpose(1, 0, 2).reshape(128, 32 * 32))
    m["lBsT"] = np.ascontiguousarray(inputs["L0_ffn_lora_B_shared"][F0:F0 + FW].T)
    for k in m:
        if m[k].dtype == np.float64:
            m[k] = m[k].astype(np.float32)
    return m


def kernel(**inputs):
    L = int(inputs["L0_attn_x"].shape[1])
    nc = build_program(L)
    in_maps = [_prep_core(inputs, c // 4, c % 4, L) for c in range(8)]
    res = run_bass_kernel_spmd(nc, in_maps, core_ids=list(range(8)))
    y = np.empty((2, L, D), dtype=np.float32)
    for c in range(8):
        y[c // 4, :, (c % 4) * FW:(c % 4 + 1) * FW] = res.results[c]["y"]
    return y


# version 23
# speedup vs baseline: 1.1108x; 1.0941x over previous
import numpy as np
import ml_dtypes
import concourse.bass as bass
import concourse.mybir as mybir
from concourse.bass_utils import run_bass_kernel_spmd

F32 = mybir.dt.float32
BF16 = mybir.dt.bfloat16
AF = mybir.ActivationFunctionType
ALU = mybir.AluOpType
AX = mybir.AxisListType

D = 4096
NCT = 32
NHC = 4
FW = 1024
EPS = 1e-5
SB_BASE = 16544


def _host_tcd():
    n = 8
    masks = sorted(range(n), key=lambda m: (bin(m).count('1'), m))
    inv = {m: i for i, m in enumerate(masks)}
    C = np.zeros((n, n, n))
    for i, a in enumerate(masks):
        for j, bb in enumerate(masks):
            sw, at = (0, a >> 1)
            while at:
                sw += bin(at & bb).count('1')
                at >>= 1
            C[i, j, inv[a ^ bb]] = -1.0 if sw & 1 else 1.0
    grades = np.array([bin(m).count('1') for m in masks])
    rev = np.where(grades * (grades - 1) // 2 % 2 == 1, -1.0, 1.0)
    bivk = C[:, :, [4, 5, 6]]
    TCD = np.zeros((4, 256, 256), dtype=np.float32)
    TCD[0] = np.kron(np.eye(32), np.diag(rev) / 16.0)
    for c in range(3):
        TCD[c + 1] = np.kron(np.eye(32), rev[:, None] * bivk[:, :, c])
    return TCD


ENGS = ("te", "ve", "ac", "sy", "gp")
SEM_OF = {"te": "TE", "ve": "VE", "ac": "AC", "sy": "SY", "gp": "GP"}


class Op:
    __slots__ = ("eng", "fn", "deps", "hazard", "dma", "stream", "pos", "val",
                 "needs_inc", "waits")

    def __init__(self, eng, fn, deps, hazard, dma, stream, pos):
        self.eng, self.fn, self.deps = eng, fn, deps
        self.hazard, self.dma, self.stream, self.pos = hazard, dma, stream, pos
        self.val = 0
        self.needs_inc = dma
        self.waits = []


class Sched:
    def __init__(self):
        self.ops = {e: [] for e in ENGS}
        self.streams = {}

    def op(self, eng, fn, deps=(), hazard=False):
        deps = [d for d in deps if d is not None]
        o = Op(eng, fn, deps, hazard, False, None, len(self.ops[eng]))
        self.ops[eng].append(o)
        return o

    def dma(self, eng, stream, fn, deps=()):
        deps = [d for d in deps if d is not None]
        o = Op(eng, fn, deps, False, True, stream, len(self.ops[eng]))
        self.ops[eng].append(o)
        self.streams.setdefault(stream, []).append(o)
        return o

    def resolve(self):
        def skip_self(d, o):
            return d.eng == o.eng and d.eng == "te" and not d.hazard

        for e in ENGS:
            for o in self.ops[e]:
                for d in o.deps:
                    if (not d.dma) and (not skip_self(d, o)):
                        d.needs_inc = True
        for e in ENGS:
            c = 0
            for o in self.ops[e]:
                if o.dma:
                    continue
                if o.needs_inc:
                    c += 1
                o.val = c
        for st, ops in self.streams.items():
            c = 0
            for o in ops:
                c += 16
                o.val = c
        for e in ENGS:
            seen = {}
            for o in self.ops[e]:
                want = {}
                for d in o.deps:
                    if d.dma:
                        s = "D:" + d.stream
                    elif skip_self(d, o):
                        assert d.pos < o.pos, (e, o.pos, d.pos)
                        continue
                    else:
                        s = SEM_OF[d.eng]
                    want[s] = max(want.get(s, 0), d.val)
                for s, v in sorted(want.items()):
                    if seen.get(s, 0) < v:
                        o.waits.append((s, v))
                        seen[s] = v

    def emit(self, nc, ent):
        self.resolve()
        sems = {}
        for s in set(SEM_OF.values()):
            sems[s] = ent(nc.semaphore(s))
        for st in self.streams:
            sems["D:" + st] = ent(nc.semaphore("D_" + st))
        with nc.Block() as block:
            hooks = {"te": block.tensor, "ve": block.vector, "ac": block.scalar,
                     "sy": block.sync, "gp": block.gpsimd}
            for e in ENGS:
                ops = self.ops[e]
                if not ops:
                    continue

                def body(eng, ops=ops, e=e):
                    for o in ops:
                        for s, v in o.waits:
                            eng.wait_ge(sems[s], v)
                        inst = o.fn(eng)
                        if o.dma:
                            inst.then_inc(sems["D:" + o.stream], 16)
                        elif o.needs_inc:
                            inst.then_inc(sems[SEM_OF[e]], 1)
                    tot = max((o.val for o in ops if not o.dma), default=0)
                    if tot > 0:
                        eng.wait_ge(sems[SEM_OF[e]], tot)
                    done = set()
                    for o in ops:
                        if o.dma and o.stream not in done:
                            done.add(o.stream)
                            eng.wait_ge(sems["D:" + o.stream],
                                        self.streams[o.stream][-1].val)
                hooks[e](body)


def build_program(L):
    assert L % 512 == 0
    NLB = L // 512
    NLT = L // 128
    NKT = L // 128
    NG = NKT // 2
    NQB = L // 512

    nc = bass.Bass("TRN2", target_bir_lowering=False, debug=False)
    ctx = []

    def ent(cm):
        v = cm.__enter__()
        ctx.append(cm)
        return v

    def din(name, shape, dt):
        return nc.dram_tensor(name, shape, dt, kind="ExternalInput")

    xbT = din("xbT", [128, NCT * L], F32)
    xres = din("xres", [L, FW], F32)
    wkT = din("wkT", [128, NCT * FW], F32)
    wvT = din("wvT", [128, NCT * FW], F32)
    kern = din("kern", [128, NCT * FW], F32)
    wq = din("wq", [128, 8 * D], F32)
    tcd = din("tcd", [128, 8 * 256], F32)
    i128 = din("i128", [128, 128], F32)
    ones128 = din("ones128", [128, 128], BF16)
    onesf = din("onesf", [1, 128], F32)
    i32t = din("i32t", [128, 32], F32)
    pn = din("pn", [16, 512], F32)
    gateT = din("gateT", [16, 4], F32)
    bwrow = din("bwrow", [1, 12], F32)
    bqc = din("bqc", [128, 8], F32)
    bkc = din("bkc", [128, 8], F32)
    bvrow = din("bvrow", [1, FW], F32)
    fbrow = din("fbrow", [1, FW], F32)
    lAnr = din("lAnr", [128, 4 * D], F32)
    lBnr = din("lBnr", [128, 4 * FW], F32)
    lAsT = din("lAsT", [128, 32 * 32], F32)
    lBsT = din("lBsT", [32, FW], F32)
    y_out = nc.dram_tensor("y", [L, FW], F32, kind="ExternalOutput")

    s_wqe = nc.dram_tensor("s_wqe", [128, NHC * NCT * 256], BF16)
    s_xnt = nc.dram_tensor("s_xnt", [NLB, 128, NCT * 512], BF16)
    s_kt = nc.dram_tensor("s_kt", [NHC, 2, 128, L], BF16)
    s_vv = nc.dram_tensor("s_vv", [NHC, NLT, 128, 256], BF16)
    s_qt = nc.dram_tensor("s_qt", [NHC, 2, 128, L], BF16)
    s_att = nc.dram_tensor("s_att", [NLT, 128, FW], BF16)
    s_aaug = nc.dram_tensor("s_aaug", [NHC, 128, NCT * 64], BF16)
    s_wvb = nc.dram_tensor("s_wvb", [128, NCT * FW], BF16)
    s_kernb = nc.dram_tensor("s_kernb", [128, NCT * FW], BF16)
    s_baug = nc.dram_tensor("s_baug", [NHC, 64, FW], BF16)

    PA = [ent(nc.psum_tensor("PA%d" % i, [128, 1024], F32)) for i in range(2)]
    PB = [ent(nc.psum_tensor("PB%d" % i, [128, 512], F32)) for i in range(4)]

    class Arena:
        def __init__(self):
            self.off = SB_BASE
            self.persist_end = None
            self.ranges = {}

        def alloc(self, name, shape, dt):
            per = int(np.prod(shape[1:])) * mybir.dt.size(dt)
            per = (per + 31) & ~31
            t = nc.alloc_sbuf_tensor_at(name, list(shape), dt, offset=self.off)
            self.ranges[id(t)] = (self.off, self.off + per)
            self.off += per
            assert self.off <= SB_BASE + 212800, (name, self.off)
            return t

        def mark_persist(self):
            self.persist_end = self.off

        def phase_reset(self):
            self.off = self.persist_end

    A = Arena()
    p_ones128 = A.alloc("p_ones128", [128, 128], BF16)
    p_onesf = A.alloc("p_onesf", [1, 128], F32)
    p_bqc = A.alloc("p_bqc", [128, 8], F32)
    p_bkc = A.alloc("p_bkc", [128, 8], F32)
    p_bqe = A.alloc("p_bqe", [128, 8], F32)
    p_bvbc = A.alloc("p_bvbc", [128, FW], F32)
    p_fbbc = A.alloc("p_fbbc", [128, FW], F32)
    p_negC = A.alloc("p_negC", [128, NHC], F32)
    A.mark_persist()

    S = Sched()

    A.phase_reset()
    t_tcd = A.alloc("t_tcd", [128, 8 * 256], F32)
    t_i32t = A.alloc("t_i32t", [128, 32], F32)
    t_pn = A.alloc("t_pn", [16, 512], F32)
    t_gateT = A.alloc("t_gateT", [16, 4], F32)
    t_bwrow = A.alloc("t_bwrow", [1, 12], F32)
    t_bvrow = A.alloc("t_bvrow", [1, FW], F32)
    t_fbrow = A.alloc("t_fbrow", [1, FW], F32)
    t_bwbc = A.alloc("t_bwbc", [128, 12], F32)
    t_gc = A.alloc("t_gc", [128, 16], F32)
    t_ttf = A.alloc("t_ttf", [128, 8 * 256], F32)
    t_ttb = A.alloc("t_ttb", [128, 8 * 256], BF16)
    t_ssb = A.alloc("t_ssb", [128, 16 * 32], BF16)
    t_lA = A.alloc("t_lA", [128, 4 * D], BF16)
    t_lB = A.alloc("t_lB", [128, 4 * FW], BF16)
    t_lAsT = A.alloc("t_lAsT", [128, 32 * 32], BF16)
    t_wqs = [A.alloc("t_wqs%d" % i, [128, 2 * D], BF16) for i in range(2)]
    t_wqeh = [A.alloc("t_wqeh%d" % i, [128, NCT * 256], BF16) for i in range(2)]
    t_aaug = A.alloc("t_aaug", [128, NCT * 64], BF16)
    t_baug = A.alloc("t_baug", [64, FW], BF16)

    for src, dst in ((ones128, p_ones128), (lAnr, t_lA),
                     (lBnr, t_lB), (lAsT, t_lAsT)):
        S.dma("gp", "g_once", lambda e, s=src, d=dst: e.dma_start(out=d[:], in_=s[:]))
    gload = S.dma("gp", "g_once",
                  lambda e: e.dma_start(out=t_baug[32:64, :], in_=lBsT[:]))
    wv_pc = S.dma("gp", "g_pc0", lambda e: e.dma_start(out=s_wvb[:], in_=wvT[:]))
    kern_pc = S.dma("gp", "g_pc1", lambda e: e.dma_start(out=s_kernb[:], in_=kern[:]))
    for src, dst in ((tcd, t_tcd), (i32t, t_i32t), (pn, t_pn), (gateT, t_gateT),
                     (bwrow, t_bwrow), (onesf, p_onesf), (bqc, p_bqc),
                     (bkc, p_bkc), (bvrow, t_bvrow)):
        S.dma("sy", "s_once", lambda e, s=src, d=dst: e.dma_start(out=d[:], in_=s[:]))
    sload = S.dma("sy", "s_once",
                  lambda e: e.dma_start(out=t_fbrow[:], in_=fbrow[:]))

    bw_mm = S.op("te", lambda e: e.matmul(PB[0][:, 0:12], p_onesf[:], t_bwrow[:],
                                          start=True, stop=True), deps=[sload])
    bv_mm = [S.op("te", lambda e, c0=c0: e.matmul(
        PA[0][:, c0:c0 + 512], p_onesf[:], t_bvrow[:, c0:c0 + 512],
        start=True, stop=True), deps=[sload]) for c0 in (0, 512)]
    fb_mm = [S.op("te", lambda e, c0=c0: e.matmul(
        PA[1][:, c0:c0 + 512], p_onesf[:], t_fbrow[:, c0:c0 + 512],
        start=True, stop=True), deps=[sload]) for c0 in (0, 512)]
    gc_mm = []
    for g4 in range(4):
        for tt in range(4):
            c = g4 * 4 + tt
            gc_mm.append(S.op("te", lambda e, c=c, g4=g4, tt=tt: e.matmul(
                PB[1][:, c:c + 1], t_pn[:, tt * 128:(tt + 1) * 128],
                t_gateT[:, g4:g4 + 1], start=True, stop=True), deps=[sload]))
    ev_bw = S.op("ve", lambda e: e.tensor_copy(t_bwbc[:], PB[0][:, 0:12]), deps=[bw_mm])
    ev_bv = S.op("ve", lambda e: e.tensor_copy(p_bvbc[:], PA[0][:, 0:1024]), deps=bv_mm)
    ev_fb = S.op("ve", lambda e: e.tensor_copy(p_fbbc[:], PA[1][:, 0:1024]), deps=fb_mm)
    ev_gc = S.op("ve", lambda e: e.tensor_copy(t_gc[:], PB[1][:, 0:16]), deps=gc_mm)

    s_tiles = []
    for c in range(16):
        s_tiles.append(S.op("ve", lambda e, c=c: e.tensor_scalar(
            t_ssb[:, c * 32:(c + 1) * 32], t_i32t[:], t_gc[:, c:c + 1], None,
            ALU.mult), deps=[ev_gc, sload]))

    tt_done = []
    for h in range(NHC):
        for st in range(2):
            o = (h * 2 + st) * 256
            tds = t_tcd[:, st * 256:(st + 1) * 256]
            tc0 = t_tcd[:, (2 + st) * 256:(3 + st) * 256]
            tc1 = t_tcd[:, (4 + st) * 256:(5 + st) * 256]
            tc2 = t_tcd[:, (6 + st) * 256:(7 + st) * 256]
            d1 = S.op("ve", lambda e, o=o, h=h, tc0=tc0, tds=tds: e.scalar_tensor_tensor(
                t_ttf[:, o:o + 256], tc0, t_bwbc[:, h * 3:h * 3 + 1], tds,
                ALU.mult, ALU.add), deps=[sload, ev_bw], hazard=True)
            d2 = S.op("ve", lambda e, o=o, h=h, tc1=tc1: e.scalar_tensor_tensor(
                t_ttf[:, o:o + 256], tc1, t_bwbc[:, h * 3 + 1:h * 3 + 2],
                t_ttf[:, o:o + 256], ALU.mult, ALU.add), deps=[d1], hazard=True)
            d3 = S.op("ve", lambda e, o=o, h=h, tc2=tc2: e.scalar_tensor_tensor(
                t_ttf[:, o:o + 256], tc2, t_bwbc[:, h * 3 + 2:h * 3 + 3],
                t_ttf[:, o:o + 256], ALU.mult, ALU.add), deps=[d2], hazard=True)
            tt_done.append(d3)
    tt_cast = S.op("ve", lambda e: e.tensor_copy(t_ttb[:], t_ttf[:]), deps=tt_done,
                   hazard=True)

    bqe_mm = []
    for h in range(NHC):
        for rs in range(2):
            col = h * 2 + rs
            mm = None
            for st in range(2):
                mm = S.op("te", lambda e, h=h, rs=rs, st=st, col=col: e.matmul(
                    PB[2][:, col:col + 1],
                    t_ttf[:, (h * 2 + st) * 256 + rs * 128:
                          (h * 2 + st) * 256 + rs * 128 + 128],
                    p_bqc[:, h * 2 + st:h * 2 + st + 1],
                    start=(st == 0), stop=(st == 1)),
                    deps=(tt_done + [sload] if st == 0 else [mm]))
            bqe_mm.append(mm)
    ev_bqe = S.op("ve", lambda e: e.tensor_copy(p_bqe[:], PB[2][:, 0:8]), deps=bqe_mm)

    ab_last = None
    st_a_prev = None
    for g4 in range(4):
        ae_ev = {}
        for isl in range(32):
            mm = None
            for tt in range(4):
                first = (tt == 0)
                dpm = [gload, s_tiles[g4 * 4 + tt]]
                if first and isl >= 8:
                    dpm.append(ae_ev[isl - 8])
                mm = S.op("te", lambda e, g4=g4, isl=isl, tt=tt: e.matmul(
                    PB[3][:, (isl % 8) * 32:(isl % 8) * 32 + 32],
                    t_lA[:, tt * D + isl * 128:tt * D + (isl + 1) * 128],
                    t_ssb[:, (g4 * 4 + tt) * 32:(g4 * 4 + tt + 1) * 32],
                    start=(tt == 0), stop=(tt == 3)),
                    deps=(dpm if first else [mm]))
            if isl % 8 == 7:
                ev = S.op("ve", lambda e, isl=isl: e.tensor_copy(
                    t_aaug[:].rearrange("p (i r) -> p i r", i=NCT)
                    [:, isl - 7:isl + 1, 0:32],
                    PB[3][:, 0:256].rearrange("p (i r) -> p i r", i=8)),
                    deps=[mm] + ([st_a_prev] if isl == 7 else []))
                for k in range(isl - 7, isl + 1):
                    ae_ev[k] = ev
        sh_cp = S.op("ve", lambda e: e.tensor_copy(
            t_aaug[:].rearrange("p (i r) -> p i r", i=NCT)[:, :, 32:64],
            t_lAsT[:].rearrange("p (i r) -> p i r", i=NCT)),
            deps=[gload] + ([ab_last] if ab_last else []))
        st_a = S.dma("gp", "g_ab", lambda e, g4=g4: e.dma_start(
            out=s_aaug[g4], in_=t_aaug[:]), deps=[ae_ev[31], sh_cp])
        st_a_prev = st_a
        be_ev = []
        for oc in range(2):
            mm = None
            for tt in range(4):
                first = (tt == 0)
                dpm = [gload, s_tiles[g4 * 4 + tt], ev_bqe]
                if first and be_ev:
                    dpm.append(be_ev[-1])
                mm = S.op("te", lambda e, g4=g4, oc=oc, tt=tt: e.matmul(
                    PB[2][0:32, 0:512],
                    t_ssb[:, (g4 * 4 + tt) * 32:(g4 * 4 + tt + 1) * 32],
                    t_lB[:, tt * FW + oc * 512:tt * FW + oc * 512 + 512],
                    start=(tt == 0), stop=(tt == 3)),
                    deps=(dpm if first else [mm]))
            be_ev.append(S.op("ve", lambda e, oc=oc: e.tensor_copy(
                t_baug[0:32, oc * 512:oc * 512 + 512], PB[2][0:32, 0:512]),
                deps=[mm]))
        ab_last = S.dma("gp", "g_ab", lambda e, g4=g4: e.dma_start(
            out=s_baug[g4], in_=t_baug[:]), deps=be_ev + [gload, st_a])

    wq_ld = []
    for h in range(NHC):
        wq_ld.append(S.dma("gp", "g_wq%d" % (h % 2), lambda e, h=h: e.dma_start(
            out=t_wqs[h % 2][:], in_=wq[:, h * 2 * D:(h + 1) * 2 * D])))
    fold_done = []
    prev_evs_last = None
    for h in range(NHC):
        evs = []
        for ct in range(NCT):
            slot = ct % 4
            pa = PA[0] if slot < 2 else PA[1]
            lo = (slot % 2) * 512
            mm = None
            for st in range(2):
                first = (st == 0)
                dpm = [wq_ld[h], tt_cast]
                if first and ct < 4:
                    dpm += [ev_bv, ev_fb, prev_evs_last]
                if first and ct >= 4:
                    dpm.append(evs[ct - 4])
                if first and ct == 0 and h >= 2:
                    dpm.append(fold_done[h - 2])
                mm = S.op("te", lambda e, h=h, ct=ct, st=st, pa=pa, lo=lo: e.matmul(
                    pa[:, lo:lo + 256],
                    t_wqs[h % 2][:, st * D + ct * 128:st * D + (ct + 1) * 128],
                    t_ttb[:, (h * 2 + st) * 256:(h * 2 + st + 1) * 256],
                    start=(st == 0), stop=(st == 1)),
                    deps=(dpm if first else [mm]))
            evs.append(S.op("ve", lambda e, h=h, ct=ct, pa=pa, lo=lo: e.tensor_copy(
                t_wqeh[h % 2][:, ct * 256:(ct + 1) * 256], pa[:, lo:lo + 256]),
                deps=[mm]))
        stw = S.dma("sy", "s_wqe%d" % (h % 2), lambda e, h=h: e.dma_start(
            out=s_wqe[:, h * NCT * 256:(h + 1) * NCT * 256], in_=t_wqeh[h % 2][:]),
            deps=[evs[-1]])
        fold_done.append(stw)
        prev_evs_last = evs[-1]
        if h + 2 < NHC:
            wq_ld[h + 2].deps.append(evs[-1])

    p0_done = [ab_last, fold_done[-1], fold_done[-2], ev_bv, ev_fb, ev_bqe,
               gload, sload]

    A.phase_reset()
    t_wkT = A.alloc("t_wkT", [128, NCT * FW], BF16)
    t_xnt = [A.alloc("t_xnt%d" % i, [128, NCT * 512], BF16) for i in range(2)]
    t_xsq = A.alloc("t_xsq", [128, 512], BF16)
    t_red = A.alloc("t_red", [128, 3 * 512], BF16)
    t_stk = A.alloc("t_stk0", [128, 2 * 512], BF16)
    t_tmp = A.alloc("t_tmp", [128, 2 * 512], F32)
    _rw = [A.alloc("t_rw%d" % i, [1, 512], F32) for i in range(6)]
    t_nm = [_rw[0][:], _rw[1][:]]
    t_rr = [_rw[2][:], _rw[3][:]]
    sc0, sc1 = (_rw[4][:], _rw[5][:])
    t_xtb = [A.alloc("t_xtb%d" % i, [128, 8 * 512], BF16) for i in range(6)]

    wk_ld = S.dma("gp", "g_wk", lambda e: e.dma_start(out=t_wkT[:], in_=wkT[:]),
                  deps=p0_done)
    xbTv = xbT[:].rearrange("p (c l) -> p c l", c=NCT)

    xload = {}
    sx_mm = {}
    ssq_mm = {}
    rowe1 = {}
    rowe2 = {}
    kmm_last = {}
    kevac = {}
    kdma = {}
    xnt_st = {}
    tt_last = {}
    qnorm = {}
    bc_ops = {}
    prev_ssq = [None]

    def p1_xload(l, q, dps):
        xload[(l, q)] = S.dma("gp", "g_xt%d" % (q % 2),
                              lambda e, l=l, q=q: e.dma_start(
            out=t_xtb[(4 * l + q) % 6][:],
            in_=xbTv[:, q * 8:(q + 1) * 8, l * 512:(l + 1) * 512]), deps=dps)

    def p1_stats_group(l, q):
        p = l % 2
        slot = t_xtb[(4 * l + q) % 6]
        cks = [slot[:, c * 512:(c + 1) * 512] for c in range(8)]
        r0, r1, r2 = [t_red[:, j * 512:(j + 1) * 512] for j in range(3)]
        dp0 = [xload[(l, q)], prev_ssq[0]]
        if q == 0:
            dp0 += [rowe1[l - 2], rowe2[l - 2]] if l >= 2 else [ev_bqe, ab_last]
            if l == 0:
                dp0 += [fold_done[-1]]
        tt = lambda o, a, b, dps: S.op(
            "ve", lambda e, o=o, a=a, b=b: e.tensor_tensor(o, a, b, ALU.add),
            deps=dps, hazard=True)
        o = tt(r0, cks[0], cks[1], dp0)
        o = tt(r1, cks[2], cks[3], [o])
        o = tt(r0, r0, r1, [o])
        o = tt(r1, cks[4], cks[5], [o])
        o = tt(r2, cks[6], cks[7], [o])
        o = tt(r1, r1, r2, [o])
        o = tt(r0, r0, r1, [o])
        mmx = S.op("te", lambda e, p=p, q=q: e.matmul(
            PB[2][32 * p:32 * p + 1, 0:512], p_ones128[:, 0:1], r0,
            start=(q == 0), stop=(q == 3)), deps=[o])
        sx_mm[(l, q)] = mmx
        sq = lambda o2, a, dps: S.op(
            "ve", lambda e, o2=o2, a=a: e.tensor_mul(o2, a, a),
            deps=dps, hazard=True)
        o = sq(r1, cks[0], [o])
        for half, rr2 in ((0, r1), (1, r2)):
            if half == 1:
                o = sq(r2, cks[4], [o])
            for c in range(half * 4 + 1, half * 4 + 4):
                o = sq(t_xsq[:], cks[c], [o])
                o = tt(rr2, rr2, t_xsq[:], [o])
        o = tt(r1, r1, r2, [o])
        mmq = S.op("te", lambda e, p=p, q=q: e.matmul(
            PB[3][32 * p:32 * p + 1, 0:512], p_ones128[:, 0:1], r1,
            start=(q == 0), stop=(q == 3)), deps=[o])
        ssq_mm[(l, q)] = mmq
        prev_ssq[0] = mmq

    prev_rm = [None]

    def p1_rowmath(l):
        p = l % 2
        e1 = S.op("ve", lambda e, p=p: e.tensor_copy(
            sc0, PB[2][32 * p:32 * p + 1, 0:512]),
            deps=[sx_mm[(l, 3)], bc_ops.get(l - 2), prev_rm[0]])
        e2 = S.op("ve", lambda e, p=p: e.tensor_copy(
            sc1, PB[3][32 * p:32 * p + 1, 0:512]),
            deps=[ssq_mm[(l, 3)]])
        s1 = S.op("ve", lambda e, p=p: e.tensor_scalar_mul(
            t_nm[p], sc0, -1.0 / D),
            deps=[e1, bc_ops.get(l - 2)], hazard=True)
        m2 = S.op("ve", lambda e, p=p: e.tensor_mul(
            sc0, t_nm[p], t_nm[p]),
            deps=[s1], hazard=True)
        me = S.op("ve", lambda e: e.tensor_scalar_add(
            sc0, sc0, -EPS), deps=[m2], hazard=True)
        vpe = S.op("ve", lambda e: e.scalar_tensor_tensor(
            sc1, sc1, 1.0 / D, sc0, ALU.mult, ALU.subtract),
            deps=[e2, me], hazard=True)
        lnv = S.op("ac", lambda e: e.activation(
            sc0, sc1, AF.Ln),
            deps=[vpe], hazard=True)
        rst = S.op("ac", lambda e, p=p: e.activation(
            t_rr[p], sc0, AF.Exp, scale=-0.5),
            deps=[lnv, bc_ops.get(l - 2)], hazard=True)
        rowe1[l] = e1
        rowe2[l] = e2
        prev_rm[0] = rst
        return s1, rst

    def p1_bc(l, s1, rst):
        p = l % 2
        b1 = S.op("te", lambda e, p=p: e.matmul(
            PA[p][:, 0:512], p_onesf[:], t_nm[p],
            start=True, stop=True),
            deps=[s1] + ([tt_last[l - 2]] if l >= 2 else list(p0_done)))
        b2 = S.op("te", lambda e, p=p: e.matmul(
            PA[p][:, 512:1024], p_onesf[:], t_rr[p],
            start=True, stop=True), deps=[b1, rst])
        bc_ops[l] = b2

    def p1_norm(l):
        p = l % 2
        prev = tt_last.get(l - 1)
        for qq in range(4):
            slot = t_xtb[(4 * l + qq) % 6]
            for q2 in range(4):
                o1 = None
                for k in range(2):
                    c8 = q2 * 2 + k
                    d1 = [xload[(l, qq)], bc_ops[l]]
                    if l >= 2 and qq == 0 and c8 == 0:
                        d1 += [kmm_last[(l - 2, 7)], xnt_st[l - 2]]
                    if k == 0 and prev is not None:
                        d1.append(prev)
                    o1 = S.op("ve", lambda e, c8=c8, k=k, slot=slot: e.tensor_tensor(
                        t_tmp[:, k * 512:(k + 1) * 512],
                        slot[:, c8 * 512:(c8 + 1) * 512],
                        PA[p][:, 0:512], ALU.add),
                        deps=d1, hazard=(k == 1))
                for k in range(2):
                    ct = qq * 8 + q2 * 2 + k
                    prev = S.op("ve", lambda e, ct=ct, k=k, p=p: e.tensor_tensor(
                        t_xnt[p][:, ct * 512:(ct + 1) * 512],
                        t_tmp[:, k * 512:(k + 1) * 512],
                        PA[p][:, 512:1024], ALU.mult),
                        deps=[o1], hazard=(k == 1))
            qnorm[(l, qq)] = prev
        tt_last[l] = prev
        xnt_st[l] = S.dma("sy", "s_xnt%d" % p,
                          lambda e, l=l, p=p: e.dma_start(
            out=s_xnt[l], in_=t_xnt[p][:]), deps=[prev])

    def p1_kt_ft(l, ft):
        bpar = l % 2
        pb = PB[ft % 2]
        mm = None
        for ct in range(NCT):
            if ft == 0 and ct % 8 == 0:
                dpm = [qnorm[(l, ct // 8)], wk_ld]
                if ct == 0 and l >= 1:
                    dpm.append(kevac[(l - 1, 6)])
            elif ct == 0:
                dpm = [wk_ld]
                if ft >= 2:
                    dpm.append(kevac[(l, ft - 2)])
                elif l >= 1:
                    dpm.append(kevac[(l - 1, 6 + ft)])
            else:
                dpm = [mm]
            mm = S.op("te", lambda e, ft=ft, ct=ct, pb=pb, bpar=bpar: e.matmul(
                pb[:, 0:512],
                t_wkT[:, ct * FW + ft * 128:ct * FW + (ft + 1) * 128],
                t_xnt[bpar][:, ct * 512:(ct + 1) * 512],
                start=(ct == 0), stop=(ct == NCT - 1)),
                deps=dpm)
        kmm_last[(l, ft)] = mm
        edp = [mm]
        if ft >= 2:
            edp.append(kdma[(l, ft - 2)])
        elif l >= 1:
            edp.append(kdma[(l - 1, ft + 6)])
        else:
            edp += [fold_done[-1], ab_last]
        kevac[(l, ft)] = S.op("ac", lambda e, ft=ft, pb=pb: e.activation(
            t_stk[:, (ft % 2) * 512:(ft % 2 + 1) * 512], pb[:, 0:512],
            AF.Identity, bias=p_bkc[:, ft:ft + 1]), deps=edp)
        kdma[(l, ft)] = S.dma("sy", "s_kd%d" % bpar,
                              lambda e, l=l, ft=ft: e.dma_start(
            out=s_kt[ft // 2, ft % 2, :, l * 512:(l + 1) * 512],
            in_=t_stk[:, (ft % 2) * 512:(ft % 2 + 1) * 512]),
            deps=[kevac[(l, ft)]])

    for q in range(4):
        p1_xload(0, q, [])
    for q in range(2):
        p1_xload(1, q, [])
    for q in range(4):
        p1_stats_group(0, q)
    _s1, _rst = p1_rowmath(0)
    p1_bc(0, _s1, _rst)
    p1_stats_group(1, 0)
    p1_stats_group(1, 1)
    p1_norm(0)
    for l in range(NLB):
        if l >= 1:
            p1_norm(l)
        if l + 1 < NLB:
            p1_xload(l + 1, 2, [qnorm[(l, 0)]])
            p1_xload(l + 1, 3, [qnorm[(l, 1)]])
        if l + 2 < NLB:
            p1_xload(l + 2, 0, [qnorm[(l, 2)]])
            p1_xload(l + 2, 1, [qnorm[(l, 3)]])
        for ft in range(8):
            p1_kt_ft(l, ft)
            if ft == 2 and l + 1 < NLB:
                p1_stats_group(l + 1, 2)
            if ft == 3 and l + 1 < NLB:
                p1_stats_group(l + 1, 3)
            if ft == 4 and l + 1 < NLB:
                _s1, _rst = p1_rowmath(l + 1)
            if ft == 5 and l + 1 < NLB:
                p1_bc(l + 1, _s1, _rst)
            if ft == 6 and l + 2 < NLB:
                p1_stats_group(l + 2, 0)
            if ft == 7 and l + 2 < NLB:
                p1_stats_group(l + 2, 1)
    p1_tail = [kdma[(NLB - 1, 7)], kdma[(NLB - 2, 7)],
               xnt_st[NLB - 1], xnt_st[NLB - 2]]

    A.phase_reset()
    t_wvT = A.alloc("t_wvT", [128, NCT * FW], BF16)
    t_xv = [A.alloc("t_xv%d" % i, [128, NCT * 512], BF16) for i in range(2)]
    t_stv = [A.alloc("t_stv%d" % i, [128, FW], BF16) for i in range(2)]

    lev = (NLB - 1) if (NLB - 1) % 2 == 0 else (NLB - 2)
    lod = (NLB - 1) if (NLB - 1) % 2 == 1 else (NLB - 2)
    p1_guards = [
        (A.ranges[id(t_wkT)], [kmm_last[(NLB - 1, 7)]]),
        (A.ranges[id(t_xnt[0])], [kmm_last[(lev, 7)], xnt_st[lev]]),
        (A.ranges[id(t_xnt[1])], [kmm_last[(lod, 7)], xnt_st[lod]]),
        ((A.ranges[id(t_xsq)][0], A.ranges[id(t_red)][1]),
         [ssq_mm[(NLB - 1, 3)]]),
        (A.ranges[id(t_stk)], [kdma[(NLB - 1, 7)], kdma[(NLB - 2, 7)]]),
        (A.ranges[id(t_tmp)], [tt_last[NLB - 1]]),
        ((A.ranges[id(_rw[0])][0], A.ranges[id(_rw[5])][1]),
         [bc_ops[NLB - 1], prev_rm[0]]),
    ] + [(A.ranges[id(t_xtb[i])], [tt_last[NLB - 1], tt_last[NLB - 2]])
         for i in range(6)]

    def p1_guard(t):
        lo, hi = A.ranges[id(t)]
        deps = []
        for (glo, ghi), ds2 in p1_guards:
            if glo < hi and lo < ghi:
                deps += ds2
        return deps

    P3_HEAD = 2 * (NLT * 257 * 2 + 31 & ~31) + 4 * L * 2 + 160 + 32 + 32 + \
        2 * 2048
    _sv = A.off
    A.off = A.persist_end + P3_HEAD
    t_kt = A.alloc("t_kt", [128, 2 * L], BF16)
    t_qt = A.alloc("t_qt", [128, 2 * L], BF16)
    p3_kt_end = A.off
    A.off = _sv
    kt_done = {}
    qt_done = {}
    LH = L // 2
    for ch in range(2):
        dpk = p1_guard(t_kt) + [kdma[(NLB - 1, 7)], kdma[(NLB - 2, 7)]] \
            if ch == 0 else []
        kt_done[(0, ch)] = [S.dma("ac", "a_k3",
                                  lambda e, rs=rs, ch=ch: e.dma_start(
            out=t_kt[:, rs * L + ch * LH:rs * L + (ch + 1) * LH],
            in_=s_kt[0, rs, :, ch * LH:(ch + 1) * LH]),
            deps=(dpk if rs == 0 else []))
            for rs in range(2)]

    wv_ld = S.dma("gp", "g_wv", lambda e: e.dma_start(out=t_wvT[:], in_=s_wvb[:]),
                  deps=p1_guard(t_wvT) + [wv_pc])
    xv_ld = {}
    vmm_last = {}
    vevac = {}
    vdma = {}
    for lblk in range(NLB):
        par = lblk % 2
        dp = (p1_guard(t_xv[par]) if lblk < 2 else [vmm_last[(lblk - 2, 3)]]) + \
            [xnt_st[lblk]]
        xv_ld[lblk] = S.dma("sy", "s_xv%d" % par, lambda e, lblk=lblk, par=par: e.dma_start(
            out=t_xv[par][:], in_=s_xnt[lblk]), deps=dp)
        for mt in range(4):
            pa = PA[mt % 2]
            mm = None
            for oc in range(2):
                for ct in range(NCT):
                    first = (oc == 0 and ct == 0)
                    dpm = [xv_ld[lblk], wv_ld]
                    if first:
                        key = (lblk, mt - 2) if mt >= 2 else (lblk - 1, mt + 2)
                        dpm.append(vevac.get(key))
                    mm = S.op("te", lambda e, mt=mt, oc=oc, ct=ct, pa=pa, par=par: e.matmul(
                        pa[:, oc * 512:(oc + 1) * 512],
                        t_xv[par][:, ct * 512 + mt * 128:ct * 512 + (mt + 1) * 128],
                        t_wvT[:, ct * FW + oc * 512:ct * FW + (oc + 1) * 512],
                        start=(ct == 0), stop=(ct == NCT - 1)),
                        deps=(dpm if first else [mm]))
            vmm_last[(lblk, mt)] = mm
            edp = [mm, ev_bv]
            key = (lblk, mt - 2) if mt >= 2 else (lblk - 1, mt + 2)
            if key in vdma:
                edp.append(vdma[key][-1])
            elif lblk == 0:
                edp += p1_guard(t_stv[mt % 2])
            vevac[(lblk, mt)] = S.op("ve", lambda e, mt=mt, pa=pa: e.tensor_tensor(
                t_stv[mt % 2][:], pa[:, 0:1024], p_bvbc[:], ALU.add), deps=edp)
            vdma[(lblk, mt)] = [S.dma("gp", "g_vd%d" % (mt % 2),
                                      lambda e, lblk=lblk, mt=mt, h=h: e.dma_start(
                out=s_vv[h, lblk * 4 + mt],
                in_=t_stv[mt % 2][:, h * 256:(h + 1) * 256]),
                deps=[vevac[(lblk, mt)]]) for h in range(NHC)]
    p2v_tail = [vdma[(NLB - 1, 2)][-1], vdma[(NLB - 1, 3)][-1]]

    A.phase_reset()
    t_wqe = A.alloc("t_wqe", [128, NHC * NCT * 256], BF16)
    t_xq = [A.alloc("t_xq%d" % i, [128, NCT * 512], BF16) for i in range(2)]
    t_stq = [A.alloc("t_stq%d" % i, [128, 512], BF16) for i in range(2)]

    def p2v_guard(t):
        lo, hi = A.ranges[id(t)]
        deps = []
        for tt2, ds2 in ((t_wvT, [vmm_last[(NLB - 1, 3)]]),
                         (t_xv[0], [vmm_last[(NLB - 1, 3)]]),
                         (t_xv[1], [vmm_last[(NLB - 1, 3)]]),
                         (t_stv[0], [vdma[(NLB - 1, 2)][-1]]),
                         (t_stv[1], [vdma[(NLB - 1, 3)][-1]])):
            glo, ghi = A.ranges[id(tt2)]
            if glo < hi and lo < ghi:
                deps += ds2
        if hi > A.ranges[id(t_stv[1])][1]:
            deps += p2v_tail
        return deps

    wqe_ld = S.dma("sy", "s_wqeld", lambda e: e.dma_start(out=t_wqe[:], in_=s_wqe[:]),
                   deps=p2v_guard(t_wqe) + [fold_done[-1], fold_done[-2]])
    xq_ld = {}
    qmm_last = {}
    qevac = {}
    qdma = {}
    for lblk in range(NLB):
        par = lblk % 2
        dp = (p2v_guard(t_xq[par]) if lblk < 2 else [qmm_last[(lblk - 2, 7)]]) + \
            [xnt_st[lblk]]
        xq_ld[lblk] = S.dma("sy", "s_xq%d" % par, lambda e, lblk=lblk, par=par: e.dma_start(
            out=t_xq[par][:], in_=s_xnt[lblk]), deps=dp)
        for rt in range(8):
            h, rs = rt // 2, rt % 2
            pb = PB[rt % 2]
            mm = None
            for ct in range(NCT):
                first = (ct == 0)
                dpm = [xq_ld[lblk], wqe_ld]
                if first:
                    key = (lblk, rt - 2) if rt >= 2 else (lblk - 1, rt + 6)
                    dpm.append(qevac.get(key))
                mm = S.op("te", lambda e, h=h, rs=rs, ct=ct, pb=pb, par=par: e.matmul(
                    pb[:, 0:512],
                    t_wqe[:, (h * NCT + ct) * 256 + rs * 128:
                          (h * NCT + ct) * 256 + (rs + 1) * 128],
                    t_xq[par][:, ct * 512:(ct + 1) * 512],
                    start=(ct == 0), stop=(ct == NCT - 1)),
                    deps=(dpm if first else [mm]))
            qmm_last[(lblk, rt)] = mm
            edp = [mm, ev_bqe]
            key = (lblk, rt - 2) if rt >= 2 else (lblk - 1, rt + 6)
            edp.append(qdma.get(key))
            if key not in qdma and lblk == 0:
                edp += p2v_guard(t_stq[rt % 2])
            qevac[(lblk, rt)] = S.op("ac", lambda e, rt=rt, pb=pb: e.activation(
                t_stq[rt % 2][:], pb[:, 0:512], AF.Identity,
                bias=p_bqe[:, rt:rt + 1]), deps=edp)
            qdma[(lblk, rt)] = S.dma("gp", "g_qd%d" % (rt % 2),
                                     lambda e, lblk=lblk, rt=rt: e.dma_start(
                out=s_qt[rt // 2, rt % 2, :, lblk * 512:(lblk + 1) * 512],
                in_=t_stq[rt % 2][:]), deps=[qevac[(lblk, rt)]])
        dq0 = [qdma[(lblk, 0)], qdma[(lblk, 1)]]
        if lblk == 0:
            dq0 = dq0 + p1_guard(t_qt)
        qt_done[(0, lblk)] = [
            S.dma("gp", "g_q3", lambda e, lblk=lblk: e.dma_start(
                out=t_qt[:, lblk * 512:(lblk + 1) * 512],
                in_=s_qt[0, 0, :, lblk * 512:(lblk + 1) * 512]), deps=dq0),
            S.dma("gp", "g_q3", lambda e, lblk=lblk: e.dma_start(
                out=t_qt[:, L + lblk * 512:L + (lblk + 1) * 512],
                in_=s_qt[0, 1, :, lblk * 512:(lblk + 1) * 512]), deps=dq0)]
    p2_tail = [qdma[(NLB - 1, 6)], qdma[(NLB - 1, 7)]]

    A.phase_reset()
    t_vs2 = [A.alloc("t_vs%d" % i, [128, NLT * 257], BF16) for i in range(2)]
    t_sqpe = A.alloc("t_sqpe", [128, 4 * L], BF16)
    t_sq = t_sqpe
    t_mx = A.alloc("t_mx", [128, 40], F32)
    t_sums = A.alloc("t_sums", [128, 4], F32)
    t_rec = A.alloc("t_rec", [128, 4], F32)
    t_sta = [A.alloc("t_sta%d" % i, [128, NHC * 256], BF16) for i in range(2)]
    q_ext = A.ranges[id(t_stq[1])][1]
    v_ext = A.ranges[id(t_stv[1])][1]
    assert A.off == A.ranges[id(t_kt)][0] >= max(q_ext, v_ext), (
        A.off, A.ranges[id(t_kt)], q_ext, v_ext)
    A.off = p3_kt_end
    _sv = A.off
    A.off = A.persist_end
    t_kern = A.alloc("t_kern", [128, NCT * FW], BF16)
    A.off = _sv
    KSPL = (NLT * 257 * 2 + 31 & ~31) // 2
    kern_ldA = None

    q_last = qmm_last[(NLB - 1, 7)]
    vs_memset = [S.op("ve", lambda e, i=i: e.memset(
        t_vs2[i][:].rearrange("p (m f) -> p m f", m=NLT)[:, :, 256:257], 1.0),
        deps=[q_last]) for i in range(2)]

    exp_ops = []
    attn_ev = {}
    attn_dma = {}
    head_explast = {}
    head_avlast = {}
    avdone = {}
    avof = {}
    expof = {}
    explast = {}
    pend = [None]

    CG = NG // 2
    nch2 = 2 * NQB

    def pexp_ap(gK, gc, sl=None):
        base = (gK % 2) * 2 * L + gc * 1024
        if sl is None:
            return t_sqpe[:, base:base + 1024]
        return t_sqpe[:, base + sl[0]:base + sl[1]]

    def emit_denoms(hp, lb, avl):
        cps = []
        for ls in range(4):
            cps.append(S.op("ve", lambda e, ls=ls: e.tensor_copy(
                t_sums[:, ls:ls + 1], PB[ls][:, 256:257]), deps=[avl]))
        rec = S.op("ve", lambda e: e.reciprocal(t_rec[:], t_sums[:]),
                   deps=cps, hazard=True)
        spar = lb % 2
        evs = []
        for ls in range(4):
            edp2 = [rec, avl]
            wkey = (hp, lb - 2) if lb >= 2 else (hp - 1, NQB - 2 + lb)
            if wkey in attn_dma:
                edp2.append(attn_dma[wkey][-1])
            evs.append(S.op("ve", lambda e, ls=ls, spar=spar: e.tensor_scalar_mul(
                t_sta[spar][:, ls * 256:(ls + 1) * 256],
                PB[ls][:, 0:256], t_rec[:, ls:ls + 1]),
                deps=edp2))
        attn_ev[(hp, lb)] = evs
        attn_dma[(hp, lb)] = [S.dma("gp", "g_att%d" % spar,
                                    lambda e, hp=hp, lb=lb, ls=ls, spar=spar: e.dma_start(
            out=s_att[lb * 4 + ls, :, hp * 256:(hp + 1) * 256],
            in_=t_sta[spar][:, ls * 256:(ls + 1) * 256]),
            deps=[evs[ls]]) for ls in range(4)]

    def emit_av_group(gc, lds_cur):
        hp, Xp, gKp, t_vsp = pend[0]
        lbp, cp_ = Xp // 2, Xp % 2
        gp = cp_ * CG + gc
        avdp0 = [expof[(gKp, gc)], lds_cur]
        if gc == 0 and cp_ == 0:
            prev_ev = (attn_ev[(hp, lbp - 1)] if lbp >= 1 else
                       (attn_ev[(hp - 1, NQB - 1)] if hp >= 1 else []))
            avdp0 += prev_ev
        avmm = None
        for kq in range(2):
            kt = 2 * gp + kq
            for ls in range(4):
                first = (kq == 0 and ls == 0)
                avmm = S.op("te", lambda e, gKp=gKp, gc=gc, kq=kq, ls=ls, kt=kt, gp=gp, cp_=cp_, t_vsp=t_vsp: e.matmul(
                    PB[ls][:, 0:257],
                    pexp_ap(gKp, gc, (kq * 512 + ls * 128,
                                      kq * 512 + (ls + 1) * 128)),
                    t_vsp[:, kt * 257:(kt + 1) * 257],
                    start=(cp_ == 0 and gp == 0 and kq == 0),
                    stop=(cp_ == 1 and gp == NG - 1 and kq == 1)),
                    deps=(avdp0 if first else [avmm]))
        avdone[gKp] = avmm
        avof[(gKp, gc)] = avmm
        if gc == CG - 1:
            if cp_ == 1:
                emit_denoms(hp, lbp, avmm)
            if Xp == nch2 - 1:
                head_avlast[hp] = avmm
            pend[0] = None
        return avmm

    smm_of = {}
    smm_head_last = {}
    for h in range(NHC):
        t_vs = t_vs2[h % 2]
        dpv = [vs_memset[h % 2]] + \
            ([head_avlast[h - 2]] if h >= 2 else [q_last] + p2v_tail)
        lds = None
        for mt8 in range(0, NLT, 8):
            lds = S.dma("sy", "s_h3", lambda e, h=h, mt8=mt8, t_vs=t_vs: e.dma_start(
                out=t_vs[:].rearrange("p (m f) -> p m f", m=NLT)
                [:, mt8:mt8 + 8, 0:256],
                in_=s_vv[h, mt8:mt8 + 8].rearrange("m p f -> p m f")),
                deps=(dpv if mt8 == 0 else []))
        if h >= 1:
            for ch in range(2):
                kt_done[(h, ch)] = [S.dma("ac", "a_k3",
                                          lambda e, h=h, rs=rs, ch=ch: e.dma_start(
                    out=t_kt[:, rs * L + ch * LH:rs * L + (ch + 1) * LH],
                    in_=s_kt[h, rs, :, ch * LH:(ch + 1) * LH]),
                    deps=([smm_head_last[h - 1]]
                          if ch == 0 and rs == 0 else []))
                    for rs in range(2)]
            for lb in range(NQB):
                dq0 = [smm_of[(h - 1, 2 * lb + 1)]]
                qt_done[(h, lb)] = [
                    S.dma("sy", "s_q3", lambda e, h=h, lb=lb: e.dma_start(
                        out=t_qt[:, lb * 512:(lb + 1) * 512],
                        in_=s_qt[h, 0, :, lb * 512:(lb + 1) * 512]), deps=dq0),
                    S.dma("gp", "g_q3", lambda e, h=h, lb=lb: e.dma_start(
                        out=t_qt[:, L + lb * 512:L + (lb + 1) * 512],
                        in_=s_qt[h, 1, :, lb * 512:(lb + 1) * 512]), deps=dq0)]
        ktq_all = (kt_done[(h, 0)] + kt_done[(h, 1)] +
                   qt_done[(h, NQB - 1)])

        gK0 = h * nch2
        fms = []
        norm_red_last = {}
        sq_mm_last = None
        for side, (src2, base) in enumerate(((t_qt, 0), (t_kt, 16))):
            muls = []
            for ct in range(2):
                dpm = list(ktq_all)
                if h == 0 and side == 0 and ct == 0:
                    dpm += [qmm_last[(NLB - 1, 7)], qmm_last[(NLB - 2, 7)]]
                    dpm += p2_tail
                if h >= 1:
                    dpm.append(avdone[gK0 - 2])
                if side == 1:
                    dpm.append(sq_mm_last)
                muls.append(S.op("ve", lambda e, src2=src2, ct=ct: e.tensor_mul(
                    t_sq[:, ct * L:(ct + 1) * L], src2[:, ct * L:(ct + 1) * L],
                    src2[:, ct * L:(ct + 1) * L]), deps=dpm))
            nch = L // 512
            pan = PA[side]
            reds = []
            for ch in range(nch):
                lo = (ch % 2) * 512
                mm = None
                for ct in range(2):
                    first = (ct == 0)
                    dpm = muls + [gload]
                    if first and ch >= 2:
                        dpm.append(reds[ch - 2])
                    if first and ch < 2 and h >= 1:
                        dpm.append(explast[gK0 - 1])
                    mm = S.op("te", lambda e, ch=ch, ct=ct, pan=pan, lo=lo: e.matmul(
                        pan[:, lo:lo + 512], p_ones128[:],
                        t_sq[:, ct * L + ch * 512:ct * L + (ch + 1) * 512],
                        start=(ct == 0), stop=(ct == 1)),
                        deps=(dpm if first else [mm]))
                sq_mm_last = mm
                reds.append(S.op("ve", lambda e, ch=ch, base=base, pan=pan, lo=lo: e.reduce_max(
                    t_mx[:, base + ch:base + ch + 1], pan[:, lo:lo + 512], axis=AX.X),
                    deps=[mm], hazard=True))
            norm_red_last[side] = reds[-1]
            fms.append(S.op("ve", lambda e, base=base, side=side, nch=nch: e.reduce_max(
                t_mx[:, 32 + side:33 + side], t_mx[:, base:base + nch], axis=AX.X),
                deps=reds, hazard=True))
        prod = S.op("ve", lambda e: e.tensor_mul(
            t_mx[:, 34:35], t_mx[:, 32:33], t_mx[:, 33:34]), deps=fms, hazard=True)
        lnc = S.op("ac", lambda e: e.activation(
            t_mx[:, 35:36], t_mx[:, 34:35], AF.Ln),
            deps=[prod] + (p2_tail if h == 0 else []), hazard=True)
        cexp = S.op("ac", lambda e: e.activation(
            t_mx[:, 36:37], t_mx[:, 35:36], AF.Exp, scale=0.5),
            deps=[lnc], hazard=True)
        negc = S.op("ve", lambda e, h=h: e.tensor_scalar(
            p_negC[:, h:h + 1], t_mx[:, 36:37], -1.0, None, ALU.mult),
            deps=[cexp], hazard=True)

        for K in range(nch2):
            lb, c = K // 2, K % 2
            gK = gK0 + K
            for gc in range(CG):
                g = c * CG + gc
                gg = len(exp_ops)
                pa = PA[gg % 2]
                if pend[0] is not None:
                    emit_av_group(gc, lds)
                smm = None
                sdeps = kt_done[(h, c)] + qt_done[(h, lb)]
                if gg >= 2:
                    sdeps.append(exp_ops[gg - 2])
                if K == 0 and gc < 2:
                    sdeps.append(norm_red_last[gg % 2])
                for kq in range(2):
                    kt = 2 * g + kq
                    for ct in range(2):
                        first = (kq == 0 and ct == 0)
                        smm = S.op("te", lambda e, lb=lb, kt=kt, ct=ct, kq=kq, pa=pa: e.matmul(
                            pa[:, kq * 512:(kq + 1) * 512],
                            t_kt[:, ct * L + kt * 128:ct * L + (kt + 1) * 128],
                            t_qt[:, ct * L + lb * 512:ct * L + (lb + 1) * 512],
                            start=(ct == 0), stop=(ct == 1)),
                            deps=(sdeps if first else [smm]))
                edp = [smm, negc]
                if (gK - 2, gc) in avof:
                    edp.append(avof[(gK - 2, gc)])
                if gc == 0 and K == 0:
                    edp.append(sq_mm_last)
                eo = S.op("ac", lambda e, h=h, gK=gK, gc=gc, pa=pa: e.activation(
                    pexp_ap(gK, gc), pa[:, 0:1024], AF.Exp,
                    bias=p_negC[:, h:h + 1]), deps=edp)
                exp_ops.append(eo)
                expof[(gK, gc)] = eo
            smm_of[(h, K)] = smm
            explast[gK] = exp_ops[-1]
            pend[0] = (h, K, gK, t_vs)
            if h == NHC - 1 and K == 0:
                kern_ldA = S.dma("gp", "g_kern", lambda e: e.dma_start(
                    out=t_kern[:, 0:KSPL], in_=s_kernb[:, 0:KSPL]),
                    deps=[head_avlast[NHC - 2], kern_pc])
        smm_head_last[h] = smm_of[(h, nch2 - 1)]
        head_explast[h] = explast[gK0 + nch2 - 1]
    for gc in range(CG):
        emit_av_group(gc, lds)
    p3_tail = [attn_dma[(NHC - 1, NQB - 1)][-1], attn_dma[(NHC - 1, NQB - 2)][-1],
               attn_ev[(NHC - 1, NQB - 1)][-1]]

    A.phase_reset()
    A.off = A.ranges[id(t_kern)][1]
    t_xf = [A.alloc("t_xf%d" % i, [128, NCT * 512], BF16) for i in range(2)]
    t_aau = A.alloc("t_aau", [128, NCT * 64], BF16)
    t_bau = A.alloc("t_bau", [64, FW], BF16)
    t_axv = [A.alloc("t_axv%d" % i, [64, 512], BF16) for i in range(2)]
    t_xr = A.alloc("t_xr", [128, 4 * FW], F32)
    t_at = A.alloc("t_at", [128, 4 * FW], F32)
    t_t1 = [A.alloc("t_t1%d" % i, [128, FW], F32) for i in range(2)]
    t_ys = [A.alloc("t_ys%d" % i, [128, FW], F32) for i in range(2)]

    p3_guards = [
        (A.ranges[id(t_kt)], head_explast[NHC - 1]),
        (A.ranges[id(t_qt)], head_explast[NHC - 1]),
        (A.ranges[id(t_vs2[0])], head_avlast.get(NHC - 2)),
        (A.ranges[id(t_vs2[1])], head_avlast[NHC - 1]),
        (A.ranges[id(t_sqpe)], head_avlast[NHC - 1]),
        (A.ranges[id(t_mx)], attn_ev[(NHC - 1, NQB - 1)][-1]),
        (A.ranges[id(t_sums)], attn_ev[(NHC - 1, NQB - 1)][-1]),
        (A.ranges[id(t_rec)], attn_ev[(NHC - 1, NQB - 1)][-1]),
        (A.ranges[id(t_sta[0])], attn_dma[(NHC - 1, NQB - 2)][-1]),
        (A.ranges[id(t_sta[1])], attn_dma[(NHC - 1, NQB - 1)][-1]),
    ]

    def p3_guard(t):
        lo, hi = A.ranges[id(t)]
        deps = []
        for (glo, ghi), dep in p3_guards:
            if glo < hi and lo < ghi and dep is not None:
                deps.append(dep)
        return deps

    kern_ld = S.dma("gp", "g_kern", lambda e: e.dma_start(
        out=t_kern[:, KSPL:NCT * FW], in_=s_kernb[:, KSPL:NCT * FW]),
        deps=p3_guard(t_kern))
    PER_G = NLB // 4
    xf_ld = {}
    aau_ld = {}
    bau_ld = {}
    at_ld = {}
    xr_ld = {}
    ffn_last = {}
    ax_mm_last = {}
    ax_ev = {}
    t1_op = {}
    y2_op = {}
    y3_op = {}
    ydma = {}
    for lblk in range(NLB):
        par = lblk % 2
        g4 = lblk // PER_G
        if lblk == 0:
            dp = p3_guard(t_xf[0])
        elif lblk == 1:
            dp = p3_guard(t_xf[1])
        else:
            dp = [ffn_last[(lblk - 2, 3)], ax_mm_last[lblk - 2]]
        xf_ld[lblk] = S.dma("sy", "s_xf%d" % par, lambda e, lblk=lblk, par=par: e.dma_start(
            out=t_xf[par][:], in_=s_xnt[lblk]), deps=dp + [xnt_st[lblk]])
        if lblk % PER_G == 0:
            adp = [ax_mm_last[lblk - 1]] if lblk >= 1 else p3_guard(t_aau)
            aau_ld[g4] = S.dma("sy", "s_aau", lambda e, g4=g4: e.dma_start(
                out=t_aau[:], in_=s_aaug[g4]), deps=adp + [ab_last])
            bdp = [ffn_last[(lblk - 1, 3)]] if lblk >= 1 else p3_guard(t_bau)
            bau_ld[g4] = S.dma("sy", "s_bau", lambda e, g4=g4: e.dma_start(
                out=t_bau[:], in_=s_baug[g4]), deps=bdp + [ab_last])
        atdp = ([y3_op[(lblk - 1, lt)] for lt in range(4)]
                if lblk >= 1 else p3_guard(t_at))
        at_ld[lblk] = S.dma("gp", "g_at", lambda e, lblk=lblk: e.dma_start(
            out=t_at[:].rearrange("p (l f) -> p l f", l=4),
            in_=s_att[lblk * 4:lblk * 4 + 4].rearrange("l p f -> p l f")),
            deps=atdp + [attn_dma[(NHC - 1, lblk)][-1]])
        xrdp = ([y2_op[(lblk - 1, lt)] for lt in range(4)]
                if lblk >= 1 else p3_guard(t_xr))
        xr_ld[lblk] = S.dma("sy", "s_xr", lambda e, lblk=lblk: e.dma_start(
            out=t_xr[:].rearrange("p (l f) -> p l f", l=4),
            in_=xres[lblk * 512:(lblk + 1) * 512, :].rearrange(
                "(l p) f -> p l f", p=128)), deps=xrdp)
        pax = PB[2 + par]
        mm = None
        for ct in range(NCT):
            first = (ct == 0)
            dpm = [xf_ld[lblk], aau_ld[g4]]
            if first and lblk < 2:
                dpm.append(attn_ev[(NHC - 1, NQB - 1)][-1])
            if first and lblk >= 2:
                dpm.append(ax_ev[lblk - 2])
            mm = S.op("te", lambda e, ct=ct, par=par, pax=pax: e.matmul(
                pax[0:64, 0:512], t_aau[:, ct * 64:(ct + 1) * 64],
                t_xf[par][:, ct * 512:(ct + 1) * 512],
                start=(ct == 0), stop=(ct == NCT - 1)),
                deps=(dpm if first else [mm]))
        ax_mm_last[lblk] = mm
        ax_ev[lblk] = S.op("ve", lambda e, par=par, pax=pax: e.tensor_copy(
            t_axv[par][:], pax[0:64, 0:512]), deps=[mm])
        for lt in range(4):
            pa = PA[lt % 2]
            mm = None
            for oc in range(2):
                for ct in range(NCT):
                    first = (oc == 0 and ct == 0)
                    dpm = [xf_ld[lblk], kern_ld]
                    if first:
                        key = (lblk, lt - 2) if lt >= 2 else (lblk - 1, lt + 2)
                        dpm.append(t1_op.get(key))
                    mm = S.op("te", lambda e, lt=lt, oc=oc, ct=ct, pa=pa, par=par: e.matmul(
                        pa[:, oc * 512:(oc + 1) * 512],
                        t_xf[par][:, ct * 512 + lt * 128:ct * 512 + (lt + 1) * 128],
                        t_kern[:, ct * FW + oc * 512:ct * FW + (oc + 1) * 512],
                        start=(ct == 0), stop=False),
                        deps=(dpm if first else [mm]))
            for oc in range(2):
                mm = S.op("te", lambda e, lt=lt, oc=oc, pa=pa, par=par: e.matmul(
                    pa[:, oc * 512:(oc + 1) * 512],
                    t_axv[par][:, lt * 128:(lt + 1) * 128],
                    t_bau[:, oc * 512:(oc + 1) * 512],
                    start=False, stop=True),
                    deps=[mm, ax_ev[lblk], bau_ld[g4]])
            ffn_last[(lblk, lt)] = mm
            key = (lblk, lt - 2) if lt >= 2 else (lblk - 1, lt + 2)
            t1_op[(lblk, lt)] = S.op("ve", lambda e, lt=lt, pa=pa: e.tensor_tensor(
                t_t1[lt % 2][:], pa[:, 0:1024], p_fbbc[:], ALU.add),
                deps=[mm, ev_fb, y2_op.get(key)])
            y2_op[(lblk, lt)] = S.op("gp", lambda e, lt=lt: e.tensor_tensor(
                t_ys[lt % 2][:], t_t1[lt % 2][:], t_xr[:, lt * FW:(lt + 1) * FW],
                ALU.add), deps=[t1_op[(lblk, lt)], xr_ld[lblk], ydma.get(key)])
            y3_op[(lblk, lt)] = S.op("ve", lambda e, lt=lt: e.tensor_tensor(
                t_ys[lt % 2][:], t_ys[lt % 2][:], t_at[:, lt * FW:(lt + 1) * FW],
                ALU.add), deps=[y2_op[(lblk, lt)], at_ld[lblk]])
            ydma[(lblk, lt)] = S.dma("gp", "g_y%d" % (lt % 2),
                                     lambda e, lblk=lblk, lt=lt: e.dma_start(
                out=y_out[lblk * 512 + lt * 128:lblk * 512 + (lt + 1) * 128, :],
                in_=t_ys[lt % 2][:]), deps=[y3_op[(lblk, lt)]])

    S.emit(nc, ent)
    return nc


def _prep_core(inputs, b, hg, L):
    F0 = hg * FW
    x = inputs["L0_attn_x"]
    Wq = inputs["L0_attn_Wq"]
    Wk = inputs["L0_attn_Wk"]
    Wv = inputs["L0_attn_Wv"]
    kernw = inputs["L0_ffn_kernel"]

    def tile_pm(a):
        W = a.shape[1]
        return np.ascontiguousarray(
            a.reshape(32, 128, W).transpose(1, 0, 2).reshape(128, 32 * W))

    m = {}
    m["xbT"] = np.ascontiguousarray(
        x[b, :L].T.reshape(32, 128, L).transpose(1, 0, 2).reshape(128, 32 * L))
    m["xres"] = np.ascontiguousarray(x[b, :L, F0:F0 + FW])
    m["wkT"] = tile_pm(np.ascontiguousarray(Wk[F0:F0 + FW].T))
    m["wvT"] = tile_pm(np.ascontiguousarray(Wv[F0:F0 + FW].T))
    m["kern"] = tile_pm(np.ascontiguousarray(kernw[:, F0:F0 + FW]))
    m["wq"] = np.ascontiguousarray(
        Wq[F0:F0 + FW].reshape(8, 128, D).transpose(1, 0, 2).reshape(128, 8 * D))
    m["tcd"] = np.ascontiguousarray(
        _host_tcd().reshape(4, 2, 128, 256).transpose(2, 0, 1, 3).reshape(128, 8 * 256))
    m["i128"] = np.eye(128, dtype=np.float32)
    m["ones128"] = np.ones((128, 128), dtype=ml_dtypes.bfloat16)
    m["onesf"] = np.ones((1, 128), dtype=np.float32)
    m["i32t"] = np.tile(np.eye(32, dtype=np.float32), (4, 1))
    pnm = np.zeros((16, 512), dtype=np.float32)
    for t in range(4):
        for p in range(128):
            pnm[(t * 128 + p) // 32, t * 128 + p] = 1.0
    m["pn"] = pnm
    m["gateT"] = np.ascontiguousarray(inputs["L0_ffn_gate"][4 * b:4 * b + 4].T)
    m["bwrow"] = np.ascontiguousarray(
        inputs["L0_attn_bias_weight"][4 * hg:4 * hg + 4].reshape(1, 12))
    m["bqc"] = np.ascontiguousarray(
        inputs["L0_attn_bq"][F0:F0 + FW].reshape(8, 128).T)
    m["bkc"] = np.ascontiguousarray(
        inputs["L0_attn_bk"][F0:F0 + FW].reshape(8, 128).T)
    m["bvrow"] = np.ascontiguousarray(
        inputs["L0_attn_bv"][F0:F0 + FW].reshape(1, FW))
    m["fbrow"] = np.ascontiguousarray(
        inputs["L0_ffn_bias"][F0:F0 + FW].reshape(1, FW))
    lA = inputs["L0_ffn_lora_A"]
    m["lAnr"] = np.ascontiguousarray(
        lA.reshape(512, D).reshape(4, 128, D).transpose(1, 0, 2).reshape(128, 4 * D))
    lB = inputs["L0_ffn_lora_B"][:, F0:F0 + FW, :]
    lbnr = lB.transpose(0, 2, 1).reshape(512, FW)
    m["lBnr"] = np.ascontiguousarray(
        lbnr.reshape(4, 128, FW).transpose(1, 0, 2).reshape(128, 4 * FW))
    lAs = inputs["L0_ffn_lora_A_shared"]
    m["lAsT"] = np.ascontiguousarray(
        lAs.T.reshape(32, 128, 32).transpose(1, 0, 2).reshape(128, 32 * 32))
    m["lBsT"] = np.ascontiguousarray(inputs["L0_ffn_lora_B_shared"][F0:F0 + FW].T)
    for k in m:
        if m[k].dtype == np.float64:
            m[k] = m[k].astype(np.float32)
    return m


def kernel(**inputs):
    L = int(inputs["L0_attn_x"].shape[1])
    nc = build_program(L)
    in_maps = [_prep_core(inputs, c // 4, c % 4, L) for c in range(8)]
    res = run_bass_kernel_spmd(nc, in_maps, core_ids=list(range(8)))
    y = np.empty((2, L, D), dtype=np.float32)
    for c in range(8):
        y[c // 4, :, (c % 4) * FW:(c % 4 + 1) * FW] = res.results[c]["y"]
    return y


# version 27
# speedup vs baseline: 1.1185x; 1.0070x over previous
import numpy as np
import ml_dtypes
import concourse.bass as bass
import concourse.mybir as mybir
from concourse.bass_utils import run_bass_kernel_spmd

F32 = mybir.dt.float32
BF16 = mybir.dt.bfloat16
AF = mybir.ActivationFunctionType
ALU = mybir.AluOpType
AX = mybir.AxisListType

D = 4096
NCT = 32
NHC = 4
FW = 1024
EPS = 1e-5
SB_BASE = 16544


def _host_tcd():
    n = 8
    masks = sorted(range(n), key=lambda m: (bin(m).count('1'), m))
    inv = {m: i for i, m in enumerate(masks)}
    C = np.zeros((n, n, n))
    for i, a in enumerate(masks):
        for j, bb in enumerate(masks):
            sw, at = (0, a >> 1)
            while at:
                sw += bin(at & bb).count('1')
                at >>= 1
            C[i, j, inv[a ^ bb]] = -1.0 if sw & 1 else 1.0
    grades = np.array([bin(m).count('1') for m in masks])
    rev = np.where(grades * (grades - 1) // 2 % 2 == 1, -1.0, 1.0)
    bivk = C[:, :, [4, 5, 6]]
    TCD = np.zeros((4, 256, 256), dtype=np.float32)
    TCD[0] = np.kron(np.eye(32), np.diag(rev) / 16.0)
    for c in range(3):
        TCD[c + 1] = np.kron(np.eye(32), rev[:, None] * bivk[:, :, c])
    return TCD


ENGS = ("te", "ve", "ac", "sy", "gp")
SEM_OF = {"te": "TE", "ve": "VE", "ac": "AC", "sy": "SY", "gp": "GP"}


class Op:
    __slots__ = ("eng", "fn", "deps", "hazard", "dma", "stream", "pos", "val",
                 "needs_inc", "waits")

    def __init__(self, eng, fn, deps, hazard, dma, stream, pos):
        self.eng, self.fn, self.deps = eng, fn, deps
        self.hazard, self.dma, self.stream, self.pos = hazard, dma, stream, pos
        self.val = 0
        self.needs_inc = dma
        self.waits = []


class Sched:
    def __init__(self):
        self.ops = {e: [] for e in ENGS}
        self.streams = {}

    def op(self, eng, fn, deps=(), hazard=False):
        deps = [d for d in deps if d is not None]
        o = Op(eng, fn, deps, hazard, False, None, len(self.ops[eng]))
        self.ops[eng].append(o)
        return o

    def dma(self, eng, stream, fn, deps=()):
        deps = [d for d in deps if d is not None]
        o = Op(eng, fn, deps, False, True, stream, len(self.ops[eng]))
        self.ops[eng].append(o)
        self.streams.setdefault(stream, []).append(o)
        return o

    def resolve(self):
        def skip_self(d, o):
            return d.eng == o.eng and d.eng == "te" and not d.hazard

        for e in ENGS:
            for o in self.ops[e]:
                for d in o.deps:
                    if (not d.dma) and (not skip_self(d, o)):
                        d.needs_inc = True
        for e in ENGS:
            c = 0
            for o in self.ops[e]:
                if o.dma:
                    continue
                if o.needs_inc:
                    c += 1
                o.val = c
        for st, ops in self.streams.items():
            c = 0
            for o in ops:
                c += 16
                o.val = c
        for e in ENGS:
            seen = {}
            for o in self.ops[e]:
                want = {}
                for d in o.deps:
                    if d.dma:
                        s = "D:" + d.stream
                    elif skip_self(d, o):
                        assert d.pos < o.pos, (e, o.pos, d.pos)
                        continue
                    else:
                        s = SEM_OF[d.eng]
                    want[s] = max(want.get(s, 0), d.val)
                for s, v in sorted(want.items()):
                    if seen.get(s, 0) < v:
                        o.waits.append((s, v))
                        seen[s] = v

    def emit(self, nc, ent):
        self.resolve()
        sems = {}
        for s in set(SEM_OF.values()):
            sems[s] = ent(nc.semaphore(s))
        for st in self.streams:
            sems["D:" + st] = ent(nc.semaphore("D_" + st))
        with nc.Block() as block:
            hooks = {"te": block.tensor, "ve": block.vector, "ac": block.scalar,
                     "sy": block.sync, "gp": block.gpsimd}
            for e in ENGS:
                ops = self.ops[e]
                if not ops:
                    continue

                def body(eng, ops=ops, e=e):
                    for o in ops:
                        for s, v in o.waits:
                            eng.wait_ge(sems[s], v)
                        inst = o.fn(eng)
                        if o.dma:
                            inst.then_inc(sems["D:" + o.stream], 16)
                        elif o.needs_inc:
                            inst.then_inc(sems[SEM_OF[e]], 1)
                    tot = max((o.val for o in ops if not o.dma), default=0)
                    if tot > 0:
                        eng.wait_ge(sems[SEM_OF[e]], tot)
                    done = set()
                    for o in ops:
                        if o.dma and o.stream not in done:
                            done.add(o.stream)
                            eng.wait_ge(sems["D:" + o.stream],
                                        self.streams[o.stream][-1].val)
                hooks[e](body)


def build_program(L):
    assert L % 512 == 0
    NLB = L // 512
    NLT = L // 128
    NKT = L // 128
    NG = NKT // 2
    NQB = L // 512

    nc = bass.Bass("TRN2", target_bir_lowering=False, debug=False)
    ctx = []

    def ent(cm):
        v = cm.__enter__()
        ctx.append(cm)
        return v

    def din(name, shape, dt):
        return nc.dram_tensor(name, shape, dt, kind="ExternalInput")

    xbT = din("xbT", [128, NCT * L], F32)
    xres = din("xres", [L, FW], F32)
    wkT = din("wkT", [128, NCT * FW], F32)
    wvT = din("wvT", [128, NCT * FW], F32)
    kern = din("kern", [128, NCT * FW], F32)
    wq = din("wq", [128, 8 * D], F32)
    tcd = din("tcd", [128, 8 * 256], F32)
    i128 = din("i128", [128, 128], F32)
    ones128 = din("ones128", [128, 128], BF16)
    onesf = din("onesf", [1, 128], F32)
    i32t = din("i32t", [128, 32], F32)
    pn = din("pn", [16, 512], F32)
    gateT = din("gateT", [16, 4], F32)
    bwrow = din("bwrow", [1, 12], F32)
    bqc = din("bqc", [128, 8], F32)
    bkc = din("bkc", [128, 8], F32)
    bvrow = din("bvrow", [1, FW], F32)
    fbrow = din("fbrow", [1, FW], F32)
    lAnr = din("lAnr", [128, 4 * D], F32)
    lBnr = din("lBnr", [128, 4 * FW], F32)
    lAsT = din("lAsT", [128, 32 * 32], F32)
    lBsT = din("lBsT", [32, FW], F32)
    y_out = nc.dram_tensor("y", [L, FW], F32, kind="ExternalOutput")

    s_wqe = nc.dram_tensor("s_wqe", [128, NHC * NCT * 256], BF16)
    s_xnt = nc.dram_tensor("s_xnt", [NLB, 128, NCT * 512], BF16)
    s_kt = nc.dram_tensor("s_kt", [NHC, 2, 128, L], BF16)
    s_vv = nc.dram_tensor("s_vv", [NHC, NLT, 128, 256], BF16)
    s_qt = nc.dram_tensor("s_qt", [NHC, 2, 128, L], BF16)
    s_att = nc.dram_tensor("s_att", [NLT, 128, FW], BF16)
    s_aaug = nc.dram_tensor("s_aaug", [NHC, 128, NCT * 64], BF16)
    s_wvb = nc.dram_tensor("s_wvb", [128, NCT * FW], BF16)
    s_kernb = nc.dram_tensor("s_kernb", [128, NCT * FW], BF16)
    s_baug = nc.dram_tensor("s_baug", [NHC, 64, FW], BF16)

    PA = [ent(nc.psum_tensor("PA%d" % i, [128, 1024], F32)) for i in range(2)]
    PB = [ent(nc.psum_tensor("PB%d" % i, [128, 512], F32)) for i in range(4)]

    class Arena:
        def __init__(self):
            self.off = SB_BASE
            self.persist_end = None
            self.ranges = {}

        def alloc(self, name, shape, dt):
            per = int(np.prod(shape[1:])) * mybir.dt.size(dt)
            per = (per + 31) & ~31
            t = nc.alloc_sbuf_tensor_at(name, list(shape), dt, offset=self.off)
            self.ranges[id(t)] = (self.off, self.off + per)
            self.off += per
            assert self.off <= SB_BASE + 212800, (name, self.off)
            return t

        def mark_persist(self):
            self.persist_end = self.off

        def phase_reset(self):
            self.off = self.persist_end

    A = Arena()
    p_ones128 = A.alloc("p_ones128", [128, 128], BF16)
    p_onesf = A.alloc("p_onesf", [1, 128], F32)
    p_bqc = A.alloc("p_bqc", [128, 8], F32)
    p_bkc = A.alloc("p_bkc", [128, 8], F32)
    p_bqe = A.alloc("p_bqe", [128, 8], F32)
    p_bvbc = A.alloc("p_bvbc", [128, FW], F32)
    p_fbbc = A.alloc("p_fbbc", [128, FW], F32)
    p_negC = A.alloc("p_negC", [128, NHC], F32)
    A.mark_persist()

    S = Sched()

    A.phase_reset()
    t_tcd = A.alloc("t_tcd", [128, 8 * 256], F32)
    t_i32t = A.alloc("t_i32t", [128, 32], F32)
    t_pn = A.alloc("t_pn", [16, 512], F32)
    t_gateT = A.alloc("t_gateT", [16, 4], F32)
    t_bwrow = A.alloc("t_bwrow", [1, 12], F32)
    t_bvrow = A.alloc("t_bvrow", [1, FW], F32)
    t_fbrow = A.alloc("t_fbrow", [1, FW], F32)
    t_bwbc = A.alloc("t_bwbc", [128, 12], F32)
    t_gc = A.alloc("t_gc", [128, 16], F32)
    t_ttf = A.alloc("t_ttf", [128, 8 * 256], F32)
    t_ttb = A.alloc("t_ttb", [128, 8 * 256], BF16)
    t_ssb = A.alloc("t_ssb", [128, 16 * 32], BF16)
    t_lA = A.alloc("t_lA", [128, 4 * D], BF16)
    t_lB = A.alloc("t_lB", [128, 4 * FW], BF16)
    t_lAsT = A.alloc("t_lAsT", [128, 32 * 32], BF16)
    t_wqs = [A.alloc("t_wqs%d" % i, [128, 2 * D], BF16) for i in range(2)]
    t_wqeh = [A.alloc("t_wqeh%d" % i, [128, NCT * 256], BF16) for i in range(2)]
    t_aaug = A.alloc("t_aaug", [128, NCT * 64], BF16)
    t_baug = A.alloc("t_baug", [64, FW], BF16)

    for src, dst in ((ones128, p_ones128), (lAnr, t_lA),
                     (lBnr, t_lB), (lAsT, t_lAsT)):
        S.dma("gp", "g_once", lambda e, s=src, d=dst: e.dma_start(out=d[:], in_=s[:]))
    gload = S.dma("gp", "g_once",
                  lambda e: e.dma_start(out=t_baug[32:64, :], in_=lBsT[:]))
    wv_pc = S.dma("gp", "g_pc0", lambda e: e.dma_start(out=s_wvb[:], in_=wvT[:]))
    kern_pc = S.dma("gp", "g_pc1", lambda e: e.dma_start(out=s_kernb[:], in_=kern[:]))
    for src, dst in ((tcd, t_tcd), (i32t, t_i32t), (pn, t_pn), (gateT, t_gateT),
                     (bwrow, t_bwrow), (onesf, p_onesf), (bqc, p_bqc),
                     (bkc, p_bkc), (bvrow, t_bvrow)):
        S.dma("sy", "s_once", lambda e, s=src, d=dst: e.dma_start(out=d[:], in_=s[:]))
    sload = S.dma("sy", "s_once",
                  lambda e: e.dma_start(out=t_fbrow[:], in_=fbrow[:]))

    bw_mm = S.op("te", lambda e: e.matmul(PB[0][:, 0:12], p_onesf[:], t_bwrow[:],
                                          start=True, stop=True), deps=[sload])
    bv_mm = [S.op("te", lambda e, c0=c0: e.matmul(
        PA[0][:, c0:c0 + 512], p_onesf[:], t_bvrow[:, c0:c0 + 512],
        start=True, stop=True), deps=[sload]) for c0 in (0, 512)]
    fb_mm = [S.op("te", lambda e, c0=c0: e.matmul(
        PA[1][:, c0:c0 + 512], p_onesf[:], t_fbrow[:, c0:c0 + 512],
        start=True, stop=True), deps=[sload]) for c0 in (0, 512)]
    gc_mm = []
    for g4 in range(4):
        for tt in range(4):
            c = g4 * 4 + tt
            gc_mm.append(S.op("te", lambda e, c=c, g4=g4, tt=tt: e.matmul(
                PB[1][:, c:c + 1], t_pn[:, tt * 128:(tt + 1) * 128],
                t_gateT[:, g4:g4 + 1], start=True, stop=True), deps=[sload]))
    ev_bw = S.op("ve", lambda e: e.tensor_copy(t_bwbc[:], PB[0][:, 0:12]), deps=[bw_mm])
    ev_bv = S.op("ve", lambda e: e.tensor_copy(p_bvbc[:], PA[0][:, 0:1024]), deps=bv_mm)
    ev_fb = S.op("ve", lambda e: e.tensor_copy(p_fbbc[:], PA[1][:, 0:1024]), deps=fb_mm)
    ev_gc = S.op("ve", lambda e: e.tensor_copy(t_gc[:], PB[1][:, 0:16]), deps=gc_mm)

    s_tiles = []
    for c in range(16):
        s_tiles.append(S.op("ve", lambda e, c=c: e.tensor_scalar(
            t_ssb[:, c * 32:(c + 1) * 32], t_i32t[:], t_gc[:, c:c + 1], None,
            ALU.mult), deps=[ev_gc, sload]))

    tt_done = []
    for h in range(NHC):
        for st in range(2):
            o = (h * 2 + st) * 256
            tds = t_tcd[:, st * 256:(st + 1) * 256]
            tc0 = t_tcd[:, (2 + st) * 256:(3 + st) * 256]
            tc1 = t_tcd[:, (4 + st) * 256:(5 + st) * 256]
            tc2 = t_tcd[:, (6 + st) * 256:(7 + st) * 256]
            d1 = S.op("ve", lambda e, o=o, h=h, tc0=tc0, tds=tds: e.scalar_tensor_tensor(
                t_ttf[:, o:o + 256], tc0, t_bwbc[:, h * 3:h * 3 + 1], tds,
                ALU.mult, ALU.add), deps=[sload, ev_bw], hazard=True)
            d2 = S.op("ve", lambda e, o=o, h=h, tc1=tc1: e.scalar_tensor_tensor(
                t_ttf[:, o:o + 256], tc1, t_bwbc[:, h * 3 + 1:h * 3 + 2],
                t_ttf[:, o:o + 256], ALU.mult, ALU.add), deps=[d1], hazard=True)
            d3 = S.op("ve", lambda e, o=o, h=h, tc2=tc2: e.scalar_tensor_tensor(
                t_ttf[:, o:o + 256], tc2, t_bwbc[:, h * 3 + 2:h * 3 + 3],
                t_ttf[:, o:o + 256], ALU.mult, ALU.add), deps=[d2], hazard=True)
            tt_done.append(d3)
    tt_cast = S.op("ve", lambda e: e.tensor_copy(t_ttb[:], t_ttf[:]), deps=tt_done,
                   hazard=True)

    bqe_mm = []
    for h in range(NHC):
        for rs in range(2):
            col = h * 2 + rs
            mm = None
            for st in range(2):
                mm = S.op("te", lambda e, h=h, rs=rs, st=st, col=col: e.matmul(
                    PB[2][:, col:col + 1],
                    t_ttf[:, (h * 2 + st) * 256 + rs * 128:
                          (h * 2 + st) * 256 + rs * 128 + 128],
                    p_bqc[:, h * 2 + st:h * 2 + st + 1],
                    start=(st == 0), stop=(st == 1)),
                    deps=(tt_done + [sload] if st == 0 else [mm]))
            bqe_mm.append(mm)
    ev_bqe = S.op("ve", lambda e: e.tensor_copy(p_bqe[:], PB[2][:, 0:8]), deps=bqe_mm)

    ab_last = None
    st_a_prev = None
    for g4 in range(4):
        ae_ev = {}
        for isl in range(32):
            mm = None
            for tt in range(4):
                first = (tt == 0)
                dpm = [gload, s_tiles[g4 * 4 + tt]]
                if first and isl >= 8:
                    dpm.append(ae_ev[isl - 8])
                mm = S.op("te", lambda e, g4=g4, isl=isl, tt=tt: e.matmul(
                    PB[3][:, (isl % 8) * 32:(isl % 8) * 32 + 32],
                    t_lA[:, tt * D + isl * 128:tt * D + (isl + 1) * 128],
                    t_ssb[:, (g4 * 4 + tt) * 32:(g4 * 4 + tt + 1) * 32],
                    start=(tt == 0), stop=(tt == 3)),
                    deps=(dpm if first else [mm]))
            if isl % 8 == 7:
                ev = S.op("ve", lambda e, isl=isl: e.tensor_copy(
                    t_aaug[:].rearrange("p (i r) -> p i r", i=NCT)
                    [:, isl - 7:isl + 1, 0:32],
                    PB[3][:, 0:256].rearrange("p (i r) -> p i r", i=8)),
                    deps=[mm] + ([st_a_prev] if isl == 7 else []))
                for k in range(isl - 7, isl + 1):
                    ae_ev[k] = ev
        sh_cp = S.op("ve", lambda e: e.tensor_copy(
            t_aaug[:].rearrange("p (i r) -> p i r", i=NCT)[:, :, 32:64],
            t_lAsT[:].rearrange("p (i r) -> p i r", i=NCT)),
            deps=[gload] + ([ab_last] if ab_last else []))
        st_a = S.dma("gp", "g_ab", lambda e, g4=g4: e.dma_start(
            out=s_aaug[g4], in_=t_aaug[:]), deps=[ae_ev[31], sh_cp])
        st_a_prev = st_a
        be_ev = []
        for oc in range(2):
            mm = None
            for tt in range(4):
                first = (tt == 0)
                dpm = [gload, s_tiles[g4 * 4 + tt], ev_bqe]
                if first and be_ev:
                    dpm.append(be_ev[-1])
                mm = S.op("te", lambda e, g4=g4, oc=oc, tt=tt: e.matmul(
                    PB[2][0:32, 0:512],
                    t_ssb[:, (g4 * 4 + tt) * 32:(g4 * 4 + tt + 1) * 32],
                    t_lB[:, tt * FW + oc * 512:tt * FW + oc * 512 + 512],
                    start=(tt == 0), stop=(tt == 3)),
                    deps=(dpm if first else [mm]))
            be_ev.append(S.op("ve", lambda e, oc=oc: e.tensor_copy(
                t_baug[0:32, oc * 512:oc * 512 + 512], PB[2][0:32, 0:512]),
                deps=[mm]))
        ab_last = S.dma("gp", "g_ab", lambda e, g4=g4: e.dma_start(
            out=s_baug[g4], in_=t_baug[:]), deps=be_ev + [gload, st_a])

    wq_ld = []
    for h in range(NHC):
        wq_ld.append(S.dma("gp", "g_wq%d" % (h % 2), lambda e, h=h: e.dma_start(
            out=t_wqs[h % 2][:], in_=wq[:, h * 2 * D:(h + 1) * 2 * D])))
    fold_done = []
    prev_evs_last = None
    for h in range(NHC):
        evs = []
        for ct in range(NCT):
            slot = ct % 4
            pa = PA[0] if slot < 2 else PA[1]
            lo = (slot % 2) * 512
            mm = None
            for st in range(2):
                first = (st == 0)
                dpm = [wq_ld[h], tt_cast]
                if first and ct < 4:
                    dpm += [ev_bv, ev_fb, prev_evs_last]
                if first and ct >= 4:
                    dpm.append(evs[ct - 4])
                if first and ct == 0 and h >= 2:
                    dpm.append(fold_done[h - 2])
                mm = S.op("te", lambda e, h=h, ct=ct, st=st, pa=pa, lo=lo: e.matmul(
                    pa[:, lo:lo + 256],
                    t_wqs[h % 2][:, st * D + ct * 128:st * D + (ct + 1) * 128],
                    t_ttb[:, (h * 2 + st) * 256:(h * 2 + st + 1) * 256],
                    start=(st == 0), stop=(st == 1)),
                    deps=(dpm if first else [mm]))
            evs.append(S.op("ve", lambda e, h=h, ct=ct, pa=pa, lo=lo: e.tensor_copy(
                t_wqeh[h % 2][:, ct * 256:(ct + 1) * 256], pa[:, lo:lo + 256]),
                deps=[mm]))
        stw = S.dma("sy", "s_wqe%d" % (h % 2), lambda e, h=h: e.dma_start(
            out=s_wqe[:, h * NCT * 256:(h + 1) * NCT * 256], in_=t_wqeh[h % 2][:]),
            deps=[evs[-1]])
        fold_done.append(stw)
        prev_evs_last = evs[-1]
        if h + 2 < NHC:
            wq_ld[h + 2].deps.append(evs[-1])

    p0_done = [ab_last, fold_done[-1], fold_done[-2], ev_bv, ev_fb, ev_bqe,
               gload, sload]

    A.phase_reset()
    t_wkT = A.alloc("t_wkT", [128, NCT * FW], BF16)
    t_xnt = [A.alloc("t_xnt%d" % i, [128, NCT * 512], BF16) for i in range(2)]
    t_xsq = A.alloc("t_xsq", [128, 2 * 512], BF16)
    t_red = A.alloc("t_red", [128, 4 * 512], BF16)
    t_stk = A.alloc("t_stk0", [128, 2 * 512], BF16)
    t_tmp = A.alloc("t_tmp", [128, 512], F32)
    _rw = [A.alloc("t_rw%d" % i, [1, 512], F32) for i in range(6)]
    t_nm = [_rw[0][:], _rw[1][:]]
    t_rr = [_rw[2][:], _rw[3][:]]
    sc0, sc1 = (_rw[4][:], _rw[5][:])
    t_xtb = [A.alloc("t_xtb%d" % i, [128, 8 * 512], BF16) for i in range(6)]

    wk_ld = S.dma("gp", "g_wk", lambda e: e.dma_start(out=t_wkT[:], in_=wkT[:]),
                  deps=p0_done)
    xbTv = xbT[:].rearrange("p (c l) -> p c l", c=NCT)

    xload = {}
    sx_mm = {}
    ssq_mm = {}
    rowe1 = {}
    rowe2 = {}
    kmm_last = {}
    kevac = {}
    kdma = {}
    xnt_st = {}
    tt_last = {}
    qnorm = {}
    bc_ops = {}
    prev_ssq = [None]

    def p1_xload(l, q, dps):
        xload[(l, q)] = S.dma("gp", "g_xt%d" % (q % 2),
                              lambda e, l=l, q=q: e.dma_start(
            out=t_xtb[(4 * l + q) % 6][:],
            in_=xbTv[:, q * 8:(q + 1) * 8, l * 512:(l + 1) * 512]), deps=dps)

    rch = lambda q, k: t_red[:, (2 * (q % 2) + k) * 512:
                             (2 * (q % 2) + k + 1) * 512]
    xadd_done = {}
    qadd_done = {}
    xsq_user = [None, None]

    def ve_tt(o, a, b, dps):
        return S.op("ve", lambda e, o=o, a=a, b=b: e.tensor_tensor(
            o, a, b, ALU.add), deps=dps, hazard=True)

    def p1_xadds(l, q):
        slot = t_xtb[(4 * l + q) % 6]
        cks = [slot[:, c * 512:(c + 1) * 512] for c in range(8)]
        r = rch(q, 0)
        dp0 = [xload[(l, q)],
               sx_mm.get((l, q - 2) if q >= 2 else (l - 1, q + 2))]
        if l == 0 and q == 0:
            dp0 += [fold_done[-1]]
        o = ve_tt(r, cks[0], cks[1], dp0)
        for c2 in range(1, 4):
            o2 = ve_tt(t_tmp[:], cks[2 * c2], cks[2 * c2 + 1], [o])
            o = ve_tt(r, r, t_tmp[:], [o2])
        xadd_done[(l, q)] = o

    def p1_sqgrp(l, q):
        slot = t_xtb[(4 * l + q) % 6]
        r = rch(q, 1)

        def mk_sq(c, dps):
            return S.op("ve", lambda e, slot=slot, c=c: e.tensor_mul(
                t_xsq[:, (c % 2) * 512:(c % 2 + 1) * 512],
                slot[:, c * 512:(c + 1) * 512],
                slot[:, c * 512:(c + 1) * 512]), deps=dps, hazard=True)
        sq0 = mk_sq(0, [xload[(l, q)]] +
                    ([fold_done[-1]] if l == 0 and q == 0 else []))
        sq1 = mk_sq(1, [sq0])
        o = ve_tt(r, t_xsq[:, 0:512], t_xsq[:, 512:1024],
                  [sq1, ssq_mm.get((l, q - 2) if q >= 2 else (l - 1, q + 2))])
        for j in range(1, 4):
            sqa = mk_sq(2 * j, [o])
            o = ve_tt(r, r, t_xsq[:, 0:512], [sqa])
            sqb = mk_sq(2 * j + 1, [o])
            o = ve_tt(r, r, t_xsq[:, 512:1024], [sqb])
        qadd_done[(l, q)] = o

    def p1_mmx(l, q):
        p = l % 2
        dpx = [xadd_done[(l, q)]]
        if q == 0:
            dpx += [rowe1[l - 2]] if l >= 2 else [ev_bqe, ab_last]
        sx_mm[(l, q)] = S.op("te", lambda e, p=p, q=q: e.matmul(
            PB[2][32 * p:32 * p + 1, 0:512], p_ones128[:, 0:1], rch(q, 0),
            start=(q == 0), stop=(q == 3)), deps=dpx)

    def p1_mmq(l, q):
        p = l % 2
        dpq = [qadd_done[(l, q)]]
        if q == 0:
            dpq += [rowe2[l - 2]] if l >= 2 else [ev_bqe, ab_last]
        ssq_mm[(l, q)] = S.op("te", lambda e, p=p, q=q: e.matmul(
            PB[3][32 * p:32 * p + 1, 0:512], p_ones128[:, 0:1], rch(q, 1),
            start=(q == 0), stop=(q == 3)), deps=dpq)

    prev_rm = [None]

    def p1_rowmath(l):
        p = l % 2
        e1 = S.op("ve", lambda e, p=p: e.tensor_copy(
            sc0, PB[2][32 * p:32 * p + 1, 0:512]),
            deps=[sx_mm[(l, 3)], bc_ops.get(l - 2), prev_rm[0]])
        e2 = S.op("ve", lambda e, p=p: e.tensor_copy(
            sc1, PB[3][32 * p:32 * p + 1, 0:512]),
            deps=[ssq_mm[(l, 3)]])
        s1 = S.op("ve", lambda e, p=p: e.tensor_scalar_mul(
            t_nm[p], sc0, -1.0 / D),
            deps=[e1, bc_ops.get(l - 2)], hazard=True)
        m2 = S.op("ve", lambda e, p=p: e.tensor_mul(
            sc0, t_nm[p], t_nm[p]),
            deps=[s1], hazard=True)
        me = S.op("ve", lambda e: e.tensor_scalar_add(
            sc0, sc0, -EPS), deps=[m2], hazard=True)
        vpe = S.op("ve", lambda e: e.scalar_tensor_tensor(
            sc1, sc1, 1.0 / D, sc0, ALU.mult, ALU.subtract),
            deps=[e2, me], hazard=True)
        lnv = S.op("ac", lambda e: e.activation(
            sc0, sc1, AF.Ln),
            deps=[vpe], hazard=True)
        rst = S.op("ac", lambda e, p=p: e.activation(
            t_rr[p], sc0, AF.Exp, scale=-0.5),
            deps=[lnv, bc_ops.get(l - 2)], hazard=True)
        rowe1[l] = e1
        rowe2[l] = e2
        prev_rm[0] = rst
        return s1, rst

    def p1_bc(l, s1, rst):
        p = l % 2
        b1 = S.op("te", lambda e, p=p: e.matmul(
            PA[p][:, 0:512], p_onesf[:], t_nm[p],
            start=True, stop=True),
            deps=[s1] + ([tt_last[l - 2]] if l >= 2 else list(p0_done)))
        b2 = S.op("te", lambda e, p=p: e.matmul(
            PA[p][:, 512:1024], p_onesf[:], t_rr[p],
            start=True, stop=True), deps=[b1, rst])
        bc_ops[l] = b2

    def p1_norm(l):
        p = l % 2
        prev = tt_last.get(l - 1)
        for qq in range(4):
            slot = t_xtb[(4 * l + qq) % 6]
            for c8 in range(8):
                ct = qq * 8 + c8
                d1 = [xload[(l, qq)], bc_ops[l]]
                if l >= 2 and qq == 0 and c8 == 0:
                    d1 += [kmm_last[(l - 2, 7)], xnt_st[l - 2]]
                if prev is not None:
                    d1.append(prev)
                o1 = S.op("ve", lambda e, c8=c8, slot=slot: e.tensor_tensor(
                    t_tmp[:], slot[:, c8 * 512:(c8 + 1) * 512],
                    PA[p][:, 0:512], ALU.add),
                    deps=d1, hazard=True)
                prev = S.op("ve", lambda e, ct=ct, p=p: e.tensor_tensor(
                    t_xnt[p][:, ct * 512:(ct + 1) * 512], t_tmp[:],
                    PA[p][:, 512:1024], ALU.mult),
                    deps=[o1], hazard=True)
            qnorm[(l, qq)] = prev
        tt_last[l] = prev
        xnt_st[l] = S.dma("sy", "s_xnt%d" % p,
                          lambda e, l=l, p=p: e.dma_start(
            out=s_xnt[l], in_=t_xnt[p][:]), deps=[prev])

    def p1_kt_ft(l, ft):
        bpar = l % 2
        pb = PB[ft % 2]
        mm = None
        for ct in range(NCT):
            if ft == 0 and ct % 8 == 0:
                dpm = [qnorm[(l, ct // 8)], wk_ld]
                if ct == 0 and l >= 1:
                    dpm.append(kevac[(l - 1, 6)])
            elif ct == 0:
                dpm = [wk_ld]
                if ft >= 2:
                    dpm.append(kevac[(l, ft - 2)])
                elif l >= 1:
                    dpm.append(kevac[(l - 1, 6 + ft)])
            else:
                dpm = [mm]
            mm = S.op("te", lambda e, ft=ft, ct=ct, pb=pb, bpar=bpar: e.matmul(
                pb[:, 0:512],
                t_wkT[:, ct * FW + ft * 128:ct * FW + (ft + 1) * 128],
                t_xnt[bpar][:, ct * 512:(ct + 1) * 512],
                start=(ct == 0), stop=(ct == NCT - 1)),
                deps=dpm)
        kmm_last[(l, ft)] = mm
        edp = [mm]
        if ft >= 2:
            edp.append(kdma[(l, ft - 2)])
        elif l >= 1:
            edp.append(kdma[(l - 1, ft + 6)])
        else:
            edp += [fold_done[-1], ab_last]
        kevac[(l, ft)] = S.op("ac", lambda e, ft=ft, pb=pb: e.activation(
            t_stk[:, (ft % 2) * 512:(ft % 2 + 1) * 512], pb[:, 0:512],
            AF.Identity, bias=p_bkc[:, ft:ft + 1]), deps=edp)
        kdma[(l, ft)] = S.dma("sy", "s_kd%d" % bpar,
                              lambda e, l=l, ft=ft: e.dma_start(
            out=s_kt[ft // 2, ft % 2, :, l * 512:(l + 1) * 512],
            in_=t_stk[:, (ft % 2) * 512:(ft % 2 + 1) * 512]),
            deps=[kevac[(l, ft)]])

    for q in range(4):
        p1_xload(0, q, [])
    for q in range(2):
        p1_xload(1, q, [])
    for qp in range(2):
        for q in (2 * qp, 2 * qp + 1):
            p1_xadds(0, q)
        for q in (2 * qp, 2 * qp + 1):
            p1_mmx(0, q)
    for qp in range(2):
        for q in (2 * qp, 2 * qp + 1):
            p1_sqgrp(0, q)
        for q in (2 * qp, 2 * qp + 1):
            p1_mmq(0, q)
    _s1, _rst = p1_rowmath(0)
    p1_bc(0, _s1, _rst)
    for q in range(2):
        p1_xadds(1, q)
        p1_mmx(1, q)
        p1_sqgrp(1, q)
    p1_norm(0)
    for l in range(NLB):
        if l >= 1:
            p1_norm(l)
        if l + 1 < NLB:
            p1_xload(l + 1, 2, [qnorm[(l, 0)]])
            p1_xload(l + 1, 3, [qnorm[(l, 1)]])
            p1_xadds(l + 1, 2)
            p1_xadds(l + 1, 3)
        if l + 2 < NLB:
            p1_xload(l + 2, 0, [qnorm[(l, 2)]])
            p1_xload(l + 2, 1, [qnorm[(l, 3)]])
        for ft in range(8):
            p1_kt_ft(l, ft)
            if l + 1 < NLB:
                if ft == 0:
                    p1_mmx(l + 1, 2)
                    p1_mmx(l + 1, 3)
                    p1_mmq(l + 1, 0)
                    p1_mmq(l + 1, 1)
                if ft == 1:
                    p1_sqgrp(l + 1, 2)
                if ft == 2:
                    p1_sqgrp(l + 1, 3)
                if ft == 3:
                    p1_mmq(l + 1, 2)
                    p1_mmq(l + 1, 3)
                    _s1, _rst = p1_rowmath(l + 1)
                    p1_bc(l + 1, _s1, _rst)
            if l + 2 < NLB:
                if ft == 4:
                    p1_xadds(l + 2, 0)
                    p1_xadds(l + 2, 1)
                if ft == 5:
                    p1_mmx(l + 2, 0)
                    p1_mmx(l + 2, 1)
                    p1_sqgrp(l + 2, 0)
                if ft == 6:
                    p1_sqgrp(l + 2, 1)
    p1_tail = [kdma[(NLB - 1, 7)], kdma[(NLB - 2, 7)],
               xnt_st[NLB - 1], xnt_st[NLB - 2]]

    A.phase_reset()
    t_wvT = A.alloc("t_wvT", [128, NCT * FW], BF16)
    t_xv = [A.alloc("t_xv%d" % i, [128, NCT * 512], BF16) for i in range(2)]
    t_stv = [A.alloc("t_stv%d" % i, [128, FW], BF16) for i in range(2)]

    lev = (NLB - 1) if (NLB - 1) % 2 == 0 else (NLB - 2)
    lod = (NLB - 1) if (NLB - 1) % 2 == 1 else (NLB - 2)
    p1_guards = [
        (A.ranges[id(t_wkT)], [kmm_last[(NLB - 1, 7)]]),
        (A.ranges[id(t_xnt[0])], [kmm_last[(lev, 7)], xnt_st[lev]]),
        (A.ranges[id(t_xnt[1])], [kmm_last[(lod, 7)], xnt_st[lod]]),
        ((A.ranges[id(t_xsq)][0], A.ranges[id(t_red)][1]),
         [ssq_mm[(NLB - 1, 3)]]),
        (A.ranges[id(t_stk)], [kdma[(NLB - 1, 7)], kdma[(NLB - 2, 7)]]),
        (A.ranges[id(t_tmp)], [tt_last[NLB - 1]]),
        ((A.ranges[id(_rw[0])][0], A.ranges[id(_rw[5])][1]),
         [bc_ops[NLB - 1], prev_rm[0]]),
    ] + [(A.ranges[id(t_xtb[i])], [tt_last[NLB - 1], tt_last[NLB - 2]])
         for i in range(6)]

    def p1_guard(t):
        lo, hi = A.ranges[id(t)]
        deps = []
        for (glo, ghi), ds2 in p1_guards:
            if glo < hi and lo < ghi:
                deps += ds2
        return deps

    P3_HEAD = 2 * (NLT * 257 * 2 + 31 & ~31) + 4 * L * 2 + 160 + 32 + 32 + \
        2 * 2048
    _sv = A.off
    A.off = A.persist_end + P3_HEAD
    t_kt = A.alloc("t_kt", [128, 2 * L], BF16)
    t_qt = A.alloc("t_qt", [128, 2 * L], BF16)
    p3_kt_end = A.off
    A.off = _sv
    kt_done = {}
    qt_done = {}
    LH = L // 2
    for ch in range(2):
        dpk = p1_guard(t_kt) + [kdma[(NLB - 1, 7)], kdma[(NLB - 2, 7)]] \
            if ch == 0 else []
        kt_done[(0, ch)] = [S.dma("ac", "a_k3",
                                  lambda e, rs=rs, ch=ch: e.dma_start(
            out=t_kt[:, rs * L + ch * LH:rs * L + (ch + 1) * LH],
            in_=s_kt[0, rs, :, ch * LH:(ch + 1) * LH]),
            deps=(dpk if rs == 0 else []))
            for rs in range(2)]

    wv_ld = S.dma("gp", "g_wv", lambda e: e.dma_start(out=t_wvT[:], in_=s_wvb[:]),
                  deps=p1_guard(t_wvT) + [wv_pc])
    xv_ld = {}
    vmm_last = {}
    vevac = {}
    vdma = {}
    for lblk in range(NLB):
        par = lblk % 2
        dp = (p1_guard(t_xv[par]) if lblk < 2 else [vmm_last[(lblk - 2, 3)]]) + \
            [xnt_st[lblk]]
        xv_ld[lblk] = S.dma("sy", "s_xv%d" % par, lambda e, lblk=lblk, par=par: e.dma_start(
            out=t_xv[par][:], in_=s_xnt[lblk]), deps=dp)
        for mt in range(4):
            pa = PA[mt % 2]
            mm = None
            for oc in range(2):
                for ct in range(NCT):
                    first = (oc == 0 and ct == 0)
                    dpm = [xv_ld[lblk], wv_ld]
                    if first:
                        key = (lblk, mt - 2) if mt >= 2 else (lblk - 1, mt + 2)
                        dpm.append(vevac.get(key))
                    mm = S.op("te", lambda e, mt=mt, oc=oc, ct=ct, pa=pa, par=par: e.matmul(
                        pa[:, oc * 512:(oc + 1) * 512],
                        t_xv[par][:, ct * 512 + mt * 128:ct * 512 + (mt + 1) * 128],
                        t_wvT[:, ct * FW + oc * 512:ct * FW + (oc + 1) * 512],
                        start=(ct == 0), stop=(ct == NCT - 1)),
                        deps=(dpm if first else [mm]))
            vmm_last[(lblk, mt)] = mm
            edp = [mm, ev_bv]
            key = (lblk, mt - 2) if mt >= 2 else (lblk - 1, mt + 2)
            if key in vdma:
                edp.append(vdma[key][-1])
            elif lblk == 0:
                edp += p1_guard(t_stv[mt % 2])
            vevac[(lblk, mt)] = S.op("ve", lambda e, mt=mt, pa=pa: e.tensor_tensor(
                t_stv[mt % 2][:], pa[:, 0:1024], p_bvbc[:], ALU.add), deps=edp)
            vdma[(lblk, mt)] = [S.dma("gp", "g_vd%d" % (mt % 2),
                                      lambda e, lblk=lblk, mt=mt, h=h: e.dma_start(
                out=s_vv[h, lblk * 4 + mt],
                in_=t_stv[mt % 2][:, h * 256:(h + 1) * 256]),
                deps=[vevac[(lblk, mt)]]) for h in range(NHC)]
    p2v_tail = [vdma[(NLB - 1, 2)][-1], vdma[(NLB - 1, 3)][-1]]

    A.phase_reset()
    t_wqe = A.alloc("t_wqe", [128, NHC * NCT * 256], BF16)
    t_xq = [A.alloc("t_xq%d" % i, [128, NCT * 512], BF16) for i in range(2)]
    t_stq = [A.alloc("t_stq%d" % i, [128, 512], BF16) for i in range(2)]

    def p2v_guard(t):
        lo, hi = A.ranges[id(t)]
        deps = []
        for tt2, ds2 in ((t_wvT, [vmm_last[(NLB - 1, 3)]]),
                         (t_xv[0], [vmm_last[(NLB - 1, 3)]]),
                         (t_xv[1], [vmm_last[(NLB - 1, 3)]]),
                         (t_stv[0], [vdma[(NLB - 1, 2)][-1]]),
                         (t_stv[1], [vdma[(NLB - 1, 3)][-1]])):
            glo, ghi = A.ranges[id(tt2)]
            if glo < hi and lo < ghi:
                deps += ds2
        if hi > A.ranges[id(t_stv[1])][1]:
            deps += p2v_tail
        return deps

    wqe_ld = S.dma("sy", "s_wqeld", lambda e: e.dma_start(out=t_wqe[:], in_=s_wqe[:]),
                   deps=p2v_guard(t_wqe) + [fold_done[-1], fold_done[-2]])
    xq_ld = {}
    qmm_last = {}
    qevac = {}
    qdma = {}
    for lblk in range(NLB):
        par = lblk % 2
        dp = (p2v_guard(t_xq[par]) if lblk < 2 else [qmm_last[(lblk - 2, 7)]]) + \
            [xnt_st[lblk]]
        xq_ld[lblk] = S.dma("sy", "s_xq%d" % par, lambda e, lblk=lblk, par=par: e.dma_start(
            out=t_xq[par][:], in_=s_xnt[lblk]), deps=dp)
        for rt in range(8):
            h, rs = rt // 2, rt % 2
            pb = PB[rt % 2]
            mm = None
            for ct in range(NCT):
                first = (ct == 0)
                dpm = [xq_ld[lblk], wqe_ld]
                if first:
                    key = (lblk, rt - 2) if rt >= 2 else (lblk - 1, rt + 6)
                    dpm.append(qevac.get(key))
                mm = S.op("te", lambda e, h=h, rs=rs, ct=ct, pb=pb, par=par: e.matmul(
                    pb[:, 0:512],
                    t_wqe[:, (h * NCT + ct) * 256 + rs * 128:
                          (h * NCT + ct) * 256 + (rs + 1) * 128],
                    t_xq[par][:, ct * 512:(ct + 1) * 512],
                    start=(ct == 0), stop=(ct == NCT - 1)),
                    deps=(dpm if first else [mm]))
            qmm_last[(lblk, rt)] = mm
            edp = [mm, ev_bqe]
            key = (lblk, rt - 2) if rt >= 2 else (lblk - 1, rt + 6)
            edp.append(qdma.get(key))
            if key not in qdma and lblk == 0:
                edp += p2v_guard(t_stq[rt % 2])
            qevac[(lblk, rt)] = S.op("ac", lambda e, rt=rt, pb=pb: e.activation(
                t_stq[rt % 2][:], pb[:, 0:512], AF.Identity,
                bias=p_bqe[:, rt:rt + 1]), deps=edp)
            qdma[(lblk, rt)] = S.dma("gp", "g_qd%d" % (rt % 2),
                                     lambda e, lblk=lblk, rt=rt: e.dma_start(
                out=s_qt[rt // 2, rt % 2, :, lblk * 512:(lblk + 1) * 512],
                in_=t_stq[rt % 2][:]), deps=[qevac[(lblk, rt)]])
        dq0 = [qdma[(lblk, 0)], qdma[(lblk, 1)]]
        if lblk == 0:
            dq0 = dq0 + p1_guard(t_qt)
        qt_done[(0, lblk)] = [
            S.dma("gp", "g_q3", lambda e, lblk=lblk: e.dma_start(
                out=t_qt[:, lblk * 512:(lblk + 1) * 512],
                in_=s_qt[0, 0, :, lblk * 512:(lblk + 1) * 512]), deps=dq0),
            S.dma("gp", "g_q3", lambda e, lblk=lblk: e.dma_start(
                out=t_qt[:, L + lblk * 512:L + (lblk + 1) * 512],
                in_=s_qt[0, 1, :, lblk * 512:(lblk + 1) * 512]), deps=dq0)]
    p2_tail = [qdma[(NLB - 1, 6)], qdma[(NLB - 1, 7)]]

    A.phase_reset()
    t_vs2 = [A.alloc("t_vs%d" % i, [128, NLT * 257], BF16) for i in range(2)]
    t_sqpe = A.alloc("t_sqpe", [128, 4 * L], BF16)
    t_sq = t_sqpe
    t_mx = A.alloc("t_mx", [128, 40], F32)
    t_sums = A.alloc("t_sums", [128, 4], F32)
    t_rec = A.alloc("t_rec", [128, 4], F32)
    t_sta = [A.alloc("t_sta%d" % i, [128, NHC * 256], BF16) for i in range(2)]
    q_ext = A.ranges[id(t_stq[1])][1]
    v_ext = A.ranges[id(t_stv[1])][1]
    assert A.off == A.ranges[id(t_kt)][0] >= max(q_ext, v_ext), (
        A.off, A.ranges[id(t_kt)], q_ext, v_ext)
    A.off = p3_kt_end
    _sv = A.off
    A.off = A.persist_end
    t_kern = A.alloc("t_kern", [128, NCT * FW], BF16)
    A.off = _sv
    KSPL = (NLT * 257 * 2 + 31 & ~31) // 2
    kern_ldA = None

    q_last = qmm_last[(NLB - 1, 7)]
    vs_memset = [S.op("ve", lambda e, i=i: e.memset(
        t_vs2[i][:].rearrange("p (m f) -> p m f", m=NLT)[:, :, 256:257], 1.0),
        deps=[q_last]) for i in range(2)]

    exp_ops = []
    attn_ev = {}
    attn_dma = {}
    head_explast = {}
    head_avlast = {}
    avdone = {}
    avof = {}
    expof = {}
    explast = {}
    pend = [None]

    CG = NG // 2
    nch2 = 2 * NQB

    def pexp_ap(gK, gc, sl=None):
        base = (gK % 2) * 2 * L + gc * 1024
        if sl is None:
            return t_sqpe[:, base:base + 1024]
        return t_sqpe[:, base + sl[0]:base + sl[1]]

    def emit_denoms(hp, lb, avl):
        cps = []
        for ls in range(4):
            cps.append(S.op("ve", lambda e, ls=ls: e.tensor_copy(
                t_sums[:, ls:ls + 1], PB[ls][:, 256:257]), deps=[avl]))
        rec = S.op("ve", lambda e: e.reciprocal(t_rec[:], t_sums[:]),
                   deps=cps, hazard=True)
        spar = lb % 2
        evs = []
        for ls in range(4):
            edp2 = [rec, avl]
            wkey = (hp, lb - 2) if lb >= 2 else (hp - 1, NQB - 2 + lb)
            if wkey in attn_dma:
                edp2.append(attn_dma[wkey][-1])
            evs.append(S.op("ve", lambda e, ls=ls, spar=spar: e.tensor_scalar_mul(
                t_sta[spar][:, ls * 256:(ls + 1) * 256],
                PB[ls][:, 0:256], t_rec[:, ls:ls + 1]),
                deps=edp2))
        attn_ev[(hp, lb)] = evs
        attn_dma[(hp, lb)] = [S.dma("gp", "g_att%d" % spar,
                                    lambda e, hp=hp, lb=lb, ls=ls, spar=spar: e.dma_start(
            out=s_att[lb * 4 + ls, :, hp * 256:(hp + 1) * 256],
            in_=t_sta[spar][:, ls * 256:(ls + 1) * 256]),
            deps=[evs[ls]]) for ls in range(4)]

    def emit_av_group(gc, lds_cur):
        hp, Xp, gKp, t_vsp = pend[0]
        lbp, cp_ = Xp // 2, Xp % 2
        gp = cp_ * CG + gc
        avdp0 = [expof[(gKp, gc)], lds_cur]
        if gc == 0 and cp_ == 0:
            prev_ev = (attn_ev[(hp, lbp - 1)] if lbp >= 1 else
                       (attn_ev[(hp - 1, NQB - 1)] if hp >= 1 else []))
            avdp0 += prev_ev
        avmm = None
        for kq in range(2):
            kt = 2 * gp + kq
            for ls in range(4):
                first = (kq == 0 and ls == 0)
                avmm = S.op("te", lambda e, gKp=gKp, gc=gc, kq=kq, ls=ls, kt=kt, gp=gp, cp_=cp_, t_vsp=t_vsp: e.matmul(
                    PB[ls][:, 0:257],
                    pexp_ap(gKp, gc, (kq * 512 + ls * 128,
                                      kq * 512 + (ls + 1) * 128)),
                    t_vsp[:, kt * 257:(kt + 1) * 257],
                    start=(cp_ == 0 and gp == 0 and kq == 0),
                    stop=(cp_ == 1 and gp == NG - 1 and kq == 1)),
                    deps=(avdp0 if first else [avmm]))
        avdone[gKp] = avmm
        avof[(gKp, gc)] = avmm
        if gc == CG - 1:
            if cp_ == 1:
                emit_denoms(hp, lbp, avmm)
            if Xp == nch2 - 1:
                head_avlast[hp] = avmm
            pend[0] = None
        return avmm

    smm_of = {}
    smm_head_last = {}
    for h in range(NHC):
        t_vs = t_vs2[h % 2]
        dpv = [vs_memset[h % 2]] + \
            ([head_avlast[h - 2]] if h >= 2 else [q_last] + p2v_tail)
        lds = None
        for mt8 in range(0, NLT, 8):
            lds = S.dma("sy", "s_h3", lambda e, h=h, mt8=mt8, t_vs=t_vs: e.dma_start(
                out=t_vs[:].rearrange("p (m f) -> p m f", m=NLT)
                [:, mt8:mt8 + 8, 0:256],
                in_=s_vv[h, mt8:mt8 + 8].rearrange("m p f -> p m f")),
                deps=(dpv if mt8 == 0 else []))
        if h >= 1:
            for ch in range(2):
                kt_done[(h, ch)] = [S.dma("ac", "a_k3",
                                          lambda e, h=h, rs=rs, ch=ch: e.dma_start(
                    out=t_kt[:, rs * L + ch * LH:rs * L + (ch + 1) * LH],
                    in_=s_kt[h, rs, :, ch * LH:(ch + 1) * LH]),
                    deps=([smm_head_last[h - 1]]
                          if ch == 0 and rs == 0 else []))
                    for rs in range(2)]
            for lb in range(NQB):
                dq0 = [smm_of[(h - 1, 2 * lb + 1)]]
                qt_done[(h, lb)] = [
                    S.dma("sy", "s_q3", lambda e, h=h, lb=lb: e.dma_start(
                        out=t_qt[:, lb * 512:(lb + 1) * 512],
                        in_=s_qt[h, 0, :, lb * 512:(lb + 1) * 512]), deps=dq0),
                    S.dma("gp", "g_q3", lambda e, h=h, lb=lb: e.dma_start(
                        out=t_qt[:, L + lb * 512:L + (lb + 1) * 512],
                        in_=s_qt[h, 1, :, lb * 512:(lb + 1) * 512]), deps=dq0)]
        ktq_all = (kt_done[(h, 0)] + kt_done[(h, 1)] +
                   qt_done[(h, NQB - 1)])

        gK0 = h * nch2
        fms = []
        norm_red_last = {}
        sq_mm_last = None
        for side, (src2, base) in enumerate(((t_qt, 0), (t_kt, 16))):
            muls = []
            for ct in range(2):
                dpm = list(ktq_all)
                if h == 0 and side == 0 and ct == 0:
                    dpm += [qmm_last[(NLB - 1, 7)], qmm_last[(NLB - 2, 7)]]
                    dpm += p2_tail
                if h >= 1:
                    dpm.append(avdone[gK0 - 2])
                if side == 1:
                    dpm.append(sq_mm_last)
                muls.append(S.op("ve", lambda e, src2=src2, ct=ct: e.tensor_mul(
                    t_sq[:, ct * L:(ct + 1) * L], src2[:, ct * L:(ct + 1) * L],
                    src2[:, ct * L:(ct + 1) * L]), deps=dpm))
            nch = L // 512
            pan = PA[side]
            reds = []
            for ch in range(nch):
                lo = (ch % 2) * 512
                mm = None
                for ct in range(2):
                    first = (ct == 0)
                    dpm = muls + [gload]
                    if first and ch >= 2:
                        dpm.append(reds[ch - 2])
                    if first and ch < 2 and h >= 1:
                        dpm.append(explast[gK0 - 1])
                    mm = S.op("te", lambda e, ch=ch, ct=ct, pan=pan, lo=lo: e.matmul(
                        pan[:, lo:lo + 512], p_ones128[:],
                        t_sq[:, ct * L + ch * 512:ct * L + (ch + 1) * 512],
                        start=(ct == 0), stop=(ct == 1)),
                        deps=(dpm if first else [mm]))
                sq_mm_last = mm
                reds.append(S.op("ve", lambda e, ch=ch, base=base, pan=pan, lo=lo: e.reduce_max(
                    t_mx[:, base + ch:base + ch + 1], pan[:, lo:lo + 512], axis=AX.X),
                    deps=[mm], hazard=True))
            norm_red_last[side] = reds[-1]
            fms.append(S.op("ve", lambda e, base=base, side=side, nch=nch: e.reduce_max(
                t_mx[:, 32 + side:33 + side], t_mx[:, base:base + nch], axis=AX.X),
                deps=reds, hazard=True))
        prod = S.op("ve", lambda e: e.tensor_mul(
            t_mx[:, 34:35], t_mx[:, 32:33], t_mx[:, 33:34]), deps=fms, hazard=True)
        lnc = S.op("ac", lambda e: e.activation(
            t_mx[:, 35:36], t_mx[:, 34:35], AF.Ln),
            deps=[prod] + (p2_tail if h == 0 else []), hazard=True)
        cexp = S.op("ac", lambda e: e.activation(
            t_mx[:, 36:37], t_mx[:, 35:36], AF.Exp, scale=0.5),
            deps=[lnc], hazard=True)
        negc = S.op("ve", lambda e, h=h: e.tensor_scalar(
            p_negC[:, h:h + 1], t_mx[:, 36:37], -1.0, None, ALU.mult),
            deps=[cexp], hazard=True)

        for K in range(nch2):
            lb, c = K // 2, K % 2
            gK = gK0 + K
            for gc in range(CG):
                g = c * CG + gc
                gg = len(exp_ops)
                pa = PA[gg % 2]
                if pend[0] is not None:
                    emit_av_group(gc, lds)
                smm = None
                sdeps = kt_done[(h, c)] + qt_done[(h, lb)]
                if gg >= 2:
                    sdeps.append(exp_ops[gg - 2])
                if K == 0 and gc < 2:
                    sdeps.append(norm_red_last[gg % 2])
                for kq in range(2):
                    kt = 2 * g + kq
                    for ct in range(2):
                        first = (kq == 0 and ct == 0)
                        smm = S.op("te", lambda e, lb=lb, kt=kt, ct=ct, kq=kq, pa=pa: e.matmul(
                            pa[:, kq * 512:(kq + 1) * 512],
                            t_kt[:, ct * L + kt * 128:ct * L + (kt + 1) * 128],
                            t_qt[:, ct * L + lb * 512:ct * L + (lb + 1) * 512],
                            start=(ct == 0), stop=(ct == 1)),
                            deps=(sdeps if first else [smm]))
                edp = [smm, negc]
                if (gK - 2, gc) in avof:
                    edp.append(avof[(gK - 2, gc)])
                if gc == 0 and K == 0:
                    edp.append(sq_mm_last)
                eo = S.op("ac", lambda e, h=h, gK=gK, gc=gc, pa=pa: e.activation(
                    pexp_ap(gK, gc), pa[:, 0:1024], AF.Exp,
                    bias=p_negC[:, h:h + 1]), deps=edp)
                exp_ops.append(eo)
                expof[(gK, gc)] = eo
            smm_of[(h, K)] = smm
            explast[gK] = exp_ops[-1]
            pend[0] = (h, K, gK, t_vs)
            if h == NHC - 1 and K == 0:
                kern_ldA = S.dma("gp", "g_kern", lambda e: e.dma_start(
                    out=t_kern[:, 0:KSPL], in_=s_kernb[:, 0:KSPL]),
                    deps=[head_avlast[NHC - 2], kern_pc])
        smm_head_last[h] = smm_of[(h, nch2 - 1)]
        head_explast[h] = explast[gK0 + nch2 - 1]
    for gc in range(CG):
        emit_av_group(gc, lds)
    p3_tail = [attn_dma[(NHC - 1, NQB - 1)][-1], attn_dma[(NHC - 1, NQB - 2)][-1],
               attn_ev[(NHC - 1, NQB - 1)][-1]]

    A.phase_reset()
    A.off = A.ranges[id(t_kern)][1]
    t_xf = [A.alloc("t_xf%d" % i, [128, NCT * 512], BF16) for i in range(2)]
    t_aau = A.alloc("t_aau", [128, NCT * 64], BF16)
    t_bau = A.alloc("t_bau", [64, FW], BF16)
    t_axv = [A.alloc("t_axv%d" % i, [64, 512], BF16) for i in range(2)]
    t_xr = A.alloc("t_xr", [128, 4 * FW], F32)
    t_at = A.alloc("t_at", [128, 4 * FW], F32)
    t_t1 = [A.alloc("t_t1%d" % i, [128, FW], F32) for i in range(2)]
    t_ys = [A.alloc("t_ys%d" % i, [128, FW], F32) for i in range(2)]

    p3_guards = [
        (A.ranges[id(t_kt)], head_explast[NHC - 1]),
        (A.ranges[id(t_qt)], head_explast[NHC - 1]),
        (A.ranges[id(t_vs2[0])], head_avlast.get(NHC - 2)),
        (A.ranges[id(t_vs2[1])], head_avlast[NHC - 1]),
        (A.ranges[id(t_sqpe)], head_avlast[NHC - 1]),
        (A.ranges[id(t_mx)], attn_ev[(NHC - 1, NQB - 1)][-1]),
        (A.ranges[id(t_sums)], attn_ev[(NHC - 1, NQB - 1)][-1]),
        (A.ranges[id(t_rec)], attn_ev[(NHC - 1, NQB - 1)][-1]),
        (A.ranges[id(t_sta[0])], attn_dma[(NHC - 1, NQB - 2)][-1]),
        (A.ranges[id(t_sta[1])], attn_dma[(NHC - 1, NQB - 1)][-1]),
    ]

    def p3_guard(t):
        lo, hi = A.ranges[id(t)]
        deps = []
        for (glo, ghi), dep in p3_guards:
            if glo < hi and lo < ghi and dep is not None:
                deps.append(dep)
        return deps

    kern_ld = S.dma("gp", "g_kern", lambda e: e.dma_start(
        out=t_kern[:, KSPL:NCT * FW], in_=s_kernb[:, KSPL:NCT * FW]),
        deps=p3_guard(t_kern))
    PER_G = NLB // 4
    xf_ld = {}
    aau_ld = {}
    bau_ld = {}
    at_ld = {}
    xr_ld = {}
    ffn_last = {}
    ax_mm_last = {}
    ax_ev = {}
    t1_op = {}
    y2_op = {}
    y3_op = {}
    ydma = {}
    for lblk in range(NLB):
        par = lblk % 2
        g4 = lblk // PER_G
        if lblk == 0:
            dp = p3_guard(t_xf[0])
        elif lblk == 1:
            dp = p3_guard(t_xf[1])
        else:
            dp = [ffn_last[(lblk - 2, 3)], ax_mm_last[lblk - 2]]
        xf_ld[lblk] = S.dma("sy", "s_xf%d" % par, lambda e, lblk=lblk, par=par: e.dma_start(
            out=t_xf[par][:], in_=s_xnt[lblk]), deps=dp + [xnt_st[lblk]])
        if lblk % PER_G == 0:
            adp = [ax_mm_last[lblk - 1]] if lblk >= 1 else p3_guard(t_aau)
            aau_ld[g4] = S.dma("sy", "s_aau", lambda e, g4=g4: e.dma_start(
                out=t_aau[:], in_=s_aaug[g4]), deps=adp + [ab_last])
            bdp = [ffn_last[(lblk - 1, 3)]] if lblk >= 1 else p3_guard(t_bau)
            bau_ld[g4] = S.dma("sy", "s_bau", lambda e, g4=g4: e.dma_start(
                out=t_bau[:], in_=s_baug[g4]), deps=bdp + [ab_last])
        atdp = ([y3_op[(lblk - 1, lt)] for lt in range(4)]
                if lblk >= 1 else p3_guard(t_at))
        at_ld[lblk] = S.dma("gp", "g_at", lambda e, lblk=lblk: e.dma_start(
            out=t_at[:].rearrange("p (l f) -> p l f", l=4),
            in_=s_att[lblk * 4:lblk * 4 + 4].rearrange("l p f -> p l f")),
            deps=atdp + [attn_dma[(NHC - 1, lblk)][-1]])
        xrdp = ([y2_op[(lblk - 1, lt)] for lt in range(4)]
                if lblk >= 1 else p3_guard(t_xr))
        xr_ld[lblk] = S.dma("sy", "s_xr", lambda e, lblk=lblk: e.dma_start(
            out=t_xr[:].rearrange("p (l f) -> p l f", l=4),
            in_=xres[lblk * 512:(lblk + 1) * 512, :].rearrange(
                "(l p) f -> p l f", p=128)), deps=xrdp)
        pax = PB[2 + par]
        mm = None
        for ct in range(NCT):
            first = (ct == 0)
            dpm = [xf_ld[lblk], aau_ld[g4]]
            if first and lblk < 2:
                dpm.append(attn_ev[(NHC - 1, NQB - 1)][-1])
            if first and lblk >= 2:
                dpm.append(ax_ev[lblk - 2])
            mm = S.op("te", lambda e, ct=ct, par=par, pax=pax: e.matmul(
                pax[0:64, 0:512], t_aau[:, ct * 64:(ct + 1) * 64],
                t_xf[par][:, ct * 512:(ct + 1) * 512],
                start=(ct == 0), stop=(ct == NCT - 1)),
                deps=(dpm if first else [mm]))
        ax_mm_last[lblk] = mm
        ax_ev[lblk] = S.op("ve", lambda e, par=par, pax=pax: e.tensor_copy(
            t_axv[par][:], pax[0:64, 0:512]), deps=[mm])
        for lt in range(4):
            pa = PA[lt % 2]
            mm = None
            for oc in range(2):
                for ct in range(NCT):
                    first = (oc == 0 and ct == 0)
                    dpm = [xf_ld[lblk], kern_ld]
                    if first:
                        key = (lblk, lt - 2) if lt >= 2 else (lblk - 1, lt + 2)
                        dpm.append(t1_op.get(key))
                    mm = S.op("te", lambda e, lt=lt, oc=oc, ct=ct, pa=pa, par=par: e.matmul(
                        pa[:, oc * 512:(oc + 1) * 512],
                        t_xf[par][:, ct * 512 + lt * 128:ct * 512 + (lt + 1) * 128],
                        t_kern[:, ct * FW + oc * 512:ct * FW + (oc + 1) * 512],
                        start=(ct == 0), stop=False),
                        deps=(dpm if first else [mm]))
            for oc in range(2):
                mm = S.op("te", lambda e, lt=lt, oc=oc, pa=pa, par=par: e.matmul(
                    pa[:, oc * 512:(oc + 1) * 512],
                    t_axv[par][:, lt * 128:(lt + 1) * 128],
                    t_bau[:, oc * 512:(oc + 1) * 512],
                    start=False, stop=True),
                    deps=[mm, ax_ev[lblk], bau_ld[g4]])
            ffn_last[(lblk, lt)] = mm
            key = (lblk, lt - 2) if lt >= 2 else (lblk - 1, lt + 2)
            t1_op[(lblk, lt)] = S.op("ve", lambda e, lt=lt, pa=pa: e.tensor_tensor(
                t_t1[lt % 2][:], pa[:, 0:1024], p_fbbc[:], ALU.add),
                deps=[mm, ev_fb, y2_op.get(key)])
            y2_op[(lblk, lt)] = S.op("gp", lambda e, lt=lt: e.tensor_tensor(
                t_ys[lt % 2][:], t_t1[lt % 2][:], t_xr[:, lt * FW:(lt + 1) * FW],
                ALU.add), deps=[t1_op[(lblk, lt)], xr_ld[lblk], ydma.get(key)])
            y3_op[(lblk, lt)] = S.op("ve", lambda e, lt=lt: e.tensor_tensor(
                t_ys[lt % 2][:], t_ys[lt % 2][:], t_at[:, lt * FW:(lt + 1) * FW],
                ALU.add), deps=[y2_op[(lblk, lt)], at_ld[lblk]])
            ydma[(lblk, lt)] = S.dma("gp", "g_y%d" % (lt % 2),
                                     lambda e, lblk=lblk, lt=lt: e.dma_start(
                out=y_out[lblk * 512 + lt * 128:lblk * 512 + (lt + 1) * 128, :],
                in_=t_ys[lt % 2][:]), deps=[y3_op[(lblk, lt)]])

    S.emit(nc, ent)
    return nc


def _prep_core(inputs, b, hg, L):
    F0 = hg * FW
    x = inputs["L0_attn_x"]
    Wq = inputs["L0_attn_Wq"]
    Wk = inputs["L0_attn_Wk"]
    Wv = inputs["L0_attn_Wv"]
    kernw = inputs["L0_ffn_kernel"]

    def tile_pm(a):
        W = a.shape[1]
        return np.ascontiguousarray(
            a.reshape(32, 128, W).transpose(1, 0, 2).reshape(128, 32 * W))

    m = {}
    m["xbT"] = np.ascontiguousarray(
        x[b, :L].T.reshape(32, 128, L).transpose(1, 0, 2).reshape(128, 32 * L))
    m["xres"] = np.ascontiguousarray(x[b, :L, F0:F0 + FW])
    m["wkT"] = tile_pm(np.ascontiguousarray(Wk[F0:F0 + FW].T))
    m["wvT"] = tile_pm(np.ascontiguousarray(Wv[F0:F0 + FW].T))
    m["kern"] = tile_pm(np.ascontiguousarray(kernw[:, F0:F0 + FW]))
    m["wq"] = np.ascontiguousarray(
        Wq[F0:F0 + FW].reshape(8, 128, D).transpose(1, 0, 2).reshape(128, 8 * D))
    m["tcd"] = np.ascontiguousarray(
        _host_tcd().reshape(4, 2, 128, 256).transpose(2, 0, 1, 3).reshape(128, 8 * 256))
    m["i128"] = np.eye(128, dtype=np.float32)
    m["ones128"] = np.ones((128, 128), dtype=ml_dtypes.bfloat16)
    m["onesf"] = np.ones((1, 128), dtype=np.float32)
    m["i32t"] = np.tile(np.eye(32, dtype=np.float32), (4, 1))
    pnm = np.zeros((16, 512), dtype=np.float32)
    for t in range(4):
        for p in range(128):
            pnm[(t * 128 + p) // 32, t * 128 + p] = 1.0
    m["pn"] = pnm
    m["gateT"] = np.ascontiguousarray(inputs["L0_ffn_gate"][4 * b:4 * b + 4].T)
    m["bwrow"] = np.ascontiguousarray(
        inputs["L0_attn_bias_weight"][4 * hg:4 * hg + 4].reshape(1, 12))
    m["bqc"] = np.ascontiguousarray(
        inputs["L0_attn_bq"][F0:F0 + FW].reshape(8, 128).T)
    m["bkc"] = np.ascontiguousarray(
        inputs["L0_attn_bk"][F0:F0 + FW].reshape(8, 128).T)
    m["bvrow"] = np.ascontiguousarray(
        inputs["L0_attn_bv"][F0:F0 + FW].reshape(1, FW))
    m["fbrow"] = np.ascontiguousarray(
        inputs["L0_ffn_bias"][F0:F0 + FW].reshape(1, FW))
    lA = inputs["L0_ffn_lora_A"]
    m["lAnr"] = np.ascontiguousarray(
        lA.reshape(512, D).reshape(4, 128, D).transpose(1, 0, 2).reshape(128, 4 * D))
    lB = inputs["L0_ffn_lora_B"][:, F0:F0 + FW, :]
    lbnr = lB.transpose(0, 2, 1).reshape(512, FW)
    m["lBnr"] = np.ascontiguousarray(
        lbnr.reshape(4, 128, FW).transpose(1, 0, 2).reshape(128, 4 * FW))
    lAs = inputs["L0_ffn_lora_A_shared"]
    m["lAsT"] = np.ascontiguousarray(
        lAs.T.reshape(32, 128, 32).transpose(1, 0, 2).reshape(128, 32 * 32))
    m["lBsT"] = np.ascontiguousarray(inputs["L0_ffn_lora_B_shared"][F0:F0 + FW].T)
    for k in m:
        if m[k].dtype == np.float64:
            m[k] = m[k].astype(np.float32)
    return m


def kernel(**inputs):
    L = int(inputs["L0_attn_x"].shape[1])
    nc = build_program(L)
    in_maps = [_prep_core(inputs, c // 4, c % 4, L) for c in range(8)]
    res = run_bass_kernel_spmd(nc, in_maps, core_ids=list(range(8)))
    y = np.empty((2, L, D), dtype=np.float32)
    for c in range(8):
        y[c // 4, :, (c % 4) * FW:(c % 4 + 1) * FW] = res.results[c]["y"]
    return y


# version 28
# speedup vs baseline: 1.1755x; 1.0509x over previous
import numpy as np
import ml_dtypes
import concourse.bass as bass
import concourse.mybir as mybir
from concourse.bass_utils import run_bass_kernel_spmd

F32 = mybir.dt.float32
BF16 = mybir.dt.bfloat16
AF = mybir.ActivationFunctionType
ALU = mybir.AluOpType
AX = mybir.AxisListType

D = 4096
NCT = 32
NHC = 4
FW = 1024
EPS = 1e-5
SB_BASE = 16544


def _host_tcd():
    n = 8
    masks = sorted(range(n), key=lambda m: (bin(m).count('1'), m))
    inv = {m: i for i, m in enumerate(masks)}
    C = np.zeros((n, n, n))
    for i, a in enumerate(masks):
        for j, bb in enumerate(masks):
            sw, at = (0, a >> 1)
            while at:
                sw += bin(at & bb).count('1')
                at >>= 1
            C[i, j, inv[a ^ bb]] = -1.0 if sw & 1 else 1.0
    grades = np.array([bin(m).count('1') for m in masks])
    rev = np.where(grades * (grades - 1) // 2 % 2 == 1, -1.0, 1.0)
    bivk = C[:, :, [4, 5, 6]]
    TCD = np.zeros((4, 256, 256), dtype=np.float32)
    TCD[0] = np.kron(np.eye(32), np.diag(rev) / 16.0)
    for c in range(3):
        TCD[c + 1] = np.kron(np.eye(32), rev[:, None] * bivk[:, :, c])
    return TCD


ENGS = ("te", "ve", "ac", "sy", "gp")
SEM_OF = {"te": "TE", "ve": "VE", "ac": "AC", "sy": "SY", "gp": "GP"}


class Op:
    __slots__ = ("eng", "fn", "deps", "hazard", "dma", "stream", "pos", "val",
                 "needs_inc", "waits")

    def __init__(self, eng, fn, deps, hazard, dma, stream, pos):
        self.eng, self.fn, self.deps = eng, fn, deps
        self.hazard, self.dma, self.stream, self.pos = hazard, dma, stream, pos
        self.val = 0
        self.needs_inc = dma
        self.waits = []


class Sched:
    def __init__(self):
        self.ops = {e: [] for e in ENGS}
        self.streams = {}

    def op(self, eng, fn, deps=(), hazard=False):
        deps = [d for d in deps if d is not None]
        o = Op(eng, fn, deps, hazard, False, None, len(self.ops[eng]))
        self.ops[eng].append(o)
        return o

    def dma(self, eng, stream, fn, deps=()):
        deps = [d for d in deps if d is not None]
        o = Op(eng, fn, deps, False, True, stream, len(self.ops[eng]))
        self.ops[eng].append(o)
        self.streams.setdefault(stream, []).append(o)
        return o

    def resolve(self):
        def skip_self(d, o):
            return d.eng == o.eng and d.eng == "te" and not d.hazard

        for e in ENGS:
            for o in self.ops[e]:
                for d in o.deps:
                    if (not d.dma) and (not skip_self(d, o)):
                        d.needs_inc = True
        for e in ENGS:
            c = 0
            for o in self.ops[e]:
                if o.dma:
                    continue
                if o.needs_inc:
                    c += 1
                o.val = c
        for st, ops in self.streams.items():
            c = 0
            for o in ops:
                c += 16
                o.val = c
        for e in ENGS:
            seen = {}
            for o in self.ops[e]:
                want = {}
                for d in o.deps:
                    if d.dma:
                        s = "D:" + d.stream
                    elif skip_self(d, o):
                        assert d.pos < o.pos, (e, o.pos, d.pos)
                        continue
                    else:
                        s = SEM_OF[d.eng]
                    want[s] = max(want.get(s, 0), d.val)
                for s, v in sorted(want.items()):
                    if seen.get(s, 0) < v:
                        o.waits.append((s, v))
                        seen[s] = v

    def emit(self, nc, ent):
        self.resolve()
        sems = {}
        for s in set(SEM_OF.values()):
            sems[s] = ent(nc.semaphore(s))
        for st in self.streams:
            sems["D:" + st] = ent(nc.semaphore("D_" + st))
        with nc.Block() as block:
            hooks = {"te": block.tensor, "ve": block.vector, "ac": block.scalar,
                     "sy": block.sync, "gp": block.gpsimd}
            for e in ENGS:
                ops = self.ops[e]
                if not ops:
                    continue

                def body(eng, ops=ops, e=e):
                    for o in ops:
                        for s, v in o.waits:
                            eng.wait_ge(sems[s], v)
                        inst = o.fn(eng)
                        if o.dma:
                            inst.then_inc(sems["D:" + o.stream], 16)
                        elif o.needs_inc:
                            inst.then_inc(sems[SEM_OF[e]], 1)
                    tot = max((o.val for o in ops if not o.dma), default=0)
                    if tot > 0:
                        eng.wait_ge(sems[SEM_OF[e]], tot)
                    done = set()
                    for o in ops:
                        if o.dma and o.stream not in done:
                            done.add(o.stream)
                            eng.wait_ge(sems["D:" + o.stream],
                                        self.streams[o.stream][-1].val)
                hooks[e](body)


def build_program(L):
    assert L % 512 == 0
    NLB = L // 512
    NLT = L // 128
    NKT = L // 128
    NG = NKT // 2
    NQB = L // 512

    nc = bass.Bass("TRN2", target_bir_lowering=False, debug=False)
    ctx = []

    def ent(cm):
        v = cm.__enter__()
        ctx.append(cm)
        return v

    def din(name, shape, dt):
        return nc.dram_tensor(name, shape, dt, kind="ExternalInput")

    xb = din("xb", [L, D], F32)
    xbT = din("xbT", [128, NCT * L], F32)
    xres = din("xres", [L, FW], F32)
    wkT = din("wkT", [128, NCT * FW], F32)
    wvT = din("wvT", [128, NCT * FW], F32)
    kern = din("kern", [128, NCT * FW], F32)
    wq = din("wq", [128, 8 * D], F32)
    tcd = din("tcd", [128, 8 * 256], F32)
    i128 = din("i128", [128, 128], F32)
    ones128 = din("ones128", [128, 128], BF16)
    onesf = din("onesf", [1, 128], F32)
    i32t = din("i32t", [128, 32], F32)
    pn = din("pn", [16, 512], F32)
    gateT = din("gateT", [16, 4], F32)
    bwrow = din("bwrow", [1, 12], F32)
    bqc = din("bqc", [128, 8], F32)
    bkc = din("bkc", [128, 8], F32)
    bvrow = din("bvrow", [1, FW], F32)
    fbrow = din("fbrow", [1, FW], F32)
    lAnr = din("lAnr", [128, 4 * D], F32)
    lBnr = din("lBnr", [128, 4 * FW], F32)
    lAsT = din("lAsT", [128, 32 * 32], F32)
    lBsT = din("lBsT", [32, FW], F32)
    y_out = nc.dram_tensor("y", [L, FW], F32, kind="ExternalOutput")

    s_wqe = nc.dram_tensor("s_wqe", [128, NHC * NCT * 256], BF16)
    s_xnt = nc.dram_tensor("s_xnt", [NLB, 128, NCT * 512], BF16)
    s_kt = nc.dram_tensor("s_kt", [NHC, 2, 128, L], BF16)
    s_vv = nc.dram_tensor("s_vv", [NHC, NLT, 128, 256], BF16)
    s_qt = nc.dram_tensor("s_qt", [NHC, 2, 128, L], BF16)
    s_att = nc.dram_tensor("s_att", [NLT, 128, FW], BF16)
    s_aaug = nc.dram_tensor("s_aaug", [NHC, 128, NCT * 64], BF16)
    s_wvb = nc.dram_tensor("s_wvb", [128, NCT * FW], BF16)
    s_kernb = nc.dram_tensor("s_kernb", [128, NCT * FW], BF16)
    s_baug = nc.dram_tensor("s_baug", [NHC, 64, FW], BF16)

    PA = [ent(nc.psum_tensor("PA%d" % i, [128, 1024], F32)) for i in range(2)]
    PB = [ent(nc.psum_tensor("PB%d" % i, [128, 512], F32)) for i in range(4)]

    class Arena:
        def __init__(self):
            self.off = SB_BASE
            self.persist_end = None
            self.ranges = {}

        def alloc(self, name, shape, dt):
            per = int(np.prod(shape[1:])) * mybir.dt.size(dt)
            per = (per + 31) & ~31
            t = nc.alloc_sbuf_tensor_at(name, list(shape), dt, offset=self.off)
            self.ranges[id(t)] = (self.off, self.off + per)
            self.off += per
            assert self.off <= SB_BASE + 212800, (name, self.off)
            return t

        def mark_persist(self):
            self.persist_end = self.off

        def phase_reset(self):
            self.off = self.persist_end

    A = Arena()
    p_i128 = A.alloc("p_i128", [128, 128], F32)
    p_ones128 = A.alloc("p_ones128", [128, 128], BF16)
    p_onesf = A.alloc("p_onesf", [1, 128], F32)
    p_bqc = A.alloc("p_bqc", [128, 8], F32)
    p_bkc = A.alloc("p_bkc", [128, 8], F32)
    p_bqe = A.alloc("p_bqe", [128, 8], F32)
    p_bvbc = A.alloc("p_bvbc", [128, FW], F32)
    p_fbbc = A.alloc("p_fbbc", [128, FW], F32)
    p_negC = A.alloc("p_negC", [128, NHC], F32)
    A.mark_persist()

    S = Sched()

    A.phase_reset()
    t_tcd = A.alloc("t_tcd", [128, 8 * 256], F32)
    t_ttf = A.alloc("t_ttf", [128, 8 * 256], F32)
    t_i32t = A.alloc("t_i32t", [128, 32], F32)
    t_pn = A.alloc("t_pn", [16, 512], F32)
    t_gateT = A.alloc("t_gateT", [16, 4], F32)
    t_bwrow = A.alloc("t_bwrow", [1, 12], F32)
    t_bvrow = A.alloc("t_bvrow", [1, FW], F32)
    t_fbrow = A.alloc("t_fbrow", [1, FW], F32)
    t_bwbc = A.alloc("t_bwbc", [128, 12], F32)
    t_gc = A.alloc("t_gc", [128, 16], F32)
    p0_early_end = A.off
    t_ssb = A.alloc("t_ssb", [128, 16 * 32], BF16)
    t_lAsT = A.alloc("t_lAsT", [128, 32 * 32], BF16)
    t_lA = A.alloc("t_lA", [128, 4 * D], BF16)
    t_lB = A.alloc("t_lB", [128, 4 * FW], BF16)
    t_aaug = A.alloc("t_aaug", [128, NCT * 64], BF16)
    t_baug = A.alloc("t_baug", [64, FW], BF16)
    p0_mid_end = A.off
    t_ttb = A.alloc("t_ttb", [128, 8 * 256], BF16)
    t_wqs = [A.alloc("t_wqs%d" % i, [128, 2 * D], BF16) for i in range(2)]
    t_wqeh = [A.alloc("t_wqeh0", [128, NCT * 256], BF16)] * 2

    for src, dst in ((i128, p_i128), (ones128, p_ones128), (lAnr, t_lA),
                     (lBnr, t_lB), (lAsT, t_lAsT)):
        S.dma("gp", "g_once", lambda e, s=src, d=dst: e.dma_start(out=d[:], in_=s[:]))
    gload = S.dma("gp", "g_once",
                  lambda e: e.dma_start(out=t_baug[32:64, :], in_=lBsT[:]))
    wv_pc = S.dma("gp", "g_pc0", lambda e: e.dma_start(out=s_wvb[:], in_=wvT[:]))
    kern_pc = S.dma("gp", "g_pc1", lambda e: e.dma_start(out=s_kernb[:], in_=kern[:]))
    for src, dst in ((tcd, t_tcd), (i32t, t_i32t), (pn, t_pn), (gateT, t_gateT),
                     (bwrow, t_bwrow), (onesf, p_onesf), (bqc, p_bqc),
                     (bkc, p_bkc), (bvrow, t_bvrow)):
        S.dma("sy", "s_once", lambda e, s=src, d=dst: e.dma_start(out=d[:], in_=s[:]))
    sload = S.dma("sy", "s_once",
                  lambda e: e.dma_start(out=t_fbrow[:], in_=fbrow[:]))

    bw_mm = S.op("te", lambda e: e.matmul(PB[0][:, 0:12], p_onesf[:], t_bwrow[:],
                                          start=True, stop=True), deps=[sload])
    bv_mm = [S.op("te", lambda e, c0=c0: e.matmul(
        PA[0][:, c0:c0 + 512], p_onesf[:], t_bvrow[:, c0:c0 + 512],
        start=True, stop=True), deps=[sload]) for c0 in (0, 512)]
    fb_mm = [S.op("te", lambda e, c0=c0: e.matmul(
        PA[1][:, c0:c0 + 512], p_onesf[:], t_fbrow[:, c0:c0 + 512],
        start=True, stop=True), deps=[sload]) for c0 in (0, 512)]
    gc_mm = []
    for g4 in range(4):
        for tt in range(4):
            c = g4 * 4 + tt
            gc_mm.append(S.op("te", lambda e, c=c, g4=g4, tt=tt: e.matmul(
                PB[1][:, c:c + 1], t_pn[:, tt * 128:(tt + 1) * 128],
                t_gateT[:, g4:g4 + 1], start=True, stop=True), deps=[sload]))
    ev_bw = S.op("ve", lambda e: e.tensor_copy(t_bwbc[:], PB[0][:, 0:12]), deps=[bw_mm])
    ev_bv = S.op("ve", lambda e: e.tensor_copy(p_bvbc[:], PA[0][:, 0:1024]), deps=bv_mm)
    ev_fb = S.op("ve", lambda e: e.tensor_copy(p_fbbc[:], PA[1][:, 0:1024]), deps=fb_mm)
    ev_gc = S.op("ve", lambda e: e.tensor_copy(t_gc[:], PB[1][:, 0:16]), deps=gc_mm)

    s_tiles = []
    for c in range(16):
        s_tiles.append(S.op("ve", lambda e, c=c: e.tensor_scalar(
            t_ssb[:, c * 32:(c + 1) * 32], t_i32t[:], t_gc[:, c:c + 1], None,
            ALU.mult), deps=[ev_gc, sload]))

    tt_done = []
    for h in range(NHC):
        for st in range(2):
            o = (h * 2 + st) * 256
            tds = t_tcd[:, st * 256:(st + 1) * 256]
            tc0 = t_tcd[:, (2 + st) * 256:(3 + st) * 256]
            tc1 = t_tcd[:, (4 + st) * 256:(5 + st) * 256]
            tc2 = t_tcd[:, (6 + st) * 256:(7 + st) * 256]
            d1 = S.op("ve", lambda e, o=o, h=h, tc0=tc0, tds=tds: e.scalar_tensor_tensor(
                t_ttf[:, o:o + 256], tc0, t_bwbc[:, h * 3:h * 3 + 1], tds,
                ALU.mult, ALU.add), deps=[sload, ev_bw], hazard=True)
            d2 = S.op("ve", lambda e, o=o, h=h, tc1=tc1: e.scalar_tensor_tensor(
                t_ttf[:, o:o + 256], tc1, t_bwbc[:, h * 3 + 1:h * 3 + 2],
                t_ttf[:, o:o + 256], ALU.mult, ALU.add), deps=[d1], hazard=True)
            d3 = S.op("ve", lambda e, o=o, h=h, tc2=tc2: e.scalar_tensor_tensor(
                t_ttf[:, o:o + 256], tc2, t_bwbc[:, h * 3 + 2:h * 3 + 3],
                t_ttf[:, o:o + 256], ALU.mult, ALU.add), deps=[d2], hazard=True)
            tt_done.append(d3)
    tt_cast = S.op("ve", lambda e: e.tensor_copy(t_ttb[:], t_ttf[:]), deps=tt_done,
                   hazard=True)

    bqe_mm = []
    for h in range(NHC):
        for rs in range(2):
            col = h * 2 + rs
            mm = None
            for st in range(2):
                mm = S.op("te", lambda e, h=h, rs=rs, st=st, col=col: e.matmul(
                    PB[2][:, col:col + 1],
                    t_ttf[:, (h * 2 + st) * 256 + rs * 128:
                          (h * 2 + st) * 256 + rs * 128 + 128],
                    p_bqc[:, h * 2 + st:h * 2 + st + 1],
                    start=(st == 0), stop=(st == 1)),
                    deps=(tt_done + [sload] if st == 0 else [mm]))
            bqe_mm.append(mm)
    ev_bqe = S.op("ve", lambda e: e.tensor_copy(p_bqe[:], PB[2][:, 0:8]), deps=bqe_mm)

    ab_last = None
    st_a_prev = None
    for g4 in range(4):
        ae_ev = {}
        for isl in range(32):
            mm = None
            for tt in range(4):
                first = (tt == 0)
                dpm = [gload, s_tiles[g4 * 4 + tt]]
                if first and isl >= 8:
                    dpm.append(ae_ev[isl - 8])
                mm = S.op("te", lambda e, g4=g4, isl=isl, tt=tt: e.matmul(
                    PB[3][:, (isl % 8) * 32:(isl % 8) * 32 + 32],
                    t_lA[:, tt * D + isl * 128:tt * D + (isl + 1) * 128],
                    t_ssb[:, (g4 * 4 + tt) * 32:(g4 * 4 + tt + 1) * 32],
                    start=(tt == 0), stop=(tt == 3)),
                    deps=(dpm if first else [mm]))
            if isl % 8 == 7:
                ev = S.op("ve", lambda e, isl=isl: e.tensor_copy(
                    t_aaug[:].rearrange("p (i r) -> p i r", i=NCT)
                    [:, isl - 7:isl + 1, 0:32],
                    PB[3][:, 0:256].rearrange("p (i r) -> p i r", i=8)),
                    deps=[mm] + ([st_a_prev] if isl == 7 else []))
                for k in range(isl - 7, isl + 1):
                    ae_ev[k] = ev
        sh_cp = S.op("ve", lambda e: e.tensor_copy(
            t_aaug[:].rearrange("p (i r) -> p i r", i=NCT)[:, :, 32:64],
            t_lAsT[:].rearrange("p (i r) -> p i r", i=NCT)),
            deps=[gload] + ([ab_last] if ab_last else []))
        st_a = S.dma("gp", "g_ab", lambda e, g4=g4: e.dma_start(
            out=s_aaug[g4], in_=t_aaug[:]), deps=[ae_ev[31], sh_cp])
        st_a_prev = st_a
        be_ev = []
        for oc in range(2):
            mm = None
            for tt in range(4):
                first = (tt == 0)
                dpm = [gload, s_tiles[g4 * 4 + tt], ev_bqe]
                if first and be_ev:
                    dpm.append(be_ev[-1])
                mm = S.op("te", lambda e, g4=g4, oc=oc, tt=tt: e.matmul(
                    PB[2][0:32, 0:512],
                    t_ssb[:, (g4 * 4 + tt) * 32:(g4 * 4 + tt + 1) * 32],
                    t_lB[:, tt * FW + oc * 512:tt * FW + oc * 512 + 512],
                    start=(tt == 0), stop=(tt == 3)),
                    deps=(dpm if first else [mm]))
            be_ev.append(S.op("ve", lambda e, oc=oc: e.tensor_copy(
                t_baug[0:32, oc * 512:oc * 512 + 512], PB[2][0:32, 0:512]),
                deps=[mm]))
        ab_last = S.dma("gp", "g_ab", lambda e, g4=g4: e.dma_start(
            out=s_baug[g4], in_=t_baug[:]), deps=be_ev + [gload, st_a])

    wq_ld = []
    for h in range(NHC):
        wq_ld.append(S.dma("gp", "g_wq%d" % (h % 2), lambda e, h=h: e.dma_start(
            out=t_wqs[h % 2][:], in_=wq[:, h * 2 * D:(h + 1) * 2 * D])))
    fold_done = []
    fold_ev_last = []
    prev_evs_last = None
    for h in range(NHC):
        evs = []
        for ct in range(NCT):
            slot = ct % 4
            pa = PA[0] if slot < 2 else PA[1]
            lo = (slot % 2) * 512
            mm = None
            for st in range(2):
                first = (st == 0)
                dpm = [wq_ld[h], tt_cast]
                if first and ct < 4:
                    dpm += [ev_bv, ev_fb, prev_evs_last]
                if first and ct >= 4:
                    dpm.append(evs[ct - 4])
                if first and ct == 0 and h >= 2:
                    dpm.append(fold_done[h - 2])
                mm = S.op("te", lambda e, h=h, ct=ct, st=st, pa=pa, lo=lo: e.matmul(
                    pa[:, lo:lo + 256],
                    t_wqs[h % 2][:, st * D + ct * 128:st * D + (ct + 1) * 128],
                    t_ttb[:, (h * 2 + st) * 256:(h * 2 + st + 1) * 256],
                    start=(st == 0), stop=(st == 1)),
                    deps=(dpm if first else [mm]))
            evs.append(S.op("ve", lambda e, h=h, ct=ct, pa=pa, lo=lo: e.tensor_copy(
                t_wqeh[h % 2][:, ct * 256:(ct + 1) * 256], pa[:, lo:lo + 256]),
                deps=[mm] + ([fold_done[h - 1]] if (ct == 0 and h >= 1) else [])))
        stw = S.dma("sy", "s_wqe%d" % (h % 2), lambda e, h=h: e.dma_start(
            out=s_wqe[:, h * NCT * 256:(h + 1) * NCT * 256], in_=t_wqeh[h % 2][:]),
            deps=[evs[-1]])
        fold_done.append(stw)
        fold_ev_last.append(evs[-1])
        prev_evs_last = evs[-1]
        if h + 2 < NHC:
            wq_ld[h + 2].deps.append(evs[-1])

    p0_done = [ab_last, fold_done[-1], fold_done[-2], ev_bv, ev_fb, ev_bqe,
               gload, sload]

    A.phase_reset()
    t_xt = [A.alloc("t_xt%d" % i, [128, D], BF16) for i in range(2)]
    t_bns = [A.alloc("t_bns%d" % i, [128, 48], F32) for i in range(4)]
    t_mv = [A.alloc("t_mv%d" % i, [128, 2], F32) for i in range(4)]
    t_lnv = [A.alloc("t_lnv%d" % i, [128, 1], F32) for i in range(4)]
    t_veps = [A.alloc("t_veps%d" % i, [128, 1], F32) for i in range(4)]
    t_mrc = [A.alloc("t_mrc%d" % i, [128, 8], F32) for i in range(2)]
    t_mrow = [A.alloc("t_mrow%d" % i, [1, 512], F32) for i in range(2)]
    t_rrow = [A.alloc("t_rrow%d" % i, [1, 512], F32) for i in range(2)]
    p1_stats_end = A.off
    t_xtb = [A.alloc("t_xtb%d" % i, [128, 16 * 512], BF16) for i in range(2)]
    t_tmp = A.alloc("t_tmp", [128, 2 * 512], F32)
    t_xnt = [A.alloc("t_xnt%d" % i, [128, NCT * 512], BF16) for i in range(2)]
    t_wkT = A.alloc("t_wkT", [128, NCT * FW], BF16)
    t_stk = [A.alloc("t_stk0", [128, 8 * 512], BF16)] * 2
    assert p1_stats_end <= p0_early_end + 8 * 1024, (p1_stats_end, p0_early_end)
    assert A.ranges[id(t_tmp)][1] <= p0_mid_end, (A.ranges[id(t_tmp)], p0_mid_end)

    wk_ld = S.dma("gp", "g_wk", lambda e: e.dma_start(out=t_wkT[:], in_=wkT[:]),
                  deps=p0_done)
    xbTv = xbT[:].rearrange("p (c l) -> p c l", c=NCT)

    xload = [None] * NLT
    bn_last = [None] * NLT
    stat_done = [None] * NLT
    stats_ring = {}
    kmm_last = {}
    kevac = {}
    kdma = {}
    xnt_st = {}
    tt_last = {}
    tth_last = {}
    bc_ops = {}
    ev_mr = {}

    def p1_stats(lt):
        lbpar = (lt // 4) % 2
        par4 = lt % 4
        slot = lt % 2
        t4 = lt % 4
        dpx = ([tt_cast, ev_bqe, sload, gload] if lt < 2
               else [bn_last[lt - 2]])
        xload[lt] = S.dma("gp", "g_x%d" % slot, lambda e, lt=lt, slot=slot: e.dma_start(
            out=t_xt[slot][:], in_=xb[lt * 128:(lt + 1) * 128, :]), deps=dpx)
        pv = stats_ring.get(lt - 4, {})
        bns = []
        for j2 in range(8):
            bns.append(S.op("ve", lambda e, slot=slot, j2=j2, par4=par4: e.bn_stats(
                t_bns[par4][:, j2 * 6:(j2 + 1) * 6],
                t_xt[slot][:, j2 * 512:(j2 + 1) * 512]),
                deps=[xload[lt]] + ([pv.get("aggr")] if j2 == 0 else []),
                hazard=True))
        bn_last[lt] = bns[-1]
        aggr = S.op("ve", lambda e, par4=par4: e.bn_aggr(
            t_mv[par4][:, 0:2], t_bns[par4][:]),
            deps=bns + [pv.get("veps"), pv.get("negm")], hazard=True)
        veps = S.op("ve", lambda e, par4=par4: e.tensor_scalar_add(
            t_veps[par4][:], t_mv[par4][:, 1:2], EPS),
            deps=[aggr, pv.get("lnv")], hazard=True)
        lnv = S.op("ac", lambda e, par4=par4: e.activation(
            t_lnv[par4][:], t_veps[par4][:], AF.Ln),
            deps=[veps, pv.get("rstd")], hazard=True)
        wdep = [lnv]
        if lt >= 8:
            wdep.append(bc_ops[lt // 4 - 2])
        rstd = S.op("ac", lambda e, par4=par4, lbpar=lbpar, t4=t4: e.activation(
            t_mrc[lbpar][:, 4 + t4:5 + t4], t_lnv[par4][:], AF.Exp, scale=-0.5),
            deps=wdep, hazard=True)
        negm = S.op("ve", lambda e, par4=par4, lbpar=lbpar, t4=t4: e.tensor_scalar_mul(
            t_mrc[lbpar][:, t4:t4 + 1], t_mv[par4][:, 0:1], -1.0),
            deps=[aggr] + ([bc_ops[lt // 4 - 2]] if lt >= 8 else []), hazard=True)
        stat_done[lt] = (rstd, negm)
        stats_ring[lt] = {"aggr": aggr, "veps": veps, "lnv": lnv,
                          "rstd": rstd, "negm": negm}

    def p1_mr_tr(lblk):
        lbpar = lblk % 2
        deps = [op for lt in range(lblk * 4, lblk * 4 + 4)
                for op in stat_done[lt]]
        if lblk >= 1:
            deps.append(ev_mr[lblk - 1])
        else:
            deps += [ab_last, ev_bqe]
        trs = []
        for j2 in range(8):
            pbx = PB[2 + j2 // 4]
            trs.append(S.op("te", lambda e, j2=j2, lbpar=lbpar, pbx=pbx: e.transpose(
                pbx[0:1, (j2 % 4) * 128:(j2 % 4) * 128 + 128],
                t_mrc[lbpar][:, j2:j2 + 1], p_i128[:]),
                deps=(deps if j2 == 0 else [])))
        edep = [trs[-1]]
        if lblk >= 2:
            edep.append(bc_ops[lblk - 2])
        e1 = S.op("ve", lambda e, lbpar=lbpar: e.tensor_copy(
            t_mrow[lbpar][:], PB[2][0:1, 0:512]), deps=edep)
        e2 = S.op("ve", lambda e, lbpar=lbpar: e.tensor_copy(
            t_rrow[lbpar][:], PB[3][0:1, 0:512]), deps=[trs[-1], e1])
        ev_mr[lblk] = e2

    def p1_mr_bc(lblk):
        lbpar = lblk % 2
        bdep = [ev_mr[lblk]]
        if lblk >= 2:
            bdep.append(tt_last[lblk - 2])
        if lblk < 2:
            bdep += fold_ev_last[2:4]
        b1 = S.op("te", lambda e, lbpar=lbpar: e.matmul(
            PA[lbpar][:, 0:512], p_onesf[:], t_mrow[lbpar][:],
            start=True, stop=True), deps=bdep)
        b2 = S.op("te", lambda e, lbpar=lbpar: e.matmul(
            PA[lbpar][:, 512:1024], p_onesf[:], t_rrow[lbpar][:],
            start=True, stop=True), deps=[b1])
        bc_ops[lblk] = b2

    def p1_norm(lblk):
        lbpar = lblk % 2
        prev = tt_last.get(lblk - 1)
        for half in range(2):
            hdep = ([tth_last[(lblk - 1, half)]] if lblk >= 1
                    else [ab_last, ev_bqe])
            xl = S.dma("gp", "g_xt%d" % half, lambda e, lblk=lblk, half=half: e.dma_start(
                out=t_xtb[half][:],
                in_=xbTv[:, half * 16:(half + 1) * 16,
                         lblk * 512:(lblk + 1) * 512]),
                deps=hdep)
            for q in range(8):
                o1 = None
                for k in range(2):
                    c16 = q * 2 + k
                    d1 = [xl, bc_ops[lblk]]
                    if lblk >= 2 and half == 0 and c16 == 0:
                        d1 += [kmm_last[(lblk - 2, 7)], xnt_st[lblk - 2]]
                    if lblk < 2 and half == 0 and c16 == 0:
                        d1 += [ab_last] + fold_ev_last[2:4]
                    if k == 0 and prev is not None:
                        d1.append(prev)
                    o1 = S.op("ve", lambda e, c16=c16, k=k, lbpar=lbpar, half=half: e.tensor_tensor(
                        t_tmp[:, k * 512:(k + 1) * 512],
                        t_xtb[half][:, c16 * 512:(c16 + 1) * 512],
                        PA[lbpar][:, 0:512], ALU.add),
                        deps=d1, hazard=(k == 1))
                for k in range(2):
                    ct = half * 16 + q * 2 + k
                    prev = S.op("ve", lambda e, ct=ct, k=k, lbpar=lbpar: e.tensor_tensor(
                        t_xnt[lbpar][:, ct * 512:(ct + 1) * 512],
                        t_tmp[:, k * 512:(k + 1) * 512],
                        PA[lbpar][:, 512:1024], ALU.mult),
                        deps=[o1], hazard=(k == 1))
            tth_last[(lblk, half)] = prev
        tt_last[lblk] = prev
        xnt_st[lblk] = S.dma("sy", "s_xnt%d" % lbpar,
                             lambda e, lblk=lblk, lbpar=lbpar: e.dma_start(
            out=s_xnt[lblk], in_=t_xnt[lbpar][:]), deps=[prev])

    def p1_kt(lblk):
        bpar = lblk % 2
        last_tt = tt_last[lblk]
        for ft in range(8):
            pb = PB[ft % 2]
            mm = None
            for ct in range(NCT):
                first = (ct == 0)
                dpm = [last_tt, wk_ld]
                if first and ft >= 2:
                    dpm.append(kevac[(lblk, ft - 2)])
                if first and ft < 2 and lblk >= 1:
                    dpm.append(kevac[(lblk - 1, 6 + ft)])
                mm = S.op("te", lambda e, ft=ft, ct=ct, pb=pb, bpar=bpar: e.matmul(
                    pb[:, 0:512],
                    t_wkT[:, ct * FW + ft * 128:ct * FW + (ft + 1) * 128],
                    t_xnt[bpar][:, ct * 512:(ct + 1) * 512],
                    start=(ct == 0), stop=(ct == NCT - 1)),
                    deps=(dpm if first else [mm]))
            kmm_last[(lblk, ft)] = mm
            edp = [mm]
            if lblk >= 1:
                edp.append(kdma[(lblk - 1, 7)])
            kevac[(lblk, ft)] = S.op("ac", lambda e, ft=ft, pb=pb, bpar=bpar: e.activation(
                t_stk[bpar][:, ft * 512:(ft + 1) * 512], pb[:, 0:512],
                AF.Identity, bias=p_bkc[:, ft:ft + 1]), deps=edp)
            kdma[(lblk, ft)] = S.dma("sy", "s_kd%d" % bpar,
                                     lambda e, lblk=lblk, ft=ft, bpar=bpar: e.dma_start(
                out=s_kt[ft // 2, ft % 2, :, lblk * 512:(lblk + 1) * 512],
                in_=t_stk[bpar][:, ft * 512:(ft + 1) * 512]),
                deps=[kevac[(lblk, ft)]])

    for lt in range(min(8, NLT)):
        p1_stats(lt)
    p1_mr_tr(0)
    p1_mr_bc(0)
    for lblk in range(NLB):
        if lblk + 1 < NLB:
            p1_mr_tr(lblk + 1)
        for lt in range(4 * (lblk + 2), min(4 * (lblk + 3), NLT)):
            p1_stats(lt)
        if lblk >= 1:
            p1_kt(lblk - 1)
        if lblk + 1 < NLB:
            p1_mr_bc(lblk + 1)
        p1_norm(lblk)
    p1_kt(NLB - 1)
    p1_tail = [kdma[(NLB - 1, 7)], kdma[(NLB - 2, 7)],
               xnt_st[NLB - 1], xnt_st[NLB - 2]]

    A.phase_reset()
    t_wvT = A.alloc("t_wvT", [128, NCT * FW], BF16)
    t_xv = [A.alloc("t_xv%d" % i, [128, NCT * 512], BF16) for i in range(2)]
    t_stv = [A.alloc("t_stv%d" % i, [128, FW], BF16) for i in range(2)]

    lev = (NLB - 1) if (NLB - 1) % 2 == 0 else (NLB - 2)
    lod = (NLB - 1) if (NLB - 1) % 2 == 1 else (NLB - 2)
    p1_guards = [
        ((A.ranges[id(t_xt[0])][0], p1_stats_end),
         [bn_last[NLT - 1], bn_last[NLT - 2], stat_done[NLT - 1][0],
          stat_done[NLT - 1][1], bc_ops[NLB - 1], ev_mr[NLB - 1]]),
        (A.ranges[id(t_xtb[0])], [tt_last[NLB - 1]]),
        (A.ranges[id(t_xtb[1])], [tt_last[NLB - 1]]),
        (A.ranges[id(t_tmp)], [tt_last[NLB - 1]]),
        (A.ranges[id(t_xnt[0])], [kmm_last[(lev, 7)], xnt_st[lev]]),
        (A.ranges[id(t_xnt[1])], [kmm_last[(lod, 7)], xnt_st[lod]]),
        (A.ranges[id(t_wkT)], [kmm_last[(NLB - 1, 7)]]),
        (A.ranges[id(t_stk[0])], [kdma[(NLB - 1, 7)], kdma[(NLB - 2, 7)]]),
    ]

    def p1_guard(t):
        lo, hi = A.ranges[id(t)]
        deps = []
        for (glo, ghi), ds2 in p1_guards:
            if glo < hi and lo < ghi:
                deps += ds2
        return deps

    wv_ld = S.dma("gp", "g_wv", lambda e: e.dma_start(out=t_wvT[:], in_=s_wvb[:]),
                  deps=p1_guard(t_wvT) + [wv_pc])
    xv_ld = {}
    vmm_last = {}
    vevac = {}
    vdma = {}
    for lblk in range(NLB):
        par = lblk % 2
        dp = (p1_guard(t_xv[par]) if lblk < 2 else [vmm_last[(lblk - 2, 3)]]) + \
            [xnt_st[lblk]]
        xv_ld[lblk] = S.dma("sy", "s_xv%d" % par, lambda e, lblk=lblk, par=par: e.dma_start(
            out=t_xv[par][:], in_=s_xnt[lblk]), deps=dp)
        for mt in range(4):
            pa = PA[mt % 2]
            mm = None
            for oc in range(2):
                for ct in range(NCT):
                    first = (oc == 0 and ct == 0)
                    dpm = [xv_ld[lblk], wv_ld]
                    if first:
                        key = (lblk, mt - 2) if mt >= 2 else (lblk - 1, mt + 2)
                        dpm.append(vevac.get(key))
                    mm = S.op("te", lambda e, mt=mt, oc=oc, ct=ct, pa=pa, par=par: e.matmul(
                        pa[:, oc * 512:(oc + 1) * 512],
                        t_xv[par][:, ct * 512 + mt * 128:ct * 512 + (mt + 1) * 128],
                        t_wvT[:, ct * FW + oc * 512:ct * FW + (oc + 1) * 512],
                        start=(ct == 0), stop=(ct == NCT - 1)),
                        deps=(dpm if first else [mm]))
            vmm_last[(lblk, mt)] = mm
            edp = [mm, ev_bv]
            key = (lblk, mt - 2) if mt >= 2 else (lblk - 1, mt + 2)
            if key in vdma:
                edp.append(vdma[key][-1])
            elif lblk == 0:
                edp += p1_guard(t_stv[mt % 2])
            vevac[(lblk, mt)] = S.op("ve", lambda e, mt=mt, pa=pa: e.tensor_tensor(
                t_stv[mt % 2][:], pa[:, 0:1024], p_bvbc[:], ALU.add), deps=edp)
            vdma[(lblk, mt)] = [S.dma("gp", "g_vd%d" % (mt % 2),
                                      lambda e, lblk=lblk, mt=mt, h=h: e.dma_start(
                out=s_vv[h, lblk * 4 + mt],
                in_=t_stv[mt % 2][:, h * 256:(h + 1) * 256]),
                deps=[vevac[(lblk, mt)]]) for h in range(NHC)]
    p2v_tail = [vdma[(NLB - 1, 2)][-1], vdma[(NLB - 1, 3)][-1]]

    A.phase_reset()
    t_wqe = A.alloc("t_wqe", [128, NHC * NCT * 256], BF16)
    t_xq = [A.alloc("t_xq%d" % i, [128, NCT * 512], BF16) for i in range(2)]
    t_stq = [A.alloc("t_stq%d" % i, [128, 512], BF16) for i in range(2)]

    def p2v_guard(t):
        lo, hi = A.ranges[id(t)]
        deps = []
        for tt2, ds2 in ((t_wvT, [vmm_last[(NLB - 1, 3)]]),
                         (t_xv[0], [vmm_last[(NLB - 1, 3)]]),
                         (t_xv[1], [vmm_last[(NLB - 1, 3)]]),
                         (t_stv[0], [vdma[(NLB - 1, 2)][-1]]),
                         (t_stv[1], [vdma[(NLB - 1, 3)][-1]])):
            glo, ghi = A.ranges[id(tt2)]
            if glo < hi and lo < ghi:
                deps += ds2
        if hi > A.ranges[id(t_stv[1])][1]:
            deps += p2v_tail
        return deps

    wqe_ld = S.dma("sy", "s_wqeld", lambda e: e.dma_start(out=t_wqe[:], in_=s_wqe[:]),
                   deps=p2v_guard(t_wqe) + [fold_done[-1], fold_done[-2]])
    xq_ld = {}
    qmm_last = {}
    qevac = {}
    qdma = {}
    for lblk in range(NLB):
        par = lblk % 2
        dp = (p2v_guard(t_xq[par]) if lblk < 2 else [qmm_last[(lblk - 2, 7)]]) + \
            [xnt_st[lblk]]
        xq_ld[lblk] = S.dma("sy", "s_xq%d" % par, lambda e, lblk=lblk, par=par: e.dma_start(
            out=t_xq[par][:], in_=s_xnt[lblk]), deps=dp)
        for rt in range(8):
            h, rs = rt // 2, rt % 2
            pb = PB[rt % 2]
            mm = None
            for ct in range(NCT):
                first = (ct == 0)
                dpm = [xq_ld[lblk], wqe_ld]
                if first:
                    key = (lblk, rt - 2) if rt >= 2 else (lblk - 1, rt + 6)
                    dpm.append(qevac.get(key))
                mm = S.op("te", lambda e, h=h, rs=rs, ct=ct, pb=pb, par=par: e.matmul(
                    pb[:, 0:512],
                    t_wqe[:, (h * NCT + ct) * 256 + rs * 128:
                          (h * NCT + ct) * 256 + (rs + 1) * 128],
                    t_xq[par][:, ct * 512:(ct + 1) * 512],
                    start=(ct == 0), stop=(ct == NCT - 1)),
                    deps=(dpm if first else [mm]))
            qmm_last[(lblk, rt)] = mm
            edp = [mm, ev_bqe]
            key = (lblk, rt - 2) if rt >= 2 else (lblk - 1, rt + 6)
            edp.append(qdma.get(key))
            if key not in qdma and lblk == 0:
                edp += p2v_guard(t_stq[rt % 2])
            qevac[(lblk, rt)] = S.op("ac", lambda e, rt=rt, pb=pb: e.activation(
                t_stq[rt % 2][:], pb[:, 0:512], AF.Identity,
                bias=p_bqe[:, rt:rt + 1]), deps=edp)
            qdma[(lblk, rt)] = S.dma("gp", "g_qd%d" % (rt % 2),
                                     lambda e, lblk=lblk, rt=rt: e.dma_start(
                out=s_qt[rt // 2, rt % 2, :, lblk * 512:(lblk + 1) * 512],
                in_=t_stq[rt % 2][:]), deps=[qevac[(lblk, rt)]])
    p2_tail = [qdma[(NLB - 1, 6)], qdma[(NLB - 1, 7)]]

    A.phase_reset()
    t_vs2 = [A.alloc("t_vs%d" % i, [128, NLT * 257], BF16) for i in range(2)]
    t_sqpe = A.alloc("t_sqpe", [128, 4 * L], BF16)
    t_sq = t_sqpe
    t_mx = A.alloc("t_mx", [128, 40], F32)
    t_sums = A.alloc("t_sums", [128, 4], F32)
    t_rec = A.alloc("t_rec", [128, 4], F32)
    t_sta = [A.alloc("t_sta%d" % i, [128, NHC * 256], BF16) for i in range(2)]
    t_kt = A.alloc("t_kt", [128, 2 * L], BF16)
    t_qt = A.alloc("t_qt", [128, 2 * L], BF16)
    q_ext = A.ranges[id(t_stq[1])][1]
    v_ext = A.ranges[id(t_stv[1])][1]
    assert A.ranges[id(t_kt)][0] >= max(q_ext, v_ext), (
        A.ranges[id(t_kt)], q_ext, v_ext)

    q_last = qmm_last[(NLB - 1, 7)]
    vs_memset = [S.op("ve", lambda e, i=i: e.memset(
        t_vs2[i][:].rearrange("p (m f) -> p m f", m=NLT)[:, :, 256:257], 1.0),
        deps=[q_last]) for i in range(2)]

    exp_ops = []
    attn_ev = {}
    attn_dma = {}
    head_explast = {}
    head_avlast = {}
    avdone = {}
    avof = {}
    expof = {}
    explast = {}
    pend = [None]

    CG = NG // 2
    nch2 = 2 * NQB

    def pexp_ap(gK, gc, sl=None):
        base = (gK % 2) * 2 * L + gc * 1024
        if sl is None:
            return t_sqpe[:, base:base + 1024]
        return t_sqpe[:, base + sl[0]:base + sl[1]]

    def emit_denoms(hp, lb, avl):
        cps = []
        for ls in range(4):
            cps.append(S.op("ve", lambda e, ls=ls: e.tensor_copy(
                t_sums[:, ls:ls + 1], PB[ls][:, 256:257]), deps=[avl]))
        rec = S.op("ve", lambda e: e.reciprocal(t_rec[:], t_sums[:]),
                   deps=cps, hazard=True)
        spar = lb % 2
        evs = []
        for ls in range(4):
            edp2 = [rec, avl]
            wkey = (hp, lb - 2) if lb >= 2 else (hp - 1, NQB - 2 + lb)
            if wkey in attn_dma:
                edp2.append(attn_dma[wkey][-1])
            evs.append(S.op("ve", lambda e, ls=ls, spar=spar: e.tensor_scalar_mul(
                t_sta[spar][:, ls * 256:(ls + 1) * 256],
                PB[ls][:, 0:256], t_rec[:, ls:ls + 1]),
                deps=edp2))
        attn_ev[(hp, lb)] = evs
        attn_dma[(hp, lb)] = [S.dma("gp", "g_att%d" % spar,
                                    lambda e, hp=hp, lb=lb, ls=ls, spar=spar: e.dma_start(
            out=s_att[lb * 4 + ls, :, hp * 256:(hp + 1) * 256],
            in_=t_sta[spar][:, ls * 256:(ls + 1) * 256]),
            deps=[evs[ls]]) for ls in range(4)]

    def emit_av_group(gc, lds_cur):
        hp, Xp, gKp, t_vsp = pend[0]
        lbp, cp_ = Xp // 2, Xp % 2
        gp = cp_ * CG + gc
        avdp0 = [expof[(gKp, gc)], lds_cur]
        if gc == 0 and cp_ == 0:
            prev_ev = (attn_ev[(hp, lbp - 1)] if lbp >= 1 else
                       (attn_ev[(hp - 1, NQB - 1)] if hp >= 1 else []))
            avdp0 += prev_ev
        avmm = None
        for kq in range(2):
            kt = 2 * gp + kq
            for ls in range(4):
                first = (kq == 0 and ls == 0)
                avmm = S.op("te", lambda e, gKp=gKp, gc=gc, kq=kq, ls=ls, kt=kt, gp=gp, cp_=cp_, t_vsp=t_vsp: e.matmul(
                    PB[ls][:, 0:257],
                    pexp_ap(gKp, gc, (kq * 512 + ls * 128,
                                      kq * 512 + (ls + 1) * 128)),
                    t_vsp[:, kt * 257:(kt + 1) * 257],
                    start=(cp_ == 0 and gp == 0 and kq == 0),
                    stop=(cp_ == 1 and gp == NG - 1 and kq == 1)),
                    deps=(avdp0 if first else [avmm]))
        avdone[gKp] = avmm
        avof[(gKp, gc)] = avmm
        if gc == CG - 1:
            if cp_ == 1:
                emit_denoms(hp, lbp, avmm)
            if Xp == nch2 - 1:
                head_avlast[hp] = avmm
            pend[0] = None
        return avmm

    kt_done = {}
    qt_done = {}
    smm_of = {}
    smm_head_last = {}
    LH = L // 2
    for h in range(NHC):
        t_vs = t_vs2[h % 2]
        dpv = [vs_memset[h % 2]] + \
            ([head_avlast[h - 2]] if h >= 2 else [q_last] + p2v_tail)
        lds = None
        for mt8 in range(0, NLT, 8):
            lds = S.dma("sy", "s_h3", lambda e, h=h, mt8=mt8, t_vs=t_vs: e.dma_start(
                out=t_vs[:].rearrange("p (m f) -> p m f", m=NLT)
                [:, mt8:mt8 + 8, 0:256],
                in_=s_vv[h, mt8:mt8 + 8].rearrange("m p f -> p m f")),
                deps=(dpv if mt8 == 0 else []))
        for ch in range(2):
            dpk = [smm_head_last[h - 1]] if h >= 1 else (
                p1_guard(t_kt) + [kdma[(NLB - 1, 7)], kdma[(NLB - 2, 7)]])
            kt_done[(h, ch)] = [S.dma("ac", "a_k3",
                                      lambda e, h=h, rs=rs, ch=ch: e.dma_start(
                out=t_kt[:, rs * L + ch * LH:rs * L + (ch + 1) * LH],
                in_=s_kt[h, rs, :, ch * LH:(ch + 1) * LH]),
                deps=(dpk if ch == 0 and rs == 0 else []))
                for rs in range(2)]
        for lb in range(NQB):
            if h == 0:
                dq0 = [qdma[(lb, 0)], qdma[(lb, 1)]]
                if lb == 0:
                    dq0 = dq0 + p1_guard(t_qt)
                dq1 = dq0
            else:
                dq0 = [smm_of[(h - 1, 2 * lb + 1)]]
                dq1 = dq0
            qt_done[(h, lb)] = [
                S.dma("sy", "s_q3", lambda e, h=h, lb=lb: e.dma_start(
                    out=t_qt[:, lb * 512:(lb + 1) * 512],
                    in_=s_qt[h, 0, :, lb * 512:(lb + 1) * 512]), deps=dq0),
                S.dma("gp", "g_q3", lambda e, h=h, lb=lb: e.dma_start(
                    out=t_qt[:, L + lb * 512:L + (lb + 1) * 512],
                    in_=s_qt[h, 1, :, lb * 512:(lb + 1) * 512]), deps=dq1)]
        ktq_all = (kt_done[(h, 0)] + kt_done[(h, 1)] +
                   qt_done[(h, NQB - 1)])

        gK0 = h * nch2
        fms = []
        norm_red_last = {}
        sq_mm_last = None
        for side, (src2, base) in enumerate(((t_qt, 0), (t_kt, 16))):
            muls = []
            for ct in range(2):
                dpm = list(ktq_all)
                if h == 0 and side == 0 and ct == 0:
                    dpm += [qmm_last[(NLB - 1, 7)], qmm_last[(NLB - 2, 7)]]
                    dpm += p2_tail
                if h >= 1:
                    dpm.append(avdone[gK0 - 2])
                if side == 1:
                    dpm.append(sq_mm_last)
                muls.append(S.op("ve", lambda e, src2=src2, ct=ct: e.tensor_mul(
                    t_sq[:, ct * L:(ct + 1) * L], src2[:, ct * L:(ct + 1) * L],
                    src2[:, ct * L:(ct + 1) * L]), deps=dpm))
            nch = L // 512
            pan = PA[side]
            reds = []
            for ch in range(nch):
                lo = (ch % 2) * 512
                mm = None
                for ct in range(2):
                    first = (ct == 0)
                    dpm = muls + [gload]
                    if first and ch >= 2:
                        dpm.append(reds[ch - 2])
                    if first and ch < 2 and h >= 1:
                        dpm.append(explast[gK0 - 1])
                    mm = S.op("te", lambda e, ch=ch, ct=ct, pan=pan, lo=lo: e.matmul(
                        pan[:, lo:lo + 512], p_ones128[:],
                        t_sq[:, ct * L + ch * 512:ct * L + (ch + 1) * 512],
                        start=(ct == 0), stop=(ct == 1)),
                        deps=(dpm if first else [mm]))
                sq_mm_last = mm
                reds.append(S.op("ve", lambda e, ch=ch, base=base, pan=pan, lo=lo: e.reduce_max(
                    t_mx[:, base + ch:base + ch + 1], pan[:, lo:lo + 512], axis=AX.X),
                    deps=[mm], hazard=True))
            norm_red_last[side] = reds[-1]
            fms.append(S.op("ve", lambda e, base=base, side=side, nch=nch: e.reduce_max(
                t_mx[:, 32 + side:33 + side], t_mx[:, base:base + nch], axis=AX.X),
                deps=reds, hazard=True))
        prod = S.op("ve", lambda e: e.tensor_mul(
            t_mx[:, 34:35], t_mx[:, 32:33], t_mx[:, 33:34]), deps=fms, hazard=True)
        lnc = S.op("ac", lambda e: e.activation(
            t_mx[:, 35:36], t_mx[:, 34:35], AF.Ln),
            deps=[prod] + (p2_tail if h == 0 else []), hazard=True)
        cexp = S.op("ac", lambda e: e.activation(
            t_mx[:, 36:37], t_mx[:, 35:36], AF.Exp, scale=0.5),
            deps=[lnc], hazard=True)
        negc = S.op("ve", lambda e, h=h: e.tensor_scalar(
            p_negC[:, h:h + 1], t_mx[:, 36:37], -1.0, None, ALU.mult),
            deps=[cexp], hazard=True)

        for K in range(nch2):
            lb, c = K // 2, K % 2
            gK = gK0 + K
            for gc in range(CG):
                g = c * CG + gc
                gg = len(exp_ops)
                pa = PA[gg % 2]
                if pend[0] is not None:
                    emit_av_group(gc, lds)
                smm = None
                sdeps = kt_done[(h, c)] + qt_done[(h, lb)]
                if gg >= 2:
                    sdeps.append(exp_ops[gg - 2])
                if K == 0 and gc < 2:
                    sdeps.append(norm_red_last[gg % 2])
                for kq in range(2):
                    kt = 2 * g + kq
                    for ct in range(2):
                        first = (kq == 0 and ct == 0)
                        smm = S.op("te", lambda e, lb=lb, kt=kt, ct=ct, kq=kq, pa=pa: e.matmul(
                            pa[:, kq * 512:(kq + 1) * 512],
                            t_kt[:, ct * L + kt * 128:ct * L + (kt + 1) * 128],
                            t_qt[:, ct * L + lb * 512:ct * L + (lb + 1) * 512],
                            start=(ct == 0), stop=(ct == 1)),
                            deps=(sdeps if first else [smm]))
                edp = [smm, negc]
                if (gK - 2, gc) in avof:
                    edp.append(avof[(gK - 2, gc)])
                if gc == 0 and K == 0:
                    edp.append(sq_mm_last)
                eo = S.op("ac", lambda e, h=h, gK=gK, gc=gc, pa=pa: e.activation(
                    pexp_ap(gK, gc), pa[:, 0:1024], AF.Exp,
                    bias=p_negC[:, h:h + 1]), deps=edp)
                exp_ops.append(eo)
                expof[(gK, gc)] = eo
            smm_of[(h, K)] = smm
            explast[gK] = exp_ops[-1]
            pend[0] = (h, K, gK, t_vs)
        smm_head_last[h] = smm_of[(h, nch2 - 1)]
        head_explast[h] = explast[gK0 + nch2 - 1]
    for gc in range(CG):
        emit_av_group(gc, lds)
    p3_tail = [attn_dma[(NHC - 1, NQB - 1)][-1], attn_dma[(NHC - 1, NQB - 2)][-1],
               attn_ev[(NHC - 1, NQB - 1)][-1]]

    A.phase_reset()
    t_kern = A.alloc("t_kern", [128, NCT * FW], BF16)
    t_xf = [A.alloc("t_xf%d" % i, [128, NCT * 512], BF16) for i in range(2)]
    t_aau = A.alloc("t_aau", [128, NCT * 64], BF16)
    t_bau = A.alloc("t_bau", [64, FW], BF16)
    t_axv = [A.alloc("t_axv%d" % i, [64, 512], BF16) for i in range(2)]
    t_xr = A.alloc("t_xr", [128, 4 * FW], F32)
    t_at = A.alloc("t_at", [128, 4 * FW], F32)
    t_t1 = [A.alloc("t_t1%d" % i, [128, FW], F32) for i in range(2)]
    t_ys = [A.alloc("t_ys%d" % i, [128, FW], F32) for i in range(2)]

    p3_guards = [
        (A.ranges[id(t_kt)], head_explast[NHC - 1]),
        (A.ranges[id(t_qt)], head_explast[NHC - 1]),
        (A.ranges[id(t_vs2[0])], head_avlast.get(NHC - 2)),
        (A.ranges[id(t_vs2[1])], head_avlast[NHC - 1]),
        (A.ranges[id(t_sqpe)], head_avlast[NHC - 1]),
        (A.ranges[id(t_mx)], attn_ev[(NHC - 1, NQB - 1)][-1]),
        (A.ranges[id(t_sums)], attn_ev[(NHC - 1, NQB - 1)][-1]),
        (A.ranges[id(t_rec)], attn_ev[(NHC - 1, NQB - 1)][-1]),
        (A.ranges[id(t_sta[0])], attn_dma[(NHC - 1, NQB - 2)][-1]),
        (A.ranges[id(t_sta[1])], attn_dma[(NHC - 1, NQB - 1)][-1]),
    ]

    def p3_guard(t):
        lo, hi = A.ranges[id(t)]
        deps = []
        for (glo, ghi), dep in p3_guards:
            if glo < hi and lo < ghi and dep is not None:
                deps.append(dep)
        return deps

    kern_ld = S.dma("gp", "g_kern", lambda e: e.dma_start(out=t_kern[:], in_=s_kernb[:]),
                    deps=p3_guard(t_kern) + [kern_pc])
    PER_G = NLB // 4
    xf_ld = {}
    aau_ld = {}
    bau_ld = {}
    at_ld = {}
    xr_ld = {}
    ffn_last = {}
    ax_mm_last = {}
    ax_ev = {}
    t1_op = {}
    y2_op = {}
    y3_op = {}
    ydma = {}
    for lblk in range(NLB):
        par = lblk % 2
        g4 = lblk // PER_G
        if lblk == 0:
            dp = p3_guard(t_xf[0])
        elif lblk == 1:
            dp = p3_guard(t_xf[1])
        else:
            dp = [ffn_last[(lblk - 2, 3)], ax_mm_last[lblk - 2]]
        xf_ld[lblk] = S.dma("sy", "s_xf%d" % par, lambda e, lblk=lblk, par=par: e.dma_start(
            out=t_xf[par][:], in_=s_xnt[lblk]), deps=dp + [xnt_st[lblk]])
        if lblk % PER_G == 0:
            adp = [ax_mm_last[lblk - 1]] if lblk >= 1 else p3_guard(t_aau)
            aau_ld[g4] = S.dma("sy", "s_aau", lambda e, g4=g4: e.dma_start(
                out=t_aau[:], in_=s_aaug[g4]), deps=adp + [ab_last])
            bdp = [ffn_last[(lblk - 1, 3)]] if lblk >= 1 else p3_guard(t_bau)
            bau_ld[g4] = S.dma("sy", "s_bau", lambda e, g4=g4: e.dma_start(
                out=t_bau[:], in_=s_baug[g4]), deps=bdp + [ab_last])
        atdp = ([y3_op[(lblk - 1, lt)] for lt in range(4)]
                if lblk >= 1 else p3_guard(t_at))
        at_ld[lblk] = S.dma("gp", "g_at", lambda e, lblk=lblk: e.dma_start(
            out=t_at[:].rearrange("p (l f) -> p l f", l=4),
            in_=s_att[lblk * 4:lblk * 4 + 4].rearrange("l p f -> p l f")),
            deps=atdp + [attn_dma[(NHC - 1, lblk)][-1]])
        xrdp = ([y2_op[(lblk - 1, lt)] for lt in range(4)]
                if lblk >= 1 else p3_guard(t_xr))
        xr_ld[lblk] = S.dma("sy", "s_xr", lambda e, lblk=lblk: e.dma_start(
            out=t_xr[:].rearrange("p (l f) -> p l f", l=4),
            in_=xres[lblk * 512:(lblk + 1) * 512, :].rearrange(
                "(l p) f -> p l f", p=128)), deps=xrdp)
        pax = PB[2 + par]
        mm = None
        for ct in range(NCT):
            first = (ct == 0)
            dpm = [xf_ld[lblk], aau_ld[g4]]
            if first and lblk < 2:
                dpm.append(attn_ev[(NHC - 1, NQB - 1)][-1])
            if first and lblk >= 2:
                dpm.append(ax_ev[lblk - 2])
            mm = S.op("te", lambda e, ct=ct, par=par, pax=pax: e.matmul(
                pax[0:64, 0:512], t_aau[:, ct * 64:(ct + 1) * 64],
                t_xf[par][:, ct * 512:(ct + 1) * 512],
                start=(ct == 0), stop=(ct == NCT - 1)),
                deps=(dpm if first else [mm]))
        ax_mm_last[lblk] = mm
        ax_ev[lblk] = S.op("ve", lambda e, par=par, pax=pax: e.tensor_copy(
            t_axv[par][:], pax[0:64, 0:512]), deps=[mm])
        for lt in range(4):
            pa = PA[lt % 2]
            mm = None
            for oc in range(2):
                for ct in range(NCT):
                    first = (oc == 0 and ct == 0)
                    dpm = [xf_ld[lblk], kern_ld]
                    if first:
                        key = (lblk, lt - 2) if lt >= 2 else (lblk - 1, lt + 2)
                        dpm.append(t1_op.get(key))
                    mm = S.op("te", lambda e, lt=lt, oc=oc, ct=ct, pa=pa, par=par: e.matmul(
                        pa[:, oc * 512:(oc + 1) * 512],
                        t_xf[par][:, ct * 512 + lt * 128:ct * 512 + (lt + 1) * 128],
                        t_kern[:, ct * FW + oc * 512:ct * FW + (oc + 1) * 512],
                        start=(ct == 0), stop=False),
                        deps=(dpm if first else [mm]))
            for oc in range(2):
                mm = S.op("te", lambda e, lt=lt, oc=oc, pa=pa, par=par: e.matmul(
                    pa[:, oc * 512:(oc + 1) * 512],
                    t_axv[par][:, lt * 128:(lt + 1) * 128],
                    t_bau[:, oc * 512:(oc + 1) * 512],
                    start=False, stop=True),
                    deps=[mm, ax_ev[lblk], bau_ld[g4]])
            ffn_last[(lblk, lt)] = mm
            key = (lblk, lt - 2) if lt >= 2 else (lblk - 1, lt + 2)
            t1_op[(lblk, lt)] = S.op("ve", lambda e, lt=lt, pa=pa: e.tensor_tensor(
                t_t1[lt % 2][:], pa[:, 0:1024], p_fbbc[:], ALU.add),
                deps=[mm, ev_fb, y2_op.get(key)])
            y2_op[(lblk, lt)] = S.op("gp", lambda e, lt=lt: e.tensor_tensor(
                t_ys[lt % 2][:], t_t1[lt % 2][:], t_xr[:, lt * FW:(lt + 1) * FW],
                ALU.add), deps=[t1_op[(lblk, lt)], xr_ld[lblk], ydma.get(key)])
            y3_op[(lblk, lt)] = S.op("ve", lambda e, lt=lt: e.tensor_tensor(
                t_ys[lt % 2][:], t_ys[lt % 2][:], t_at[:, lt * FW:(lt + 1) * FW],
                ALU.add), deps=[y2_op[(lblk, lt)], at_ld[lblk]])
            ydma[(lblk, lt)] = S.dma("gp", "g_y%d" % (lt % 2),
                                     lambda e, lblk=lblk, lt=lt: e.dma_start(
                out=y_out[lblk * 512 + lt * 128:lblk * 512 + (lt + 1) * 128, :],
                in_=t_ys[lt % 2][:]), deps=[y3_op[(lblk, lt)]])

    S.emit(nc, ent)
    return nc


def _prep_core(inputs, b, hg, L):
    F0 = hg * FW
    x = inputs["L0_attn_x"]
    Wq = inputs["L0_attn_Wq"]
    Wk = inputs["L0_attn_Wk"]
    Wv = inputs["L0_attn_Wv"]
    kernw = inputs["L0_ffn_kernel"]

    def tile_pm(a):
        W = a.shape[1]
        return np.ascontiguousarray(
            a.reshape(32, 128, W).transpose(1, 0, 2).reshape(128, 32 * W))

    m = {}
    m["xb"] = np.ascontiguousarray(x[b, :L])
    m["xbT"] = np.ascontiguousarray(
        x[b, :L].T.reshape(32, 128, L).transpose(1, 0, 2).reshape(128, 32 * L))
    m["xres"] = np.ascontiguousarray(x[b, :L, F0:F0 + FW])
    m["wkT"] = tile_pm(np.ascontiguousarray(Wk[F0:F0 + FW].T))
    m["wvT"] = tile_pm(np.ascontiguousarray(Wv[F0:F0 + FW].T))
    m["kern"] = tile_pm(np.ascontiguousarray(kernw[:, F0:F0 + FW]))
    m["wq"] = np.ascontiguousarray(
        Wq[F0:F0 + FW].reshape(8, 128, D).transpose(1, 0, 2).reshape(128, 8 * D))
    m["tcd"] = np.ascontiguousarray(
        _host_tcd().reshape(4, 2, 128, 256).transpose(2, 0, 1, 3).reshape(128, 8 * 256))
    m["i128"] = np.eye(128, dtype=np.float32)
    m["ones128"] = np.ones((128, 128), dtype=ml_dtypes.bfloat16)
    m["onesf"] = np.ones((1, 128), dtype=np.float32)
    m["i32t"] = np.tile(np.eye(32, dtype=np.float32), (4, 1))
    pnm = np.zeros((16, 512), dtype=np.float32)
    for t in range(4):
        for p in range(128):
            pnm[(t * 128 + p) // 32, t * 128 + p] = 1.0
    m["pn"] = pnm
    m["gateT"] = np.ascontiguousarray(inputs["L0_ffn_gate"][4 * b:4 * b + 4].T)
    m["bwrow"] = np.ascontiguousarray(
        inputs["L0_attn_bias_weight"][4 * hg:4 * hg + 4].reshape(1, 12))
    m["bqc"] = np.ascontiguousarray(
        inputs["L0_attn_bq"][F0:F0 + FW].reshape(8, 128).T)
    m["bkc"] = np.ascontiguousarray(
        inputs["L0_attn_bk"][F0:F0 + FW].reshape(8, 128).T)
    m["bvrow"] = np.ascontiguousarray(
        inputs["L0_attn_bv"][F0:F0 + FW].reshape(1, FW))
    m["fbrow"] = np.ascontiguousarray(
        inputs["L0_ffn_bias"][F0:F0 + FW].reshape(1, FW))
    lA = inputs["L0_ffn_lora_A"]
    m["lAnr"] = np.ascontiguousarray(
        lA.reshape(512, D).reshape(4, 128, D).transpose(1, 0, 2).reshape(128, 4 * D))
    lB = inputs["L0_ffn_lora_B"][:, F0:F0 + FW, :]
    lbnr = lB.transpose(0, 2, 1).reshape(512, FW)
    m["lBnr"] = np.ascontiguousarray(
        lbnr.reshape(4, 128, FW).transpose(1, 0, 2).reshape(128, 4 * FW))
    lAs = inputs["L0_ffn_lora_A_shared"]
    m["lAsT"] = np.ascontiguousarray(
        lAs.T.reshape(32, 128, 32).transpose(1, 0, 2).reshape(128, 32 * 32))
    m["lBsT"] = np.ascontiguousarray(inputs["L0_ffn_lora_B_shared"][F0:F0 + FW].T)
    for k in m:
        if m[k].dtype == np.float64:
            m[k] = m[k].astype(np.float32)
    return m


def kernel(**inputs):
    L = int(inputs["L0_attn_x"].shape[1])
    nc = build_program(L)
    in_maps = [_prep_core(inputs, c // 4, c % 4, L) for c in range(8)]
    res = run_bass_kernel_spmd(nc, in_maps, core_ids=list(range(8)))
    y = np.empty((2, L, D), dtype=np.float32)
    for c in range(8):
        y[c // 4, :, (c % 4) * FW:(c % 4 + 1) * FW] = res.results[c]["y"]
    return y


# version 29
# speedup vs baseline: 1.1804x; 1.0042x over previous
import numpy as np
import ml_dtypes
import concourse.bass as bass
import concourse.mybir as mybir
from concourse.bass_utils import run_bass_kernel_spmd

F32 = mybir.dt.float32
BF16 = mybir.dt.bfloat16
AF = mybir.ActivationFunctionType
ALU = mybir.AluOpType
AX = mybir.AxisListType

D = 4096
NCT = 32
NHC = 4
FW = 1024
EPS = 1e-5
SB_BASE = 16544


def _host_tcd():
    n = 8
    masks = sorted(range(n), key=lambda m: (bin(m).count('1'), m))
    inv = {m: i for i, m in enumerate(masks)}
    C = np.zeros((n, n, n))
    for i, a in enumerate(masks):
        for j, bb in enumerate(masks):
            sw, at = (0, a >> 1)
            while at:
                sw += bin(at & bb).count('1')
                at >>= 1
            C[i, j, inv[a ^ bb]] = -1.0 if sw & 1 else 1.0
    grades = np.array([bin(m).count('1') for m in masks])
    rev = np.where(grades * (grades - 1) // 2 % 2 == 1, -1.0, 1.0)
    bivk = C[:, :, [4, 5, 6]]
    TCD = np.zeros((4, 256, 256), dtype=np.float32)
    TCD[0] = np.kron(np.eye(32), np.diag(rev) / 16.0)
    for c in range(3):
        TCD[c + 1] = np.kron(np.eye(32), rev[:, None] * bivk[:, :, c])
    return TCD


ENGS = ("te", "ve", "ac", "sy", "gp")
SEM_OF = {"te": "TE", "ve": "VE", "ac": "AC", "sy": "SY", "gp": "GP"}


class Op:
    __slots__ = ("eng", "fn", "deps", "hazard", "dma", "stream", "pos", "val",
                 "needs_inc", "waits")

    def __init__(self, eng, fn, deps, hazard, dma, stream, pos):
        self.eng, self.fn, self.deps = eng, fn, deps
        self.hazard, self.dma, self.stream, self.pos = hazard, dma, stream, pos
        self.val = 0
        self.needs_inc = dma
        self.waits = []


class Sched:
    def __init__(self):
        self.ops = {e: [] for e in ENGS}
        self.streams = {}

    def op(self, eng, fn, deps=(), hazard=False):
        deps = [d for d in deps if d is not None]
        o = Op(eng, fn, deps, hazard, False, None, len(self.ops[eng]))
        self.ops[eng].append(o)
        return o

    def dma(self, eng, stream, fn, deps=()):
        deps = [d for d in deps if d is not None]
        o = Op(eng, fn, deps, False, True, stream, len(self.ops[eng]))
        self.ops[eng].append(o)
        self.streams.setdefault(stream, []).append(o)
        return o

    def resolve(self):
        def skip_self(d, o):
            return d.eng == o.eng and d.eng == "te" and not d.hazard

        for e in ENGS:
            for o in self.ops[e]:
                for d in o.deps:
                    if (not d.dma) and (not skip_self(d, o)):
                        d.needs_inc = True
        for e in ENGS:
            c = 0
            for o in self.ops[e]:
                if o.dma:
                    continue
                if o.needs_inc:
                    c += 1
                o.val = c
        for st, ops in self.streams.items():
            c = 0
            for o in ops:
                c += 16
                o.val = c
        for e in ENGS:
            seen = {}
            for o in self.ops[e]:
                want = {}
                for d in o.deps:
                    if d.dma:
                        s = "D:" + d.stream
                    elif skip_self(d, o):
                        assert d.pos < o.pos, (e, o.pos, d.pos)
                        continue
                    else:
                        s = SEM_OF[d.eng]
                    want[s] = max(want.get(s, 0), d.val)
                for s, v in sorted(want.items()):
                    if seen.get(s, 0) < v:
                        o.waits.append((s, v))
                        seen[s] = v

    def emit(self, nc, ent):
        self.resolve()
        sems = {}
        for s in set(SEM_OF.values()):
            sems[s] = ent(nc.semaphore(s))
        for st in self.streams:
            sems["D:" + st] = ent(nc.semaphore("D_" + st))
        with nc.Block() as block:
            hooks = {"te": block.tensor, "ve": block.vector, "ac": block.scalar,
                     "sy": block.sync, "gp": block.gpsimd}
            for e in ENGS:
                ops = self.ops[e]
                if not ops:
                    continue

                def body(eng, ops=ops, e=e):
                    for o in ops:
                        for s, v in o.waits:
                            eng.wait_ge(sems[s], v)
                        inst = o.fn(eng)
                        if o.dma:
                            inst.then_inc(sems["D:" + o.stream], 16)
                        elif o.needs_inc:
                            inst.then_inc(sems[SEM_OF[e]], 1)
                    tot = max((o.val for o in ops if not o.dma), default=0)
                    if tot > 0:
                        eng.wait_ge(sems[SEM_OF[e]], tot)
                    done = set()
                    for o in ops:
                        if o.dma and o.stream not in done:
                            done.add(o.stream)
                            eng.wait_ge(sems["D:" + o.stream],
                                        self.streams[o.stream][-1].val)
                hooks[e](body)


def build_program(L):
    assert L % 512 == 0
    NLB = L // 512
    NLT = L // 128
    NKT = L // 128
    NG = NKT // 2
    NQB = L // 512

    nc = bass.Bass("TRN2", target_bir_lowering=False, debug=False)
    ctx = []

    def ent(cm):
        v = cm.__enter__()
        ctx.append(cm)
        return v

    def din(name, shape, dt):
        return nc.dram_tensor(name, shape, dt, kind="ExternalInput")

    xb = din("xb", [L, D], F32)
    xbT = din("xbT", [128, NCT * L], F32)
    xres = din("xres", [L, FW], F32)
    wkT = din("wkT", [128, NCT * FW], F32)
    wvT = din("wvT", [128, NCT * FW], F32)
    kern = din("kern", [128, NCT * FW], F32)
    wq = din("wq", [128, 8 * D], F32)
    tcd = din("tcd", [128, 8 * 256], F32)
    i128 = din("i128", [128, 128], F32)
    ones128 = din("ones128", [128, 128], BF16)
    onesf = din("onesf", [1, 128], F32)
    i32t = din("i32t", [128, 32], F32)
    pn = din("pn", [16, 512], F32)
    gateT = din("gateT", [16, 4], F32)
    bwrow = din("bwrow", [1, 12], F32)
    bqc = din("bqc", [128, 8], F32)
    bkc = din("bkc", [128, 8], F32)
    bvrow = din("bvrow", [1, FW], F32)
    fbrow = din("fbrow", [1, FW], F32)
    lAnr = din("lAnr", [128, 4 * D], F32)
    lBnr = din("lBnr", [128, 4 * FW], F32)
    lAsT = din("lAsT", [128, 32 * 32], F32)
    lBsT = din("lBsT", [32, FW], F32)
    y_out = nc.dram_tensor("y", [L, FW], F32, kind="ExternalOutput")

    s_wqe = nc.dram_tensor("s_wqe", [128, NHC * NCT * 256], BF16)
    s_xnt = nc.dram_tensor("s_xnt", [NLB, 128, NCT * 512], BF16)
    s_kt = nc.dram_tensor("s_kt", [NHC, 2, 128, L], BF16)
    s_vv = nc.dram_tensor("s_vv", [NHC, NLT, 128, 256], BF16)
    s_qt = nc.dram_tensor("s_qt", [NHC, 2, 128, L], BF16)
    s_att = nc.dram_tensor("s_att", [NLT, 128, FW], BF16)
    s_aaug = nc.dram_tensor("s_aaug", [NHC, 128, NCT * 64], BF16)
    s_wvb = nc.dram_tensor("s_wvb", [128, NCT * FW], BF16)
    s_kernb = nc.dram_tensor("s_kernb", [128, NCT * FW], BF16)
    s_baug = nc.dram_tensor("s_baug", [NHC, 64, FW], BF16)

    PA = [ent(nc.psum_tensor("PA%d" % i, [128, 1024], F32)) for i in range(2)]
    PB = [ent(nc.psum_tensor("PB%d" % i, [128, 512], F32)) for i in range(4)]

    class Arena:
        def __init__(self):
            self.off = SB_BASE
            self.persist_end = None
            self.ranges = {}

        def alloc(self, name, shape, dt):
            per = int(np.prod(shape[1:])) * mybir.dt.size(dt)
            per = (per + 31) & ~31
            t = nc.alloc_sbuf_tensor_at(name, list(shape), dt, offset=self.off)
            self.ranges[id(t)] = (self.off, self.off + per)
            self.off += per
            assert self.off <= SB_BASE + 212800, (name, self.off)
            return t

        def mark_persist(self):
            self.persist_end = self.off

        def phase_reset(self):
            self.off = self.persist_end

    A = Arena()
    p_i128 = A.alloc("p_i128", [128, 128], F32)
    p_ones128 = A.alloc("p_ones128", [128, 128], BF16)
    p_onesf = A.alloc("p_onesf", [1, 128], F32)
    p_bqc = A.alloc("p_bqc", [128, 8], F32)
    p_bkc = A.alloc("p_bkc", [128, 8], F32)
    p_bqe = A.alloc("p_bqe", [128, 8], F32)
    p_bvbc = A.alloc("p_bvbc", [128, FW], F32)
    p_fbbc = A.alloc("p_fbbc", [128, FW], F32)
    p_negC = A.alloc("p_negC", [128, NHC], F32)
    A.mark_persist()

    S = Sched()

    A.phase_reset()
    t_tcd = A.alloc("t_tcd", [128, 8 * 256], F32)
    t_i32t = A.alloc("t_i32t", [128, 32], F32)
    t_pn = A.alloc("t_pn", [16, 512], F32)
    t_gateT = A.alloc("t_gateT", [16, 4], F32)
    t_bwrow = A.alloc("t_bwrow", [1, 12], F32)
    t_bvrow = A.alloc("t_bvrow", [1, FW], F32)
    t_fbrow = A.alloc("t_fbrow", [1, FW], F32)
    t_bwbc = A.alloc("t_bwbc", [128, 12], F32)
    t_gc = A.alloc("t_gc", [128, 16], F32)
    t_ttf = A.alloc("t_ttf", [128, 8 * 256], F32)
    t_ttb = A.alloc("t_ttb", [128, 8 * 256], BF16)
    t_ssb = A.alloc("t_ssb", [128, 16 * 32], BF16)
    t_lA = A.alloc("t_lA", [128, 4 * D], BF16)
    t_lB = A.alloc("t_lB", [128, 4 * FW], BF16)
    t_lAsT = A.alloc("t_lAsT", [128, 32 * 32], BF16)
    t_wqs = [A.alloc("t_wqs%d" % i, [128, 2 * D], BF16) for i in range(2)]
    t_wqeh = [A.alloc("t_wqeh%d" % i, [128, NCT * 256], BF16) for i in range(2)]
    t_aaug = A.alloc("t_aaug", [128, NCT * 64], BF16)
    t_baug = A.alloc("t_baug", [64, FW], BF16)

    for src, dst in ((i128, p_i128), (ones128, p_ones128), (lAnr, t_lA),
                     (lBnr, t_lB), (lAsT, t_lAsT)):
        S.dma("gp", "g_once", lambda e, s=src, d=dst: e.dma_start(out=d[:], in_=s[:]))
    gload = S.dma("gp", "g_once",
                  lambda e: e.dma_start(out=t_baug[32:64, :], in_=lBsT[:]))
    wv_pc = S.dma("gp", "g_pc0", lambda e: e.dma_start(out=s_wvb[:], in_=wvT[:]))
    for src, dst in ((tcd, t_tcd), (i32t, t_i32t), (pn, t_pn), (gateT, t_gateT),
                     (bwrow, t_bwrow), (onesf, p_onesf), (bqc, p_bqc),
                     (bkc, p_bkc), (bvrow, t_bvrow)):
        S.dma("sy", "s_once", lambda e, s=src, d=dst: e.dma_start(out=d[:], in_=s[:]))
    sload = S.dma("sy", "s_once",
                  lambda e: e.dma_start(out=t_fbrow[:], in_=fbrow[:]))

    bw_mm = S.op("te", lambda e: e.matmul(PB[0][:, 0:12], p_onesf[:], t_bwrow[:],
                                          start=True, stop=True), deps=[sload])
    bv_mm = [S.op("te", lambda e, c0=c0: e.matmul(
        PA[0][:, c0:c0 + 512], p_onesf[:], t_bvrow[:, c0:c0 + 512],
        start=True, stop=True), deps=[sload]) for c0 in (0, 512)]
    fb_mm = [S.op("te", lambda e, c0=c0: e.matmul(
        PA[1][:, c0:c0 + 512], p_onesf[:], t_fbrow[:, c0:c0 + 512],
        start=True, stop=True), deps=[sload]) for c0 in (0, 512)]
    gc_mm = []
    for g4 in range(4):
        for tt in range(4):
            c = g4 * 4 + tt
            gc_mm.append(S.op("te", lambda e, c=c, g4=g4, tt=tt: e.matmul(
                PB[1][:, c:c + 1], t_pn[:, tt * 128:(tt + 1) * 128],
                t_gateT[:, g4:g4 + 1], start=True, stop=True), deps=[sload]))
    ev_bw = S.op("ve", lambda e: e.tensor_copy(t_bwbc[:], PB[0][:, 0:12]), deps=[bw_mm])
    ev_bv = S.op("ve", lambda e: e.tensor_copy(p_bvbc[:], PA[0][:, 0:1024]), deps=bv_mm)
    ev_fb = S.op("ve", lambda e: e.tensor_copy(p_fbbc[:], PA[1][:, 0:1024]), deps=fb_mm)
    ev_gc = S.op("ve", lambda e: e.tensor_copy(t_gc[:], PB[1][:, 0:16]), deps=gc_mm)

    s_tiles = []
    for c in range(16):
        s_tiles.append(S.op("ve", lambda e, c=c: e.tensor_scalar(
            t_ssb[:, c * 32:(c + 1) * 32], t_i32t[:], t_gc[:, c:c + 1], None,
            ALU.mult), deps=[ev_gc, sload]))

    tt_done = []
    for h in range(NHC):
        for st in range(2):
            o = (h * 2 + st) * 256
            tds = t_tcd[:, st * 256:(st + 1) * 256]
            tc0 = t_tcd[:, (2 + st) * 256:(3 + st) * 256]
            tc1 = t_tcd[:, (4 + st) * 256:(5 + st) * 256]
            tc2 = t_tcd[:, (6 + st) * 256:(7 + st) * 256]
            d1 = S.op("ve", lambda e, o=o, h=h, tc0=tc0, tds=tds: e.scalar_tensor_tensor(
                t_ttf[:, o:o + 256], tc0, t_bwbc[:, h * 3:h * 3 + 1], tds,
                ALU.mult, ALU.add), deps=[sload, ev_bw], hazard=True)
            d2 = S.op("ve", lambda e, o=o, h=h, tc1=tc1: e.scalar_tensor_tensor(
                t_ttf[:, o:o + 256], tc1, t_bwbc[:, h * 3 + 1:h * 3 + 2],
                t_ttf[:, o:o + 256], ALU.mult, ALU.add), deps=[d1], hazard=True)
            d3 = S.op("ve", lambda e, o=o, h=h, tc2=tc2: e.scalar_tensor_tensor(
                t_ttf[:, o:o + 256], tc2, t_bwbc[:, h * 3 + 2:h * 3 + 3],
                t_ttf[:, o:o + 256], ALU.mult, ALU.add), deps=[d2], hazard=True)
            tt_done.append(d3)
    tt_cast = S.op("ve", lambda e: e.tensor_copy(t_ttb[:], t_ttf[:]), deps=tt_done,
                   hazard=True)

    bqe_mm = []
    for h in range(NHC):
        for rs in range(2):
            col = h * 2 + rs
            mm = None
            for st in range(2):
                mm = S.op("te", lambda e, h=h, rs=rs, st=st, col=col: e.matmul(
                    PB[2][:, col:col + 1],
                    t_ttf[:, (h * 2 + st) * 256 + rs * 128:
                          (h * 2 + st) * 256 + rs * 128 + 128],
                    p_bqc[:, h * 2 + st:h * 2 + st + 1],
                    start=(st == 0), stop=(st == 1)),
                    deps=(tt_done + [sload] if st == 0 else [mm]))
            bqe_mm.append(mm)
    ev_bqe = S.op("ve", lambda e: e.tensor_copy(p_bqe[:], PB[2][:, 0:8]), deps=bqe_mm)

    ab_last = None
    st_a_prev = None
    for g4 in range(4):
        ae_ev = {}
        for isl in range(32):
            mm = None
            for tt in range(4):
                first = (tt == 0)
                dpm = [gload, s_tiles[g4 * 4 + tt]]
                if first and isl >= 8:
                    dpm.append(ae_ev[isl - 8])
                mm = S.op("te", lambda e, g4=g4, isl=isl, tt=tt: e.matmul(
                    PB[3][:, (isl % 8) * 32:(isl % 8) * 32 + 32],
                    t_lA[:, tt * D + isl * 128:tt * D + (isl + 1) * 128],
                    t_ssb[:, (g4 * 4 + tt) * 32:(g4 * 4 + tt + 1) * 32],
                    start=(tt == 0), stop=(tt == 3)),
                    deps=(dpm if first else [mm]))
            if isl % 8 == 7:
                ev = S.op("ve", lambda e, isl=isl: e.tensor_copy(
                    t_aaug[:].rearrange("p (i r) -> p i r", i=NCT)
                    [:, isl - 7:isl + 1, 0:32],
                    PB[3][:, 0:256].rearrange("p (i r) -> p i r", i=8)),
                    deps=[mm] + ([st_a_prev] if isl == 7 else []))
                for k in range(isl - 7, isl + 1):
                    ae_ev[k] = ev
        sh_cp = S.op("ve", lambda e: e.tensor_copy(
            t_aaug[:].rearrange("p (i r) -> p i r", i=NCT)[:, :, 32:64],
            t_lAsT[:].rearrange("p (i r) -> p i r", i=NCT)),
            deps=[gload] + ([ab_last] if ab_last else []))
        st_a = S.dma("gp", "g_ab", lambda e, g4=g4: e.dma_start(
            out=s_aaug[g4], in_=t_aaug[:]), deps=[ae_ev[31], sh_cp])
        st_a_prev = st_a
        be_ev = []
        for oc in range(2):
            mm = None
            for tt in range(4):
                first = (tt == 0)
                dpm = [gload, s_tiles[g4 * 4 + tt], ev_bqe]
                if first and be_ev:
                    dpm.append(be_ev[-1])
                mm = S.op("te", lambda e, g4=g4, oc=oc, tt=tt: e.matmul(
                    PB[2][0:32, 0:512],
                    t_ssb[:, (g4 * 4 + tt) * 32:(g4 * 4 + tt + 1) * 32],
                    t_lB[:, tt * FW + oc * 512:tt * FW + oc * 512 + 512],
                    start=(tt == 0), stop=(tt == 3)),
                    deps=(dpm if first else [mm]))
            be_ev.append(S.op("ve", lambda e, oc=oc: e.tensor_copy(
                t_baug[0:32, oc * 512:oc * 512 + 512], PB[2][0:32, 0:512]),
                deps=[mm]))
        ab_last = S.dma("gp", "g_ab", lambda e, g4=g4: e.dma_start(
            out=s_baug[g4], in_=t_baug[:]), deps=be_ev + [gload, st_a])

    wq_ld = []
    for h in range(NHC):
        wq_ld.append(S.dma("gp", "g_wq%d" % (h % 2), lambda e, h=h: e.dma_start(
            out=t_wqs[h % 2][:], in_=wq[:, h * 2 * D:(h + 1) * 2 * D])))
    fold_done = []
    prev_evs_last = None
    for h in range(NHC):
        evs = []
        for ct in range(NCT):
            slot = ct % 4
            pa = PA[0] if slot < 2 else PA[1]
            lo = (slot % 2) * 512
            mm = None
            for st in range(2):
                first = (st == 0)
                dpm = [wq_ld[h], tt_cast]
                if first and ct < 4:
                    dpm += [ev_bv, ev_fb, prev_evs_last]
                if first and ct >= 4:
                    dpm.append(evs[ct - 4])
                if first and ct == 0 and h >= 2:
                    dpm.append(fold_done[h - 2])
                mm = S.op("te", lambda e, h=h, ct=ct, st=st, pa=pa, lo=lo: e.matmul(
                    pa[:, lo:lo + 256],
                    t_wqs[h % 2][:, st * D + ct * 128:st * D + (ct + 1) * 128],
                    t_ttb[:, (h * 2 + st) * 256:(h * 2 + st + 1) * 256],
                    start=(st == 0), stop=(st == 1)),
                    deps=(dpm if first else [mm]))
            evs.append(S.op("ve", lambda e, h=h, ct=ct, pa=pa, lo=lo: e.tensor_copy(
                t_wqeh[h % 2][:, ct * 256:(ct + 1) * 256], pa[:, lo:lo + 256]),
                deps=[mm]))
        stw = S.dma("sy", "s_wqe%d" % (h % 2), lambda e, h=h: e.dma_start(
            out=s_wqe[:, h * NCT * 256:(h + 1) * NCT * 256], in_=t_wqeh[h % 2][:]),
            deps=[evs[-1]])
        fold_done.append(stw)
        prev_evs_last = evs[-1]
        if h + 2 < NHC:
            wq_ld[h + 2].deps.append(evs[-1])

    p0_done = [ab_last, fold_done[-1], fold_done[-2], ev_bv, ev_fb, ev_bqe,
               gload, sload]

    A.phase_reset()
    t_wkT = A.alloc("t_wkT", [128, NCT * FW], BF16)
    t_xt = [A.alloc("t_xt%d" % i, [128, D], BF16) for i in range(2)]
    t_xtb = [A.alloc("t_xtb%d" % i, [128, 16 * 512], BF16) for i in range(2)]
    t_xnt = [A.alloc("t_xnt%d" % i, [128, NCT * 512], BF16) for i in range(2)]
    t_stk = [A.alloc("t_stk0", [128, 8 * 512], BF16)] * 2
    t_tmp = A.alloc("t_tmp", [128, 2 * 512], F32)
    t_bns = [A.alloc("t_bns%d" % i, [128, 48], F32) for i in range(4)]
    t_mv = [A.alloc("t_mv%d" % i, [128, 2], F32) for i in range(4)]
    t_lnv = [A.alloc("t_lnv%d" % i, [128, 1], F32) for i in range(4)]
    t_veps = [A.alloc("t_veps%d" % i, [128, 1], F32) for i in range(4)]
    t_mrc = [A.alloc("t_mrc%d" % i, [128, 8], F32) for i in range(2)]
    t_mrow = [A.alloc("t_mrow%d" % i, [1, 512], F32) for i in range(2)]
    t_rrow = [A.alloc("t_rrow%d" % i, [1, 512], F32) for i in range(2)]

    wk_ld = S.dma("gp", "g_wk", lambda e: e.dma_start(out=t_wkT[:], in_=wkT[:]),
                  deps=p0_done)
    xbTv = xbT[:].rearrange("p (c l) -> p c l", c=NCT)

    xload = [None] * NLT
    bn_last = [None] * NLT
    stat_done = [None] * NLT
    stats_ring = {}
    kmm_last = {}
    kevac = {}
    kdma = {}
    xnt_st = {}
    tt_last = {}
    tth_last = {}
    bc_ops = {}
    ev_mr = {}

    def p1_stats(lt):
        lbpar = (lt // 4) % 2
        par4 = lt % 4
        slot = lt % 2
        t4 = lt % 4
        dpx = list(p0_done) if lt < 2 else [bn_last[lt - 2]]
        xload[lt] = S.dma("gp", "g_x%d" % slot, lambda e, lt=lt, slot=slot: e.dma_start(
            out=t_xt[slot][:], in_=xb[lt * 128:(lt + 1) * 128, :]), deps=dpx)
        pv = stats_ring.get(lt - 4, {})
        bns = []
        for j2 in range(8):
            bns.append(S.op("ve", lambda e, slot=slot, j2=j2, par4=par4: e.bn_stats(
                t_bns[par4][:, j2 * 6:(j2 + 1) * 6],
                t_xt[slot][:, j2 * 512:(j2 + 1) * 512]),
                deps=[xload[lt]] + ([pv.get("aggr")] if j2 == 0 else []),
                hazard=True))
        bn_last[lt] = bns[-1]
        aggr = S.op("ve", lambda e, par4=par4: e.bn_aggr(
            t_mv[par4][:, 0:2], t_bns[par4][:]),
            deps=bns + [pv.get("veps"), pv.get("negm")], hazard=True)
        veps = S.op("ve", lambda e, par4=par4: e.tensor_scalar_add(
            t_veps[par4][:], t_mv[par4][:, 1:2], EPS),
            deps=[aggr, pv.get("lnv")], hazard=True)
        lnv = S.op("ac", lambda e, par4=par4: e.activation(
            t_lnv[par4][:], t_veps[par4][:], AF.Ln),
            deps=[veps, pv.get("rstd")], hazard=True)
        wdep = [lnv]
        if lt >= 8:
            wdep.append(bc_ops[lt // 4 - 2])
        rstd = S.op("ac", lambda e, par4=par4, lbpar=lbpar, t4=t4: e.activation(
            t_mrc[lbpar][:, 4 + t4:5 + t4], t_lnv[par4][:], AF.Exp, scale=-0.5),
            deps=wdep, hazard=True)
        negm = S.op("ve", lambda e, par4=par4, lbpar=lbpar, t4=t4: e.tensor_scalar_mul(
            t_mrc[lbpar][:, t4:t4 + 1], t_mv[par4][:, 0:1], -1.0),
            deps=[aggr] + ([bc_ops[lt // 4 - 2]] if lt >= 8 else []), hazard=True)
        stat_done[lt] = (rstd, negm)
        stats_ring[lt] = {"aggr": aggr, "veps": veps, "lnv": lnv,
                          "rstd": rstd, "negm": negm}

    def p1_mr_tr(lblk):
        lbpar = lblk % 2
        deps = [op for lt in range(lblk * 4, lblk * 4 + 4)
                for op in stat_done[lt]]
        if lblk >= 1:
            deps.append(ev_mr[lblk - 1])
        trs = []
        for j2 in range(8):
            pbx = PB[2 + j2 // 4]
            trs.append(S.op("te", lambda e, j2=j2, lbpar=lbpar, pbx=pbx: e.transpose(
                pbx[0:1, (j2 % 4) * 128:(j2 % 4) * 128 + 128],
                t_mrc[lbpar][:, j2:j2 + 1], p_i128[:]),
                deps=(deps if j2 == 0 else [])))
        edep = [trs[-1]]
        if lblk >= 2:
            edep.append(bc_ops[lblk - 2])
        e1 = S.op("ve", lambda e, lbpar=lbpar: e.tensor_copy(
            t_mrow[lbpar][:], PB[2][0:1, 0:512]), deps=edep)
        e2 = S.op("ve", lambda e, lbpar=lbpar: e.tensor_copy(
            t_rrow[lbpar][:], PB[3][0:1, 0:512]), deps=[trs[-1], e1])
        ev_mr[lblk] = e2

    def p1_mr_bc(lblk):
        lbpar = lblk % 2
        bdep = [ev_mr[lblk]]
        if lblk >= 2:
            bdep.append(tt_last[lblk - 2])
        b1 = S.op("te", lambda e, lbpar=lbpar: e.matmul(
            PA[lbpar][:, 0:512], p_onesf[:], t_mrow[lbpar][:],
            start=True, stop=True), deps=bdep)
        b2 = S.op("te", lambda e, lbpar=lbpar: e.matmul(
            PA[lbpar][:, 512:1024], p_onesf[:], t_rrow[lbpar][:],
            start=True, stop=True), deps=[b1])
        bc_ops[lblk] = b2

    def p1_norm(lblk):
        lbpar = lblk % 2
        prev = tt_last.get(lblk - 1)
        for half in range(2):
            hdep = [tth_last[(lblk - 1, half)]] if lblk >= 1 else []
            xl = S.dma("gp", "g_xt%d" % half, lambda e, lblk=lblk, half=half: e.dma_start(
                out=t_xtb[half][:],
                in_=xbTv[:, half * 16:(half + 1) * 16,
                         lblk * 512:(lblk + 1) * 512]),
                deps=hdep)
            for q in range(8):
                o1 = None
                for k in range(2):
                    c16 = q * 2 + k
                    d1 = [xl, bc_ops[lblk]]
                    if lblk >= 2 and half == 0 and c16 == 0:
                        d1 += [kmm_last[(lblk - 2, 7)], xnt_st[lblk - 2]]
                    if k == 0 and prev is not None:
                        d1.append(prev)
                    o1 = S.op("ve", lambda e, c16=c16, k=k, lbpar=lbpar, half=half: e.tensor_tensor(
                        t_tmp[:, k * 512:(k + 1) * 512],
                        t_xtb[half][:, c16 * 512:(c16 + 1) * 512],
                        PA[lbpar][:, 0:512], ALU.add),
                        deps=d1, hazard=(k == 1))
                for k in range(2):
                    ct = half * 16 + q * 2 + k
                    prev = S.op("ve", lambda e, ct=ct, k=k, lbpar=lbpar: e.tensor_tensor(
                        t_xnt[lbpar][:, ct * 512:(ct + 1) * 512],
                        t_tmp[:, k * 512:(k + 1) * 512],
                        PA[lbpar][:, 512:1024], ALU.mult),
                        deps=[o1], hazard=(k == 1))
            tth_last[(lblk, half)] = prev
        tt_last[lblk] = prev
        xnt_st[lblk] = S.dma("sy", "s_xnt%d" % lbpar,
                             lambda e, lblk=lblk, lbpar=lbpar: e.dma_start(
            out=s_xnt[lblk], in_=t_xnt[lbpar][:]), deps=[prev])

    def p1_kt(lblk):
        bpar = lblk % 2
        last_tt = tt_last[lblk]
        for ft in range(8):
            pb = PB[ft % 2]
            mm = None
            for ct in range(NCT):
                first = (ct == 0)
                dpm = [last_tt, wk_ld]
                if first and ft >= 2:
                    dpm.append(kevac[(lblk, ft - 2)])
                if first and ft < 2 and lblk >= 1:
                    dpm.append(kevac[(lblk - 1, 6 + ft)])
                mm = S.op("te", lambda e, ft=ft, ct=ct, pb=pb, bpar=bpar: e.matmul(
                    pb[:, 0:512],
                    t_wkT[:, ct * FW + ft * 128:ct * FW + (ft + 1) * 128],
                    t_xnt[bpar][:, ct * 512:(ct + 1) * 512],
                    start=(ct == 0), stop=(ct == NCT - 1)),
                    deps=(dpm if first else [mm]))
            kmm_last[(lblk, ft)] = mm
            edp = [mm]
            if lblk >= 1:
                edp.append(kdma[(lblk - 1, 7)])
            kevac[(lblk, ft)] = S.op("ac", lambda e, ft=ft, pb=pb, bpar=bpar: e.activation(
                t_stk[bpar][:, ft * 512:(ft + 1) * 512], pb[:, 0:512],
                AF.Identity, bias=p_bkc[:, ft:ft + 1]), deps=edp)
            kdma[(lblk, ft)] = S.dma("sy", "s_kd%d" % bpar,
                                     lambda e, lblk=lblk, ft=ft, bpar=bpar: e.dma_start(
                out=s_kt[ft // 2, ft % 2, :, lblk * 512:(lblk + 1) * 512],
                in_=t_stk[bpar][:, ft * 512:(ft + 1) * 512]),
                deps=[kevac[(lblk, ft)]])

    for lt in range(min(8, NLT)):
        p1_stats(lt)
    p1_mr_tr(0)
    p1_mr_bc(0)
    for lblk in range(NLB):
        if lblk + 1 < NLB:
            p1_mr_tr(lblk + 1)
        for lt in range(4 * (lblk + 2), min(4 * (lblk + 3), NLT)):
            p1_stats(lt)
        if lblk >= 1:
            p1_kt(lblk - 1)
        if lblk + 1 < NLB:
            p1_mr_bc(lblk + 1)
        p1_norm(lblk)
    p1_kt(NLB - 1)
    p1_tail = [kdma[(NLB - 1, 7)], kdma[(NLB - 2, 7)],
               xnt_st[NLB - 1], xnt_st[NLB - 2]]
    kern_pc = S.dma("gp", "g_pc1", lambda e: e.dma_start(out=s_kernb[:], in_=kern[:]))

    A.phase_reset()
    t_wvT = A.alloc("t_wvT", [128, NCT * FW], BF16)
    t_xv = [A.alloc("t_xv%d" % i, [128, NCT * 512], BF16) for i in range(2)]
    t_stv = [A.alloc("t_stv%d" % i, [128, FW], BF16) for i in range(2)]

    lev = (NLB - 1) if (NLB - 1) % 2 == 0 else (NLB - 2)
    lod = (NLB - 1) if (NLB - 1) % 2 == 1 else (NLB - 2)
    p1_guards = [
        (A.ranges[id(t_wkT)], [kmm_last[(NLB - 1, 7)]]),
        (A.ranges[id(t_xt[0])], [bn_last[NLT - 1], bn_last[NLT - 2]]),
        (A.ranges[id(t_xt[1])], [bn_last[NLT - 1], bn_last[NLT - 2]]),
        (A.ranges[id(t_xtb[0])], [tt_last[NLB - 1]]),
        (A.ranges[id(t_xtb[1])], [tt_last[NLB - 1]]),
        (A.ranges[id(t_xnt[0])], [kmm_last[(lev, 7)], xnt_st[lev]]),
        (A.ranges[id(t_xnt[1])], [kmm_last[(lod, 7)], xnt_st[lod]]),
        (A.ranges[id(t_stk[0])], [kdma[(NLB - 1, 7)], kdma[(NLB - 2, 7)]]),
        (A.ranges[id(t_tmp)], [tt_last[NLB - 1]]),
        (A.ranges[id(t_bns[0])], [tt_last[NLB - 1], stat_done[NLT - 1][0],
                                  stat_done[NLT - 1][1]]),
    ]

    def p1_guard(t):
        lo, hi = A.ranges[id(t)]
        deps = []
        for (glo, ghi), ds2 in p1_guards:
            if glo < hi and lo < ghi:
                deps += ds2
        if hi > A.ranges[id(t_bns[0])][0]:
            deps += [tt_last[NLB - 1], stat_done[NLT - 1][0], stat_done[NLT - 1][1],
                     ev_mr[NLB - 1], bc_ops[NLB - 1]]
        return deps

    wv_ld = S.dma("gp", "g_wv", lambda e: e.dma_start(out=t_wvT[:], in_=s_wvb[:]),
                  deps=p1_guard(t_wvT) + [wv_pc])
    xv_ld = {}
    vmm_last = {}
    vevac = {}
    vdma = {}
    for lblk in range(NLB):
        par = lblk % 2
        dp = (p1_guard(t_xv[par]) if lblk < 2 else [vmm_last[(lblk - 2, 3)]]) + \
            [xnt_st[lblk]]
        xv_ld[lblk] = S.dma("sy", "s_xv%d" % par, lambda e, lblk=lblk, par=par: e.dma_start(
            out=t_xv[par][:], in_=s_xnt[lblk]), deps=dp)
        for mt in range(4):
            pa = PA[mt % 2]
            mm = None
            for oc in range(2):
                for ct in range(NCT):
                    first = (oc == 0 and ct == 0)
                    dpm = [xv_ld[lblk], wv_ld]
                    if first:
                        key = (lblk, mt - 2) if mt >= 2 else (lblk - 1, mt + 2)
                        dpm.append(vevac.get(key))
                    mm = S.op("te", lambda e, mt=mt, oc=oc, ct=ct, pa=pa, par=par: e.matmul(
                        pa[:, oc * 512:(oc + 1) * 512],
                        t_xv[par][:, ct * 512 + mt * 128:ct * 512 + (mt + 1) * 128],
                        t_wvT[:, ct * FW + oc * 512:ct * FW + (oc + 1) * 512],
                        start=(ct == 0), stop=(ct == NCT - 1)),
                        deps=(dpm if first else [mm]))
            vmm_last[(lblk, mt)] = mm
            edp = [mm, ev_bv]
            key = (lblk, mt - 2) if mt >= 2 else (lblk - 1, mt + 2)
            if key in vdma:
                edp.append(vdma[key][-1])
            elif lblk == 0:
                edp += p1_guard(t_stv[mt % 2])
            vevac[(lblk, mt)] = S.op("ve", lambda e, mt=mt, pa=pa: e.tensor_tensor(
                t_stv[mt % 2][:], pa[:, 0:1024], p_bvbc[:], ALU.add), deps=edp)
            vdma[(lblk, mt)] = [S.dma("gp", "g_vd%d" % (mt % 2),
                                      lambda e, lblk=lblk, mt=mt, h=h: e.dma_start(
                out=s_vv[h, lblk * 4 + mt],
                in_=t_stv[mt % 2][:, h * 256:(h + 1) * 256]),
                deps=[vevac[(lblk, mt)]]) for h in range(NHC)]
    p2v_tail = [vdma[(NLB - 1, 2)][-1], vdma[(NLB - 1, 3)][-1]]

    A.phase_reset()
    t_wqe = A.alloc("t_wqe", [128, NHC * NCT * 256], BF16)
    t_xq = [A.alloc("t_xq%d" % i, [128, NCT * 512], BF16) for i in range(2)]
    t_stq = [A.alloc("t_stq%d" % i, [128, 512], BF16) for i in range(2)]

    def p2v_guard(t):
        lo, hi = A.ranges[id(t)]
        deps = []
        for tt2, ds2 in ((t_wvT, [vmm_last[(NLB - 1, 3)]]),
                         (t_xv[0], [vmm_last[(NLB - 1, 3)]]),
                         (t_xv[1], [vmm_last[(NLB - 1, 3)]]),
                         (t_stv[0], [vdma[(NLB - 1, 2)][-1]]),
                         (t_stv[1], [vdma[(NLB - 1, 3)][-1]])):
            glo, ghi = A.ranges[id(tt2)]
            if glo < hi and lo < ghi:
                deps += ds2
        if hi > A.ranges[id(t_stv[1])][1]:
            deps += p2v_tail
        return deps

    wqe_ld = S.dma("sy", "s_wqeld", lambda e: e.dma_start(out=t_wqe[:], in_=s_wqe[:]),
                   deps=p2v_guard(t_wqe) + [fold_done[-1], fold_done[-2]])
    xq_ld = {}
    qmm_last = {}
    qevac = {}
    qdma = {}
    for lblk in range(NLB):
        par = lblk % 2
        dp = (p2v_guard(t_xq[par]) if lblk < 2 else [qmm_last[(lblk - 2, 7)]]) + \
            [xnt_st[lblk]]
        xq_ld[lblk] = S.dma("sy", "s_xq%d" % par, lambda e, lblk=lblk, par=par: e.dma_start(
            out=t_xq[par][:], in_=s_xnt[lblk]), deps=dp)
        for rt in range(8):
            h, rs = rt // 2, rt % 2
            pb = PB[rt % 2]
            mm = None
            for ct in range(NCT):
                first = (ct == 0)
                dpm = [xq_ld[lblk], wqe_ld]
                if first:
                    key = (lblk, rt - 2) if rt >= 2 else (lblk - 1, rt + 6)
                    dpm.append(qevac.get(key))
                mm = S.op("te", lambda e, h=h, rs=rs, ct=ct, pb=pb, par=par: e.matmul(
                    pb[:, 0:512],
                    t_wqe[:, (h * NCT + ct) * 256 + rs * 128:
                          (h * NCT + ct) * 256 + (rs + 1) * 128],
                    t_xq[par][:, ct * 512:(ct + 1) * 512],
                    start=(ct == 0), stop=(ct == NCT - 1)),
                    deps=(dpm if first else [mm]))
            qmm_last[(lblk, rt)] = mm
            edp = [mm, ev_bqe]
            key = (lblk, rt - 2) if rt >= 2 else (lblk - 1, rt + 6)
            edp.append(qdma.get(key))
            if key not in qdma and lblk == 0:
                edp += p2v_guard(t_stq[rt % 2])
            qevac[(lblk, rt)] = S.op("ac", lambda e, rt=rt, pb=pb: e.activation(
                t_stq[rt % 2][:], pb[:, 0:512], AF.Identity,
                bias=p_bqe[:, rt:rt + 1]), deps=edp)
            qdma[(lblk, rt)] = S.dma("gp", "g_qd%d" % (rt % 2),
                                     lambda e, lblk=lblk, rt=rt: e.dma_start(
                out=s_qt[rt // 2, rt % 2, :, lblk * 512:(lblk + 1) * 512],
                in_=t_stq[rt % 2][:]), deps=[qevac[(lblk, rt)]])
    p2_tail = [qdma[(NLB - 1, 6)], qdma[(NLB - 1, 7)]]

    A.phase_reset()
    t_vs2 = [A.alloc("t_vs%d" % i, [128, NLT * 257], BF16) for i in range(2)]
    t_sqpe = A.alloc("t_sqpe", [128, 4 * L], BF16)
    t_sq = t_sqpe
    t_mx = A.alloc("t_mx", [128, 40], F32)
    t_sums = A.alloc("t_sums", [128, 4], F32)
    t_rec = A.alloc("t_rec", [128, 4], F32)
    t_sta = [A.alloc("t_sta%d" % i, [128, NHC * 256], BF16) for i in range(2)]
    t_kt = A.alloc("t_kt", [128, 2 * L], BF16)
    t_qt = A.alloc("t_qt", [128, 2 * L], BF16)
    q_ext = A.ranges[id(t_stq[1])][1]
    v_ext = A.ranges[id(t_stv[1])][1]
    assert A.ranges[id(t_kt)][0] >= max(q_ext, v_ext), (
        A.ranges[id(t_kt)], q_ext, v_ext)

    q_last = qmm_last[(NLB - 1, 7)]
    vs_memset = [S.op("ve", lambda e, i=i: e.memset(
        t_vs2[i][:].rearrange("p (m f) -> p m f", m=NLT)[:, :, 256:257], 1.0),
        deps=[q_last]) for i in range(2)]

    exp_ops = []
    attn_ev = {}
    attn_dma = {}
    head_explast = {}
    head_avlast = {}
    avdone = {}
    avof = {}
    expof = {}
    explast = {}
    pend = [None]

    CG = NG // 2
    nch2 = 2 * NQB

    def pexp_ap(gK, gc, sl=None):
        base = (gK % 2) * 2 * L + gc * 1024
        if sl is None:
            return t_sqpe[:, base:base + 1024]
        return t_sqpe[:, base + sl[0]:base + sl[1]]

    def emit_denoms(hp, lb, avl):
        cps = []
        for ls in range(4):
            cps.append(S.op("ve", lambda e, ls=ls: e.tensor_copy(
                t_sums[:, ls:ls + 1], PB[ls][:, 256:257]), deps=[avl]))
        rec = S.op("ve", lambda e: e.reciprocal(t_rec[:], t_sums[:]),
                   deps=cps, hazard=True)
        spar = lb % 2
        evs = []
        for ls in range(4):
            edp2 = [rec, avl]
            wkey = (hp, lb - 2) if lb >= 2 else (hp - 1, NQB - 2 + lb)
            if wkey in attn_dma:
                edp2.append(attn_dma[wkey][-1])
            evs.append(S.op("ve", lambda e, ls=ls, spar=spar: e.tensor_scalar_mul(
                t_sta[spar][:, ls * 256:(ls + 1) * 256],
                PB[ls][:, 0:256], t_rec[:, ls:ls + 1]),
                deps=edp2))
        attn_ev[(hp, lb)] = evs
        attn_dma[(hp, lb)] = [S.dma("gp", "g_att%d" % spar,
                                    lambda e, hp=hp, lb=lb, ls=ls, spar=spar: e.dma_start(
            out=s_att[lb * 4 + ls, :, hp * 256:(hp + 1) * 256],
            in_=t_sta[spar][:, ls * 256:(ls + 1) * 256]),
            deps=[evs[ls]]) for ls in range(4)]

    def emit_av_group(gc, lds_cur):
        hp, Xp, gKp, t_vsp = pend[0]
        lbp, cp_ = Xp // 2, Xp % 2
        gp = cp_ * CG + gc
        avdp0 = [expof[(gKp, gc)], lds_cur]
        if gc == 0 and cp_ == 0:
            prev_ev = (attn_ev[(hp, lbp - 1)] if lbp >= 1 else
                       (attn_ev[(hp - 1, NQB - 1)] if hp >= 1 else []))
            avdp0 += prev_ev
        avmm = None
        for kq in range(2):
            kt = 2 * gp + kq
            for ls in range(4):
                first = (kq == 0 and ls == 0)
                avmm = S.op("te", lambda e, gKp=gKp, gc=gc, kq=kq, ls=ls, kt=kt, gp=gp, cp_=cp_, t_vsp=t_vsp: e.matmul(
                    PB[ls][:, 0:257],
                    pexp_ap(gKp, gc, (kq * 512 + ls * 128,
                                      kq * 512 + (ls + 1) * 128)),
                    t_vsp[:, kt * 257:(kt + 1) * 257],
                    start=(cp_ == 0 and gp == 0 and kq == 0),
                    stop=(cp_ == 1 and gp == NG - 1 and kq == 1)),
                    deps=(avdp0 if first else [avmm]))
        avdone[gKp] = avmm
        avof[(gKp, gc)] = avmm
        if gc == CG - 1:
            if cp_ == 1:
                emit_denoms(hp, lbp, avmm)
            if Xp == nch2 - 1:
                head_avlast[hp] = avmm
            pend[0] = None
        return avmm

    kt_done = {}
    qt_done = {}
    smm_of = {}
    smm_head_last = {}
    LH = L // 2
    for h in range(NHC):
        t_vs = t_vs2[h % 2]
        dpv = [vs_memset[h % 2]] + \
            ([head_avlast[h - 2]] if h >= 2 else [q_last] + p2v_tail)
        lds = None
        for mt8 in range(0, NLT, 8):
            lds = S.dma("sy", "s_h3", lambda e, h=h, mt8=mt8, t_vs=t_vs: e.dma_start(
                out=t_vs[:].rearrange("p (m f) -> p m f", m=NLT)
                [:, mt8:mt8 + 8, 0:256],
                in_=s_vv[h, mt8:mt8 + 8].rearrange("m p f -> p m f")),
                deps=(dpv if mt8 == 0 else []))
        for ch in range(2):
            dpk = [smm_head_last[h - 1]] if h >= 1 else (
                p1_guard(t_kt) + [kdma[(NLB - 1, 7)], kdma[(NLB - 2, 7)]])
            kt_done[(h, ch)] = [S.dma("ac", "a_k3",
                                      lambda e, h=h, rs=rs, ch=ch: e.dma_start(
                out=t_kt[:, rs * L + ch * LH:rs * L + (ch + 1) * LH],
                in_=s_kt[h, rs, :, ch * LH:(ch + 1) * LH]),
                deps=(dpk if ch == 0 and rs == 0 else []))
                for rs in range(2)]
        for lb in range(NQB):
            if h == 0:
                dq0 = [qdma[(lb, 0)], qdma[(lb, 1)]]
                if lb == 0:
                    dq0 = dq0 + p1_guard(t_qt)
                dq1 = dq0
            else:
                dq0 = [smm_of[(h - 1, 2 * lb + 1)]]
                dq1 = dq0
            qt_done[(h, lb)] = [
                S.dma("sy", "s_q3", lambda e, h=h, lb=lb: e.dma_start(
                    out=t_qt[:, lb * 512:(lb + 1) * 512],
                    in_=s_qt[h, 0, :, lb * 512:(lb + 1) * 512]), deps=dq0),
                S.dma("gp", "g_q3", lambda e, h=h, lb=lb: e.dma_start(
                    out=t_qt[:, L + lb * 512:L + (lb + 1) * 512],
                    in_=s_qt[h, 1, :, lb * 512:(lb + 1) * 512]), deps=dq1)]
        ktq_all = (kt_done[(h, 0)] + kt_done[(h, 1)] +
                   qt_done[(h, NQB - 1)])

        gK0 = h * nch2
        fms = []
        norm_red_last = {}
        sq_mm_last = None
        for side, (src2, base) in enumerate(((t_qt, 0), (t_kt, 16))):
            muls = []
            for ct in range(2):
                dpm = list(ktq_all)
                if h == 0 and side == 0 and ct == 0:
                    dpm += [qmm_last[(NLB - 1, 7)], qmm_last[(NLB - 2, 7)]]
                    dpm += p2_tail
                if h >= 1:
                    dpm.append(avdone[gK0 - 2])
                if side == 1:
                    dpm.append(sq_mm_last)
                muls.append(S.op("ve", lambda e, src2=src2, ct=ct: e.tensor_mul(
                    t_sq[:, ct * L:(ct + 1) * L], src2[:, ct * L:(ct + 1) * L],
                    src2[:, ct * L:(ct + 1) * L]), deps=dpm))
            nch = L // 512
            pan = PA[side]
            reds = []
            for ch in range(nch):
                lo = (ch % 2) * 512
                mm = None
                for ct in range(2):
                    first = (ct == 0)
                    dpm = muls + [gload]
                    if first and ch >= 2:
                        dpm.append(reds[ch - 2])
                    if first and ch < 2 and h >= 1:
                        dpm.append(explast[gK0 - 1])
                    mm = S.op("te", lambda e, ch=ch, ct=ct, pan=pan, lo=lo: e.matmul(
                        pan[:, lo:lo + 512], p_ones128[:],
                        t_sq[:, ct * L + ch * 512:ct * L + (ch + 1) * 512],
                        start=(ct == 0), stop=(ct == 1)),
                        deps=(dpm if first else [mm]))
                sq_mm_last = mm
                reds.append(S.op("ve", lambda e, ch=ch, base=base, pan=pan, lo=lo: e.reduce_max(
                    t_mx[:, base + ch:base + ch + 1], pan[:, lo:lo + 512], axis=AX.X),
                    deps=[mm], hazard=True))
            norm_red_last[side] = reds[-1]
            fms.append(S.op("ve", lambda e, base=base, side=side, nch=nch: e.reduce_max(
                t_mx[:, 32 + side:33 + side], t_mx[:, base:base + nch], axis=AX.X),
                deps=reds, hazard=True))
        prod = S.op("ve", lambda e: e.tensor_mul(
            t_mx[:, 34:35], t_mx[:, 32:33], t_mx[:, 33:34]), deps=fms, hazard=True)
        lnc = S.op("ac", lambda e: e.activation(
            t_mx[:, 35:36], t_mx[:, 34:35], AF.Ln),
            deps=[prod] + (p2_tail if h == 0 else []), hazard=True)
        cexp = S.op("ac", lambda e: e.activation(
            t_mx[:, 36:37], t_mx[:, 35:36], AF.Exp, scale=0.5),
            deps=[lnc], hazard=True)
        negc = S.op("ve", lambda e, h=h: e.tensor_scalar(
            p_negC[:, h:h + 1], t_mx[:, 36:37], -1.0, None, ALU.mult),
            deps=[cexp], hazard=True)

        for K in range(nch2):
            lb, c = K // 2, K % 2
            gK = gK0 + K
            for gc in range(CG):
                g = c * CG + gc
                gg = len(exp_ops)
                pa = PA[gg % 2]
                if pend[0] is not None:
                    emit_av_group(gc, lds)
                smm = None
                sdeps = kt_done[(h, c)] + qt_done[(h, lb)]
                if gg >= 2:
                    sdeps.append(exp_ops[gg - 2])
                if K == 0 and gc < 2:
                    sdeps.append(norm_red_last[gg % 2])
                for kq in range(2):
                    kt = 2 * g + kq
                    for ct in range(2):
                        first = (kq == 0 and ct == 0)
                        smm = S.op("te", lambda e, lb=lb, kt=kt, ct=ct, kq=kq, pa=pa: e.matmul(
                            pa[:, kq * 512:(kq + 1) * 512],
                            t_kt[:, ct * L + kt * 128:ct * L + (kt + 1) * 128],
                            t_qt[:, ct * L + lb * 512:ct * L + (lb + 1) * 512],
                            start=(ct == 0), stop=(ct == 1)),
                            deps=(sdeps if first else [smm]))
                edp = [smm, negc]
                if (gK - 2, gc) in avof:
                    edp.append(avof[(gK - 2, gc)])
                if gc == 0 and K == 0:
                    edp.append(sq_mm_last)
                eo = S.op("ac", lambda e, h=h, gK=gK, gc=gc, pa=pa: e.activation(
                    pexp_ap(gK, gc), pa[:, 0:1024], AF.Exp,
                    bias=p_negC[:, h:h + 1]), deps=edp)
                exp_ops.append(eo)
                expof[(gK, gc)] = eo
            smm_of[(h, K)] = smm
            explast[gK] = exp_ops[-1]
            pend[0] = (h, K, gK, t_vs)
        smm_head_last[h] = smm_of[(h, nch2 - 1)]
        head_explast[h] = explast[gK0 + nch2 - 1]
    for gc in range(CG):
        emit_av_group(gc, lds)
    p3_tail = [attn_dma[(NHC - 1, NQB - 1)][-1], attn_dma[(NHC - 1, NQB - 2)][-1],
               attn_ev[(NHC - 1, NQB - 1)][-1]]

    A.phase_reset()
    t_kern = A.alloc("t_kern", [128, NCT * FW], BF16)
    t_xf = [A.alloc("t_xf%d" % i, [128, NCT * 512], BF16) for i in range(2)]
    t_aau = A.alloc("t_aau", [128, NCT * 64], BF16)
    t_bau = A.alloc("t_bau", [64, FW], BF16)
    t_axv = [A.alloc("t_axv%d" % i, [64, 512], BF16) for i in range(2)]
    t_xr = A.alloc("t_xr", [128, 4 * FW], F32)
    t_at = A.alloc("t_at", [128, 4 * FW], F32)
    t_t1 = [A.alloc("t_t1%d" % i, [128, FW], F32) for i in range(2)]
    t_ys = [A.alloc("t_ys%d" % i, [128, FW], F32) for i in range(2)]

    p3_guards = [
        (A.ranges[id(t_kt)], head_explast[NHC - 1]),
        (A.ranges[id(t_qt)], head_explast[NHC - 1]),
        (A.ranges[id(t_vs2[0])], head_avlast.get(NHC - 2)),
        (A.ranges[id(t_vs2[1])], head_avlast[NHC - 1]),
        (A.ranges[id(t_sqpe)], head_avlast[NHC - 1]),
        (A.ranges[id(t_mx)], attn_ev[(NHC - 1, NQB - 1)][-1]),
        (A.ranges[id(t_sums)], attn_ev[(NHC - 1, NQB - 1)][-1]),
        (A.ranges[id(t_rec)], attn_ev[(NHC - 1, NQB - 1)][-1]),
        (A.ranges[id(t_sta[0])], attn_dma[(NHC - 1, NQB - 2)][-1]),
        (A.ranges[id(t_sta[1])], attn_dma[(NHC - 1, NQB - 1)][-1]),
    ]

    def p3_guard(t):
        lo, hi = A.ranges[id(t)]
        deps = []
        for (glo, ghi), dep in p3_guards:
            if glo < hi and lo < ghi and dep is not None:
                deps.append(dep)
        return deps

    kern_ld = S.dma("gp", "g_kern", lambda e: e.dma_start(out=t_kern[:], in_=s_kernb[:]),
                    deps=p3_guard(t_kern) + [kern_pc])
    PER_G = NLB // 4
    xf_ld = {}
    aau_ld = {}
    bau_ld = {}
    at_ld = {}
    xr_ld = {}
    ffn_last = {}
    ax_mm_last = {}
    ax_ev = {}
    t1_op = {}
    y2_op = {}
    y3_op = {}
    ydma = {}
    for lblk in range(NLB):
        par = lblk % 2
        g4 = lblk // PER_G
        if lblk == 0:
            dp = p3_guard(t_xf[0])
        elif lblk == 1:
            dp = p3_guard(t_xf[1])
        else:
            dp = [ffn_last[(lblk - 2, 3)], ax_mm_last[lblk - 2]]
        xf_ld[lblk] = S.dma("sy", "s_xf%d" % par, lambda e, lblk=lblk, par=par: e.dma_start(
            out=t_xf[par][:], in_=s_xnt[lblk]), deps=dp + [xnt_st[lblk]])
        if lblk % PER_G == 0:
            adp = [ax_mm_last[lblk - 1]] if lblk >= 1 else p3_guard(t_aau)
            aau_ld[g4] = S.dma("sy", "s_aau", lambda e, g4=g4: e.dma_start(
                out=t_aau[:], in_=s_aaug[g4]), deps=adp + [ab_last])
            bdp = [ffn_last[(lblk - 1, 3)]] if lblk >= 1 else p3_guard(t_bau)
            bau_ld[g4] = S.dma("sy", "s_bau", lambda e, g4=g4: e.dma_start(
                out=t_bau[:], in_=s_baug[g4]), deps=bdp + [ab_last])
        atdp = ([y3_op[(lblk - 1, lt)] for lt in range(4)]
                if lblk >= 1 else p3_guard(t_at))
        at_ld[lblk] = S.dma("gp", "g_at", lambda e, lblk=lblk: e.dma_start(
            out=t_at[:].rearrange("p (l f) -> p l f", l=4),
            in_=s_att[lblk * 4:lblk * 4 + 4].rearrange("l p f -> p l f")),
            deps=atdp + [attn_dma[(NHC - 1, lblk)][-1]])
        xrdp = ([y2_op[(lblk - 1, lt)] for lt in range(4)]
                if lblk >= 1 else p3_guard(t_xr))
        xr_ld[lblk] = S.dma("sy", "s_xr", lambda e, lblk=lblk: e.dma_start(
            out=t_xr[:].rearrange("p (l f) -> p l f", l=4),
            in_=xres[lblk * 512:(lblk + 1) * 512, :].rearrange(
                "(l p) f -> p l f", p=128)), deps=xrdp)
        pax = PB[2 + par]
        mm = None
        for ct in range(NCT):
            first = (ct == 0)
            dpm = [xf_ld[lblk], aau_ld[g4]]
            if first and lblk < 2:
                dpm.append(attn_ev[(NHC - 1, NQB - 1)][-1])
            if first and lblk >= 2:
                dpm.append(ax_ev[lblk - 2])
            mm = S.op("te", lambda e, ct=ct, par=par, pax=pax: e.matmul(
                pax[0:64, 0:512], t_aau[:, ct * 64:(ct + 1) * 64],
                t_xf[par][:, ct * 512:(ct + 1) * 512],
                start=(ct == 0), stop=(ct == NCT - 1)),
                deps=(dpm if first else [mm]))
        ax_mm_last[lblk] = mm
        ax_ev[lblk] = S.op("ve", lambda e, par=par, pax=pax: e.tensor_copy(
            t_axv[par][:], pax[0:64, 0:512]), deps=[mm])
        for lt in range(4):
            pa = PA[lt % 2]
            mm = None
            for oc in range(2):
                for ct in range(NCT):
                    first = (oc == 0 and ct == 0)
                    dpm = [xf_ld[lblk], kern_ld]
                    if first:
                        key = (lblk, lt - 2) if lt >= 2 else (lblk - 1, lt + 2)
                        dpm.append(t1_op.get(key))
                    mm = S.op("te", lambda e, lt=lt, oc=oc, ct=ct, pa=pa, par=par: e.matmul(
                        pa[:, oc * 512:(oc + 1) * 512],
                        t_xf[par][:, ct * 512 + lt * 128:ct * 512 + (lt + 1) * 128],
                        t_kern[:, ct * FW + oc * 512:ct * FW + (oc + 1) * 512],
                        start=(ct == 0), stop=False),
                        deps=(dpm if first else [mm]))
            for oc in range(2):
                mm = S.op("te", lambda e, lt=lt, oc=oc, pa=pa, par=par: e.matmul(
                    pa[:, oc * 512:(oc + 1) * 512],
                    t_axv[par][:, lt * 128:(lt + 1) * 128],
                    t_bau[:, oc * 512:(oc + 1) * 512],
                    start=False, stop=True),
                    deps=[mm, ax_ev[lblk], bau_ld[g4]])
            ffn_last[(lblk, lt)] = mm
            key = (lblk, lt - 2) if lt >= 2 else (lblk - 1, lt + 2)
            t1_op[(lblk, lt)] = S.op("ve", lambda e, lt=lt, pa=pa: e.tensor_tensor(
                t_t1[lt % 2][:], pa[:, 0:1024], p_fbbc[:], ALU.add),
                deps=[mm, ev_fb, y2_op.get(key)])
            y2_op[(lblk, lt)] = S.op("gp", lambda e, lt=lt: e.tensor_tensor(
                t_ys[lt % 2][:], t_t1[lt % 2][:], t_xr[:, lt * FW:(lt + 1) * FW],
                ALU.add), deps=[t1_op[(lblk, lt)], xr_ld[lblk], ydma.get(key)])
            y3_op[(lblk, lt)] = S.op("ve", lambda e, lt=lt: e.tensor_tensor(
                t_ys[lt % 2][:], t_ys[lt % 2][:], t_at[:, lt * FW:(lt + 1) * FW],
                ALU.add), deps=[y2_op[(lblk, lt)], at_ld[lblk]])
            ydma[(lblk, lt)] = S.dma("gp", "g_y%d" % (lt % 2),
                                     lambda e, lblk=lblk, lt=lt: e.dma_start(
                out=y_out[lblk * 512 + lt * 128:lblk * 512 + (lt + 1) * 128, :],
                in_=t_ys[lt % 2][:]), deps=[y3_op[(lblk, lt)]])

    S.emit(nc, ent)
    return nc


def _prep_core(inputs, b, hg, L):
    F0 = hg * FW
    x = inputs["L0_attn_x"]
    Wq = inputs["L0_attn_Wq"]
    Wk = inputs["L0_attn_Wk"]
    Wv = inputs["L0_attn_Wv"]
    kernw = inputs["L0_ffn_kernel"]

    def tile_pm(a):
        W = a.shape[1]
        return np.ascontiguousarray(
            a.reshape(32, 128, W).transpose(1, 0, 2).reshape(128, 32 * W))

    m = {}
    m["xb"] = np.ascontiguousarray(x[b, :L])
    m["xbT"] = np.ascontiguousarray(
        x[b, :L].T.reshape(32, 128, L).transpose(1, 0, 2).reshape(128, 32 * L))
    m["xres"] = np.ascontiguousarray(x[b, :L, F0:F0 + FW])
    m["wkT"] = tile_pm(np.ascontiguousarray(Wk[F0:F0 + FW].T))
    m["wvT"] = tile_pm(np.ascontiguousarray(Wv[F0:F0 + FW].T))
    m["kern"] = tile_pm(np.ascontiguousarray(kernw[:, F0:F0 + FW]))
    m["wq"] = np.ascontiguousarray(
        Wq[F0:F0 + FW].reshape(8, 128, D).transpose(1, 0, 2).reshape(128, 8 * D))
    m["tcd"] = np.ascontiguousarray(
        _host_tcd().reshape(4, 2, 128, 256).transpose(2, 0, 1, 3).reshape(128, 8 * 256))
    m["i128"] = np.eye(128, dtype=np.float32)
    m["ones128"] = np.ones((128, 128), dtype=ml_dtypes.bfloat16)
    m["onesf"] = np.ones((1, 128), dtype=np.float32)
    m["i32t"] = np.tile(np.eye(32, dtype=np.float32), (4, 1))
    pnm = np.zeros((16, 512), dtype=np.float32)
    for t in range(4):
        for p in range(128):
            pnm[(t * 128 + p) // 32, t * 128 + p] = 1.0
    m["pn"] = pnm
    m["gateT"] = np.ascontiguousarray(inputs["L0_ffn_gate"][4 * b:4 * b + 4].T)
    m["bwrow"] = np.ascontiguousarray(
        inputs["L0_attn_bias_weight"][4 * hg:4 * hg + 4].reshape(1, 12))
    m["bqc"] = np.ascontiguousarray(
        inputs["L0_attn_bq"][F0:F0 + FW].reshape(8, 128).T)
    m["bkc"] = np.ascontiguousarray(
        inputs["L0_attn_bk"][F0:F0 + FW].reshape(8, 128).T)
    m["bvrow"] = np.ascontiguousarray(
        inputs["L0_attn_bv"][F0:F0 + FW].reshape(1, FW))
    m["fbrow"] = np.ascontiguousarray(
        inputs["L0_ffn_bias"][F0:F0 + FW].reshape(1, FW))
    lA = inputs["L0_ffn_lora_A"]
    m["lAnr"] = np.ascontiguousarray(
        lA.reshape(512, D).reshape(4, 128, D).transpose(1, 0, 2).reshape(128, 4 * D))
    lB = inputs["L0_ffn_lora_B"][:, F0:F0 + FW, :]
    lbnr = lB.transpose(0, 2, 1).reshape(512, FW)
    m["lBnr"] = np.ascontiguousarray(
        lbnr.reshape(4, 128, FW).transpose(1, 0, 2).reshape(128, 4 * FW))
    lAs = inputs["L0_ffn_lora_A_shared"]
    m["lAsT"] = np.ascontiguousarray(
        lAs.T.reshape(32, 128, 32).transpose(1, 0, 2).reshape(128, 32 * 32))
    m["lBsT"] = np.ascontiguousarray(inputs["L0_ffn_lora_B_shared"][F0:F0 + FW].T)
    for k in m:
        if m[k].dtype == np.float64:
            m[k] = m[k].astype(np.float32)
    return m


def kernel(**inputs):
    L = int(inputs["L0_attn_x"].shape[1])
    nc = build_program(L)
    in_maps = [_prep_core(inputs, c // 4, c % 4, L) for c in range(8)]
    res = run_bass_kernel_spmd(nc, in_maps, core_ids=list(range(8)))
    y = np.empty((2, L, D), dtype=np.float32)
    for c in range(8):
        y[c // 4, :, (c % 4) * FW:(c % 4 + 1) * FW] = res.results[c]["y"]
    return y
